# version 07
# baseline (speedup 1.0000x reference)
import numpy as np
import ml_dtypes
import concourse.bass as bass
import concourse.mybir as mybir
from concourse.bass_utils import run_bass_kernel_spmd
from contextlib import ExitStack

F32 = mybir.dt.float32
BF16 = mybir.dt.bfloat16
AF = mybir.ActivationFunctionType
ALU = mybir.AluOpType

S = 8192
D = 2048
H = 16
HD = 128
NC = 8
ROWS = S // NC
NT = 8
DH = 8192
EPS = 1e-5
SCALE = 1.0 / np.sqrt(HD)
T_BAND = 150.0
NOQFDUMP = 0
LN2B = 0
SBMAP = {}
NEG = -3.0e38
MASKNEG = -1.0e30

SLOPES = [2.0 ** (-0.5 * (h + 1)) for h in range(H)]
GSIZE = [1, 1, 1, 1, 1, 1, 1, 1, 1, 2, 2, 4, 4, 4, 4, 4]
HALVED = [True, True] + [False] * 14


def keep_subtile(h, t_near, b, s):
    dist = 1024 * (t_near - b) - 128 * s - 127
    return SLOPES[h] * max(0, dist) <= T_BAND


def attention_schedule():
    out = []
    for h in range(H):
        G = GSIZE[h]
        for tg in range(2):
            items = []
            for qg in range(4 // G):
                t_lo = tg * 4 + qg * G
                t_hi = t_lo + G - 1
                for b in range(t_lo):
                    for s in range(NC):
                        if not keep_subtile(h, t_lo, b, s):
                            continue
                        if HALVED[h]:
                            for hf in range(2):
                                items.append((b, s, ("n", h, t_hi - b, s, hf),
                                              128 * t_lo + 64 * hf, 64,
                                              128 * (t_lo - 4 * tg) + 64 * hf, None))
                        else:
                            items.append((b, s, ("n", h, t_hi - b, s, 0),
                                          128 * t_lo, 128 * G, 128 * (t_lo - 4 * tg), None))
                for t in range(t_lo, t_hi + 1):
                    for b in range(t_lo, t + 1):
                        diag = (b == t)
                        for s in range(NC):
                            if not keep_subtile(h, t, b, s):
                                continue
                            halves = (0, 1) if HALVED[h] else (0,)
                            for hf in halves:
                                n = 64 if HALVED[h] else 128
                                key = (("d", h, t_hi - t, s, hf) if diag
                                       else ("n", h, t_hi - b, s, hf))
                                mask = (128 * s + 64 * hf) if diag else None
                                items.append((b, s, key,
                                              128 * t + 64 * hf, n,
                                              128 * (t - 4 * tg) + 64 * hf, mask))
            out.append((h, tg, items))
    return out


SCHED = attention_schedule()


def build_bias_table(c):
    keys = []
    seen = set()
    for (_h, _tg, items) in SCHED:
        for (_b, _s, key, _qc, _n, _cc, _m) in items:
            if key not in seen:
                seen.add(key)
                keys.append(key)
    p = np.arange(128, dtype=np.float64)
    cols = {}
    data = []
    for key in keys:
        kind, h, dlt, s, hf = key
        sl = SLOPES[h]
        if kind == "d" and s > c:
            v = np.full(128, NEG)
        else:
            v = sl * (-1024.0 * dlt + 128.0 * (s - c) - 64.0 * hf + p)
        cols[key] = len(data)
        data.append(v)
    a = np.stack(data, axis=1).astype(np.float32)
    ncol = int(np.ceil(a.shape[1] / 16.0) * 16)
    arr = np.zeros((128, max(ncol, 16)), dtype=np.float32)
    arr[:, : a.shape[1]] = a
    return arr, cols


BT_ARR0, COLS = build_bias_table(0)
NCOL = BT_ARR0.shape[1]


def build_masks(c):
    m = np.zeros((128, NC * 128), dtype=np.float32)
    tri = np.where(np.arange(128)[:, None] <= np.arange(128)[None, :], 0.0, MASKNEG)
    m[:, 128 * c : 128 * (c + 1)] = tri
    return m.astype(ml_dtypes.bfloat16)


class Sem:
    __slots__ = ("h", "n")

    def __init__(self, h):
        self.h = h
        self.n = 0


class E:

    def __init__(self, which, eng):
        self.w = which
        self.e = eng

    def _fin(self, inst, sem, k):
        if sem is not None:
            if inst is not None:
                inst.then_inc(sem.h, k)
            sem.n += k
        return inst

    def dma(self, q, out, in_, sem=None, k=16, acc=False):
        i = None
        if self.w == q:
            if acc:
                i = self.e.dma_start(out=out, in_=in_, accum_op=ALU.add)
            else:
                i = self.e.dma_start(out=out, in_=in_)
        return self._fin(i, sem, k)

    def wait(self, q, sem, val=None):
        v = sem.n if val is None else val
        if self.w == q and v > 0:
            self.e.wait_ge(sem.h, v)

    def mm(self, out, lhsT, rhs, start, stop, sem=None, k=1):
        i = None
        if self.w == "tensor":
            i = self.e.matmul(out, lhsT, rhs, start=start, stop=stop)
        return self._fin(i, sem, k)

    def act(self, out, in_, func, bias=0.0, scale=1.0, sem=None, k=1):
        i = None
        if self.w == "scalar":
            i = self.e.activation(out, in_, func, bias=bias, scale=scale)
        return self._fin(i, sem, k)

    def act_table(self, func):
        if self.w == "scalar":
            self.e.preload_act_table(func)

    def vop(self, fn, *args, sem=None, k=1, **kw):
        i = None
        if self.w == "vector":
            i = getattr(self.e, fn)(*args, **kw)
        return self._fin(i, sem, k)

    def pop(self, fn, *args, sem=None, k=1, **kw):
        i = None
        if self.w == "gpsimd":
            i = getattr(self.e, fn)(*args, **kw)
        return self._fin(i, sem, k)


def build_nc(debug=0, upto=99, p4_groups=16, p5_mode=3):
    nc = bass.Bass("TRN2", target_bir_lowering=False, dynamic_dma_scratch_size=4096)

    ap = lambda t: t.ap()
    xT = ap(nc.dram_tensor("xT", [D, S], F32, kind="ExternalInput"))
    xoT = ap(nc.dram_tensor("xoT", [D, ROWS], F32, kind="ExternalInput"))
    Wq = ap(nc.dram_tensor("Wq", [D, D], F32, kind="ExternalInput"))
    Wk = ap(nc.dram_tensor("Wk", [D, D], F32, kind="ExternalInput"))
    Wv = ap(nc.dram_tensor("Wv", [D, D], F32, kind="ExternalInput"))
    Wo = ap(nc.dram_tensor("Wo", [D, D], F32, kind="ExternalInput"))
    bq = ap(nc.dram_tensor("bq", [128, H], F32, kind="ExternalInput"))
    bk = ap(nc.dram_tensor("bk", [128, H], F32, kind="ExternalInput"))
    bo = ap(nc.dram_tensor("bo", [128, H], F32, kind="ExternalInput"))
    bvr = ap(nc.dram_tensor("bvr", [1, D], BF16, kind="ExternalInput"))
    WgT = ap(nc.dram_tensor("WgT", [D, DH], F32, kind="ExternalInput"))
    WuT = ap(nc.dram_tensor("WuT", [D, DH], F32, kind="ExternalInput"))
    WdT = ap(nc.dram_tensor("WdT", [DH, D], F32, kind="ExternalInput"))
    btab = ap(nc.dram_tensor("btab", [128, NCOL], F32, kind="ExternalInput"))
    masks = ap(nc.dram_tensor("masks", [128, NC * 128], BF16, kind="ExternalInput"))
    identb = ap(nc.dram_tensor("identb", [128, 128], BF16, kind="ExternalInput"))
    onecol = ap(nc.dram_tensor("onecol", [128, 1], BF16, kind="ExternalInput"))
    onerow_b = ap(nc.dram_tensor("onerow_b", [1, 128], BF16, kind="ExternalInput"))
    onecol_f = ap(nc.dram_tensor("onecol_f", [128, 1], BF16, kind="ExternalInput"))
    onerow_f = ap(nc.dram_tensor("onerow_f", [1, 128], F32, kind="ExternalInput"))
    zerob = ap(nc.dram_tensor("zerob", [128, 128], BF16, kind="ExternalInput"))
    fconst = ap(nc.dram_tensor("fconst", [1, 8], F32, kind="ExternalInput"))
    fcb = ap(nc.dram_tensor("fcb", [128, 2], F32, kind="ExternalInput"))
    fcu = ap(nc.dram_tensor("fcu", [128, 4], mybir.dt.uint8, kind="ExternalInput"))

    y = ap(nc.dram_tensor("y", [D, ROWS], F32, kind="ExternalOutput"))

    ikind = "ExternalOutput" if debug >= 1 else "Internal"
    ln_dram = ap(nc.dram_tensor("ln_dram", [D, S], BF16, kind=ikind))
    K_g = ap(nc.dram_tensor("K_g", [D, S], BF16, kind=ikind))
    V_g = ap(nc.dram_tensor("V_g", [S, D], BF16, kind=ikind))
    ao_dram = ap(nc.dram_tensor("ao_dram", [D, ROWS], F32, kind=ikind))
    x2_dram = ap(nc.dram_tensor("x2_dram", [D, ROWS], F32, kind=ikind))
    gB_dram = ap(nc.dram_tensor("gB_dram", [DH, 512], BF16, kind=ikind))
    acc_dram = ap(nc.dram_tensor("acc_dram", [D, ROWS], F32, kind=ikind))

    pad_sink = ap(nc.dram_tensor("pad_sink", [1, 64], mybir.dt.uint8, kind="Internal"))

    dbg = {}
    if debug >= 1:
        dbg["Qf"] = ap(nc.dram_tensor("dbg_Qf", [128, H * 1024], BF16, kind="ExternalOutput"))
    if debug >= 2:
        dbg["ctx"] = ap(nc.dram_tensor("dbg_ctx", [128, H * 1024], BF16, kind="ExternalOutput"))
    if debug >= 3:
        dbg["x2"] = ap(nc.dram_tensor("dbg_x2", [D, ROWS], F32, kind="ExternalOutput"))
    if debug >= 3 or LN2B:
        dbg["ln2"] = ap(nc.dram_tensor("dbg_ln2", [128, 16 * 1024], BF16, kind="ExternalOutput"))
    if LN2B:
        dbg["ln2b"] = ap(nc.dram_tensor("dbg_ln2b", [128, 16 * 1024], BF16, kind="ExternalOutput"))
    if LN2B >= 2:
        U8 = mybir.dt.uint8
        dbg["sb1"] = ap(nc.dram_tensor("dbg_sb1", [128, 49152], U8, kind="ExternalOutput"))
        dbg["sb2"] = ap(nc.dram_tensor("dbg_sb2", [128, 49152], U8, kind="ExternalOutput"))
    if debug >= 5:
        dbg["rows"] = ap(nc.dram_tensor("dbg_rows", [1, 3072], F32, kind="ExternalOutput"))
        dbg["bc"] = ap(nc.dram_tensor("dbg_bc", [128, 2048], F32, kind="ExternalOutput"))
        dbg["xbb"] = ap(nc.dram_tensor("dbg_xbb", [128, 1024], BF16, kind="ExternalOutput"))
        dbg["sqb"] = ap(nc.dram_tensor("dbg_sqb", [128, 1024], BF16, kind="ExternalOutput"))
        dbg["pl"] = ap(nc.dram_tensor("dbg_pl", [1, 8192], F32, kind="ExternalOutput"))

    with ExitStack() as stk:
        EE = stk.enter_context

        from math import prod
        ring_pad = nc.alloc_sbuf_tensor_at("ring_pad", [128, 12288], mybir.dt.uint8, offset=4096)
        _cur = [16384]
        sbraw = nc.alloc_sbuf_tensor_at("sbraw", [128, 49152], mybir.dt.uint8, offset=4096) if LN2B >= 2 else None

        def region(tensors):
            base = _cur[0]
            out = []
            end = base
            for name, shape, dt in tensors:
                out.append(nc.alloc_sbuf_tensor_at(name, shape, dt, offset=base))
                SBMAP[name] = (base, base + prod(shape[1:]) * mybir.dt.size(dt))
                end = max(end, base + prod(shape[1:]) * mybir.dt.size(dt))
            _cur[0] = (end + 31) // 32 * 32
            assert _cur[0] <= nc.sbuf_top, f"SBUF overflow: {_cur[0]} > {nc.sbuf_top}"
            return out if len(out) > 1 else out[0]

        btab_sb = region([("btab_sb", [128, NCOL], F32)])
        masks_sb = region([("masks_sb", [128, NC * 128], BF16)])
        identb_sb = region([("identb_sb", [128, 128], BF16)])
        onecol_sb = region([("onecol_sb", [128, 1], BF16)])
        onerow_b_sb = region([("onerow_b_sb", [1, 128], BF16)])
        onecol_f_sb = region([("onecol_f_sb", [128, 1], BF16)])
        onerow_f_sb = region([("onerow_f_sb", [1, 128], F32)])
        zerob_sb = region([("zerob_sb", [128, 128], BF16)])
        bq_sb = region([("bq_sb", [128, H], F32)])
        bk_sb = region([("bk_sb", [128, H], F32)])
        bo_sb = region([("bo_sb", [128, H], F32)])
        fc_sb = region([("fc_sb", [1, 8], F32)])
        fcb_sb = region([("fcb_sb", [128, 2], F32)])
        fcu_sb = region([("fcu_sb", [128, 4], mybir.dt.uint8)])

        LNO, ln_sl2, ln2T, gbuf2 = region([
            ("LNO", [128, 16 * 1024], BF16),
            ("ln_sl2", [128, 2 * 8192], BF16),
            ("ln2T", [128, 16 * 1024], BF16),
            ("gbuf2", [128, 2 * 4096], BF16),
        ])
        Qf, gated_lo = region([
            ("Qf", [128, H * 1024], BF16),
            ("gated_lo", [128, 32 * 512], BF16),
        ])
        ctx_fm, gated_hi = region([
            ("ctx_fm", [128, H * 1024], BF16),
            ("gated_hi", [128, 32 * 512], BF16),
        ])
        W_sb, KV_sb, WGU_sb, WD_sb = region([
            ("W_sb", [128, 16 * 2048], BF16),
            ("KV_sb", [128, 2 * 16384], BF16),
            ("WGU_sb", [128, 4 * 8192], BF16),
            ("WD_sb", [128, 2 * 16384], BF16),
        ])
        xst_b = region([("xst_b", [128, 2 * 512], F32)])
        xnm_b = region([("xnm_b", [128, 2 * 512], F32)])
        sqb = region([("sqb", [128, 2 * 512], BF16)])
        xbb = region([("xbb", [128, 2 * 512], BF16)])
        kst, vst, ptil = region([
            ("kst", [128, 2 * 512], BF16),
            ("vst", [128, 2 * 1024], BF16),
            ("ptil", [128, 2 * 512], BF16),
        ])
        aosb_a = nc.alloc_sbuf_tensor_at("aosb_a", [128, 1024], F32, offset=SBMAP["xnm_b"][0])
        aosb_b = nc.alloc_sbuf_tensor_at("aosb_b", [128, 1024], F32, offset=SBMAP["kst"][0])
        bvr_sb = nc.alloc_sbuf_tensor_at("bvr_sb2", [1, D], BF16, offset=SBMAP["ctx_fm"][0])
        rows_sb = region([("rows_sb", [1, 3072], F32)])
        linv_sb = rows_sb
        bc_sb = region([("bc_sb", [128, 4 * 512], F32)])
        t1b, lbc_sb, sg_sb, st32 = region([
            ("t1b", [128, 2 * 512], F32),
            ("lbc_sb", [128, 2 * 512], F32),
            ("sg_sb", [128, 2 * 512], F32),
            ("st32", [128, 2 * 512], F32),
        ])

        PS = [EE(nc.psum_tensor(f"PS{i}", [128, 512], F32)) for i in range(6)]
        PL = [EE(nc.psum_tensor(f"PL{i}", [1, 512], F32)) for i in range(2)]

        nc.const_aps.register(F32, EPS, fc_sb[0:1, 0:1])
        nc.const_aps.register(F32, 0.0, fcb_sb[:, 0:1])
        nc.const_aps.register(F32, 1.0, fcb_sb[:, 1:2])
        nc.const_aps.register(BF16, 1.0, onecol_sb[:, 0:1])
        nc.const_aps.register(mybir.dt.uint8, 127, fcu_sb[:, 0:1])
        sem_names = ["ld", "ldc", "pe", "acts", "dve", "pool", "sto", "pe2", "acte",
                     "la0", "la1", "ln0", "ln1", "lb0", "lb1", "kv0", "kv1",
                     "gu0", "gu1", "wd0", "wd1", "gb0", "gb1",
                     "sk0", "sk1", "sv0", "sv1", "sl0", "sl1",
                     "st_ao", "st_x2", "st_acc", "st_gb", "st_kv", "st_y", "dbgs"]
        sem_h = {n: EE(nc.semaphore(n)) for n in sem_names}

        blk = EE(nc.Block())
        dbg_state = {}

        def ln_fm(e, sm, dram_col0, dram_src, n, out, war_sem=None, war_val=0):
            ld = sm["ld"]; pe = sm["pe"]; acts = sm["acts"]; dve = sm["dve"]; pool = sm["pool"]
            la = (sm["la0"], sm["la1"])
            lnS = (sm["ln0"], sm["ln1"])
            nh = n // 512
            sums = [(PL[0], PL[1]), (PS[4], PS[5])]
            cnt = 0
            war = []
            for g in range(16):
                for hh in range(nh):
                    sl = cnt % 2
                    xc = xst_b[:, 512 * sl : 512 * (sl + 1)]
                    xb = xbb[:, 512 * sl : 512 * (sl + 1)]
                    sq = sqb[:, 512 * sl : 512 * (sl + 1)]
                    if cnt >= 2:
                        e.wait("sync", pe, war[cnt - 2])
                    e.dma("sync", xc, dram_src[128 * g : 128 * (g + 1),
                                              dram_col0 + 512 * hh : dram_col0 + 512 * (hh + 1)], la[sl])
                    e.wait("scalar", la[sl], la[sl].n)
                    if cnt >= 2:
                        e.wait("scalar", pe, war[cnt - 2])
                    e.act(xb, xc, AF.Copy, sem=acts)
                    e.wait("vector", acts, acts.n)
                    e.vop("tensor_mul", sq, xb, xb, sem=dve)
                    e.wait("tensor", dve, dve.n)
                    ts_, tq_ = sums[hh]
                    e.mm(ts_[0:1, 0:512], onecol_f_sb[:], xb, start=(g == 0), stop=(g == 15))
                    e.mm(tq_[0:1, 0:512], onecol_f_sb[:], sq, start=(g == 0), stop=(g == 15),
                         sem=pe)
                    war.append(pe.n)
                    cnt += 1
            if debug >= 5 and not dbg_state.get("pl_dumped"):
                dbg_state["pl_dumped"] = True
                e.wait("vector", pe, pe.n)
                e.vop("tensor_copy", t1b[:, 0:512][0:1, :], PL[0][0:1, :], sem=dve)
                e.vop("tensor_copy", t1b[:, 512:1024][0:1, :], PL[1][0:1, :], sem=dve)
                e.wait("sync", dve, dve.n)
                e.dma("sync", dbg["pl"][0:1, 0:1024], t1b[0:1, 0:1024], sm["dbgs"])
            r_mean = rows_sb[0:1, 0:n]
            r_var = rows_sb[0:1, 1024 : 1024 + n]
            r_rstd = rows_sb[0:1, 2048 : 2048 + n]
            e.wait("vector", pe, pe.n)
            for hh in range(nh):
                ts_, tq_ = sums[hh]
                e.vop("tensor_scalar_mul", r_mean[0:1, 512 * hh : 512 * (hh + 1)],
                      ts_[0:1, 0:512], 1.0 / D, sem=dve)
                e.vop("tensor_scalar_mul", r_var[0:1, 512 * hh : 512 * (hh + 1)],
                      tq_[0:1, 0:512], 1.0 / D, sem=dve)
            dbgrows = debug >= 5 and not dbg_state.get("rows_dumped")
            def snap(i):
                if dbgrows:
                    e.wait("sync", dve, dve.n)
                    e.wait("sync", acts, acts.n)
                    e.dma("sync", dbg["pl"][0:1, 1024 * i : 1024 * i + 512],
                          rows_sb[0:1, 1024:1536], sm["dbgs"])
                    e.dma("sync", dbg["pl"][0:1, 1024 * i + 512 : 1024 * (i + 1)],
                          rows_sb[0:1, 2048:2560], sm["dbgs"])
                    e.wait("vector", sm["dbgs"], sm["dbgs"].n)
                    e.wait("scalar", sm["dbgs"], sm["dbgs"].n)
            e.wait("vector", dve, dve.n)
            snap(1)
            e.vop("tensor_tensor", r_rstd, r_mean, r_mean, op=ALU.mult, sem=dve)
            e.wait("vector", dve, dve.n)
            snap(2)
            e.vop("tensor_sub", r_var, r_var, r_rstd, sem=dve)
            e.wait("scalar", dve, dve.n)
            snap(3)
            e.act(r_rstd, r_var, AF.Ln, bias=EPS, sem=acts)
            e.wait("scalar", acts, acts.n)
            snap(4)
            e.act(r_rstd, r_rstd, AF.Exp, scale=-0.5, sem=acts)
            e.wait("vector", acts, acts.n)
            snap(5)
            if dbgrows:
                dbg_state["rows_dumped"] = True
            e.vop("tensor_tensor", r_var, r_mean, r_rstd, op=ALU.mult, sem=dve)
            e.wait("tensor", dve, dve.n)
            for hh in range(nh):
                if hh >= 1:
                    e.wait("tensor", dve, dve.n)
                e.mm(PS[4][:, 0:512], onerow_f_sb[:], r_rstd[0:1, 512 * hh : 512 * (hh + 1)],
                     start=True, stop=True, sem=pe)
                e.mm(PS[5][:, 0:512], onerow_f_sb[:], r_var[0:1, 512 * hh : 512 * (hh + 1)],
                     start=True, stop=True, sem=pe)
                e.wait("vector", pe, pe.n)
                e.vop("tensor_copy", bc_sb[:, 1024 * hh : 1024 * hh + 512], PS[4][:, 0:512], sem=dve)
                e.vop("tensor_copy", bc_sb[:, 1024 * hh + 512 : 1024 * (hh + 1)], PS[5][:, 0:512],
                      sem=dve)
            if war_sem is not None:
                e.wait("gpsimd", war_sem, war_val)
            cnt = 0
            dwar = []
            pwar = []
            for g in range(16):
                for hh in range(nh):
                    sl = cnt % 2
                    xc = xnm_b[:, 512 * sl : 512 * (sl + 1)]
                    t1 = t1b[:, 512 * sl : 512 * (sl + 1)]
                    if cnt >= 2:
                        e.wait("sync", dve, dwar[cnt - 2])
                    e.dma("sync", xc, dram_src[128 * g : 128 * (g + 1),
                                              dram_col0 + 512 * hh : dram_col0 + 512 * (hh + 1)], lnS[sl])
                    if cnt >= 2:
                        e.wait("vector", pool, pwar[cnt - 2])
                    e.wait("vector", lnS[sl], lnS[sl].n)
                    e.wait("vector", dve, dve.n)
                    e.vop("tensor_tensor", t1, xc,
                          bc_sb[:, 1024 * hh : 1024 * hh + 512], op=ALU.mult, sem=dve)
                    dwar.append(dve.n)
                    e.wait("gpsimd", dve, dve.n)
                    e.pop("tensor_tensor", out(g)[:, 512 * hh : 512 * (hh + 1)],
                          t1, bc_sb[:, 1024 * hh + 512 : 1024 * (hh + 1)],
                          op=ALU.subtract, sem=pool)
                    pwar.append(pool.n)
                    cnt += 1

        def program(e):
            dbg_state.clear()
            sm = {n: Sem(h) for n, h in sem_h.items()}
            phases(e, sm)
            for nm in ("st_y", "dbgs", "sto", "st_ao", "st_x2", "st_acc", "st_gb",
                       "sk0", "sk1", "sv0", "sv1", "sl0", "sl1"):
                e.wait("sync", sm[nm])
            e.wait("gpsimd", sm["st_y"])

        def phases(e, sm):
            ld = sm["ld"]; ldc = sm["ldc"]; pe = sm["pe"]; acts = sm["acts"]
            dve = sm["dve"]; pool = sm["pool"]; sto = sm["sto"]
            pe2 = sm["pe2"]; acte = sm["acte"]
            sk = (sm["sk0"], sm["sk1"]); sv = (sm["sv0"], sm["sv1"])
            slst = (sm["sl0"], sm["sl1"]); kvs = (sm["kv0"], sm["kv1"])
            gus = (sm["gu0"], sm["gu1"]); wds = (sm["wd0"], sm["wd1"])
            gbs = (sm["gb0"], sm["gb1"])
            st_ao = sm["st_ao"]; st_x2 = sm["st_x2"]; st_acc = sm["st_acc"]
            st_gb = sm["st_gb"]; st_kv = sm["st_kv"]; st_y = sm["st_y"]

            for dst, src in [(fc_sb, fconst), (fcb_sb, fcb), (fcu_sb, fcu), (btab_sb, btab), (masks_sb, masks), (identb_sb, identb),
                             (onecol_sb, onecol), (onerow_b_sb, onerow_b),
                             (onecol_f_sb, onecol_f), (onerow_f_sb, onerow_f),
                             (zerob_sb, zerob), (bq_sb, bq), (bk_sb, bk),
                             (bo_sb, bo), (bvr_sb, bvr)]:
                e.dma("sync", dst[:], src, ld)
            for q in ("sync", "gpsimd", "tensor", "scalar", "vector"):
                e.wait(q, ld)

            ln_fm(e, sm, 0, xoT, 1024,
                  out=lambda g: LNO[:, 1024 * g : 1024 * (g + 1)])
            lno_done = pool.n
            if debug >= 5:
                e.wait("sync", pool, pool.n)
                e.dma("sync", dbg["rows"], rows_sb[0:1, :], sm["dbgs"])
                e.dma("sync", dbg["bc"], bc_sb[:], sm["dbgs"])
                e.dma("sync", dbg["xbb"], xbb[:], sm["dbgs"])
                e.dma("sync", dbg["sqb"], sqb[:], sm["dbgs"])

            for d in range(16):
                e.dma("gpsimd", W_sb[:, 2048 * d : 2048 * (d + 1)],
                      Wq[128 * d : 128 * (d + 1), :], ldc)
            e.wait("tensor", ldc, ldc.n)
            e.wait("tensor", pool, lno_done)
            q_ev = []
            for h in range(H):
                for half in range(2):
                    i = 2 * h + half
                    ps = PS[i % 2]
                    if i >= 2:
                        e.wait("tensor", acts, q_ev[i - 2])
                    for d in range(16):
                        e.mm(ps[:], W_sb[:, 2048 * d + 128 * h : 2048 * d + 128 * (h + 1)],
                             LNO[:, 1024 * d + 512 * half : 1024 * d + 512 * (half + 1)],
                             start=(d == 0), stop=(d == 15), sem=pe if d == 15 else None)
                    e.wait("scalar", pe, pe.n)
                    e.act(Qf[:, 1024 * h + 512 * half : 1024 * h + 512 * (half + 1)], ps[:],
                          AF.Identity, bias=bq_sb[:, h : h + 1], sem=acts)
                    q_ev.append(acts.n)

            if debug >= 1 and not NOQFDUMP:
                e.wait("sync", acts, acts.n)
                for i in range(4):
                    e.dma("sync", dbg["Qf"][:, 4096 * i : 4096 * (i + 1)],
                          Qf[:, 4096 * i : 4096 * (i + 1)], sm["dbgs"])

            if upto <= 0:
                return
            e.wait("gpsimd", pe, pe.n)
            for d in range(16):
                e.dma("gpsimd", W_sb[:, 2048 * d : 2048 * (d + 1)],
                      Wk[128 * d : 128 * (d + 1), :], ldc)
            wk_ld = ldc.n

            ln_done = []
            ln_st = []
            k_ev = []
            kst_free = []
            kts_pe = []

            def k_pass(ts):
                lnb = ln_sl2[:, (ts % 2) * 8192 : (ts % 2 + 1) * 8192]
                if ts == 0:
                    e.wait("tensor", ldc, wk_ld)
                e.wait("tensor", pool, ln_done[ts])
                for h in range(H):
                    i = 16 * ts + h
                    sl = i % 2
                    ps = PS[2 + sl]
                    if i >= 2:
                        e.wait("tensor", acts, k_ev[i - 2])
                    for d in range(16):
                        e.mm(ps[:], W_sb[:, 2048 * d + 128 * h : 2048 * d + 128 * (h + 1)],
                             lnb[:, 512 * d : 512 * (d + 1)],
                             start=(d == 0), stop=(d == 15), sem=pe if d == 15 else None)
                    e.wait("scalar", pe, pe.n)
                    if i >= 2:
                        e.wait("scalar", sk[sl], 16 * (i // 2))
                    e.act(kst[:, 512 * sl : 512 * (sl + 1)], ps[:], AF.Identity,
                          bias=bk_sb[:, h : h + 1], sem=acts)
                    k_ev.append(acts.n)
                    e.wait("sync", acts, acts.n)
                    e.dma("sync", K_g[128 * h : 128 * (h + 1), 512 * ts : 512 * (ts + 1)],
                          kst[:, 512 * sl : 512 * (sl + 1)], sk[sl])
                kts_pe.append(pe.n)

            for ts in range(16):
                lnb = ln_sl2[:, (ts % 2) * 8192 : (ts % 2 + 1) * 8192]
                if ts >= 2:
                    e.wait("gpsimd", slst[ts % 2], 256 * (ts // 2))
                war_val = kts_pe[ts - 2] if ts >= 2 else 0
                ln_fm(e, sm, 512 * ts, xT, 512,
                      out=lambda g, lnb=lnb: lnb[:, 512 * g : 512 * (g + 1)],
                      war_sem=pe if ts >= 2 else None, war_val=war_val)
                ln_done.append(pool.n)
                e.wait("sync", pool, pool.n)
                for g in range(16):
                    e.dma("sync", ln_dram[128 * g : 128 * (g + 1), 512 * ts : 512 * (ts + 1)],
                          lnb[:, 512 * g : 512 * (g + 1)], slst[ts % 2])
                if ts >= 1:
                    k_pass(ts - 1)
            k_pass(15)

            e.wait("gpsimd", pe, pe.n)
            for d in range(16):
                e.dma("gpsimd", W_sb[:, 2048 * d : 2048 * (d + 1)],
                      Wv[128 * d : 128 * (d + 1), :], ldc)
            e.wait("tensor", ldc, ldc.n)
            e.wait("sync", sm["sl0"], sm["sl0"].n)
            e.wait("sync", sm["sl1"], sm["sl1"].n)
            e.wait("sync", pe, kts_pe[-1])
            e.wait("scalar", sk[0], sk[0].n)
            e.wait("scalar", sk[1], sk[1].n)
            v_ev = []
            vts_pe = []
            lb = (sm["lb0"], sm["lb1"])
            for ts in range(16):
                lnb = ln_sl2[:, (ts % 2) * 8192 : (ts % 2 + 1) * 8192]
                if ts >= 2:
                    e.wait("sync", pe, vts_pe[ts - 2])
                for g in range(16):
                    e.dma("sync", lnb[:, 512 * g : 512 * (g + 1)],
                          ln_dram[128 * g : 128 * (g + 1), 512 * ts : 512 * (ts + 1)], lb[ts % 2])
                e.wait("tensor", lb[ts % 2], lb[ts % 2].n)
                for tt in range(4):
                    for dh in range(2):
                        i = len(v_ev)
                        sl = i % 2
                        pa, pb = PS[2 * sl], PS[2 * sl + 1]
                        if i >= 2:
                            e.wait("tensor", acts, v_ev[i - 2])
                        for d in range(16):
                            lhsT = lnb[:, 512 * d + 128 * tt : 512 * d + 128 * (tt + 1)]
                            e.mm(pa[:], lhsT,
                                 W_sb[:, 2048 * d + 1024 * dh : 2048 * d + 1024 * dh + 512],
                                 start=(d == 0), stop=False)
                            e.mm(pb[:], lhsT,
                                 W_sb[:, 2048 * d + 1024 * dh + 512 : 2048 * d + 1024 * (dh + 1)],
                                 start=(d == 0), stop=False)
                        e.mm(pa[:], onerow_b_sb[:], bvr_sb[0:1, 1024 * dh : 1024 * dh + 512],
                             start=False, stop=True)
                        e.mm(pb[:], onerow_b_sb[:], bvr_sb[0:1, 1024 * dh + 512 : 1024 * (dh + 1)],
                             start=False, stop=True, sem=pe)
                        e.wait("scalar", pe, pe.n)
                        if i >= 2:
                            e.wait("scalar", sv[sl], 16 * (i // 2))
                        e.act(vst[:, 1024 * sl : 1024 * sl + 512], pa[:], AF.Identity)
                        e.act(vst[:, 1024 * sl + 512 : 1024 * (sl + 1)], pb[:], AF.Identity, sem=acts)
                        v_ev.append(acts.n)
                        e.wait("sync", acts, acts.n)
                        e.dma("sync", V_g[512 * ts + 128 * tt : 512 * ts + 128 * (tt + 1),
                                          1024 * dh : 1024 * (dh + 1)],
                              vst[:, 1024 * sl : 1024 * (sl + 1)], sv[sl])
                vts_pe.append(pe.n)

            if upto <= 1:
                return
            for s_ in (sm["sk0"], sm["sk1"], sm["sv0"], sm["sv1"]):
                e.wait("sync", s_, s_.n)
            e.wait("sync", pe, pe.n)
            e.wait("scalar", sv[0], sv[0].n)
            e.wait("scalar", sv[1], sv[1].n)
            e.wait("tensor", acts, acts.n)
            e.wait("tensor", dve, dve.n)
            kv_ld = []
            att_done = []
            ctx_evac = {0: 0, 1: 0}

            def load_kv(h):
                half = h % 2
                kb = KV_sb[:, 16384 * half : 16384 * half + 8192]
                vb = KV_sb[:, 16384 * half + 8192 : 16384 * (half + 1)]
                if h >= 2:
                    e.wait("sync", pe, att_done[h - 2])
                e.dma("sync", kb[:], K_g[128 * h : 128 * (h + 1), :], kvs[half])
                for b in range(NC):
                    src = V_g[1024 * b : 1024 * (b + 1), 128 * h : 128 * (h + 1)]
                    src = src.rearrange("(s k) d -> k s d", k=128)
                    dst = vb[:, 1024 * b : 1024 * (b + 1)].rearrange("k (s d) -> k s d", s=8)
                    e.dma("sync", dst, src, kvs[half])
                kv_ld.append(kvs[half].n)

            load_kv(0)

            q_state = {"q": []}

            def flush(idx):
                q = q_state["q"]
                ent = q[idx]
                if ent["done"]:
                    return
                e.wait("tensor", acte, ent["exp"])
                pt = ptil[:, 512 * ent["slot"] : 512 * ent["slot"] + ent["N"]]
                e.mm(ent["l_ps"][0:1, ent["ctxcol"] : ent["ctxcol"] + ent["N"]],
                     onecol_sb[:], pt, start=False, stop=False)
                e.mm(ent["ctx_ps"][:, ent["ctxcol"] : ent["ctxcol"] + ent["N"]],
                     ent["vb"][:, 1024 * ent["b"] + 128 * ent["s"] : 1024 * ent["b"] + 128 * (ent["s"] + 1)],
                     pt, start=False, stop=False, sem=pe)
                ent["done"] = True
                ent["ctxpe"] = pe.n

            for (h, tg, items) in SCHED:
                half = h % 2
                kb = KV_sb[:, 16384 * half : 16384 * half + 8192]
                vb = KV_sb[:, 16384 * half + 8192 : 16384 * (half + 1)]
                if tg == 0:
                    if h + 1 < H:
                        load_kv(h + 1)
                    e.wait("tensor", kvs[h % 2], kv_ld[h])
                pp = (2 * h + tg) % 2
                ctx_ps = PS[2 + pp]
                l_ps = PL[pp]
                if 2 * h + tg >= 2:
                    e.wait("tensor", dve, ctx_evac[pp])
                e.mm(ctx_ps[:], zerob_sb[:], Qf[:, 0:512], start=True, stop=False)
                e.mm(l_ps[0:1, :], zerob_sb[:, 0:1], Qf[:, 0:512], start=True, stop=False)
                for (b, s, colkey, qoff, N, ctxcol, mask) in items:
                    q = q_state["q"]
                    i = len(q)
                    slot = i % 2
                    sps = PS[(0, 1)[slot]]
                    if i >= 2:
                        e.wait("tensor", acte, q[i - 2]["exp"])
                    e.mm(sps[:, 0:N], kb[:, 1024 * b + 128 * s : 1024 * b + 128 * (s + 1)],
                         Qf[:, 1024 * h + qoff : 1024 * h + qoff + N],
                         start=True, stop=(mask is None), sem=None if mask is not None else pe2)
                    if mask is not None:
                        e.mm(sps[:, 0:N], identb_sb[:], masks_sb[:, mask : mask + N],
                             start=False, stop=True, sem=pe2)
                    e.wait("scalar", pe2, pe2.n)
                    if i >= 2:
                        e.wait("scalar", pe, q[i - 2]["ctxpe"])
                    cidx = COLS[colkey]
                    e.act(ptil[:, 512 * slot : 512 * slot + N], sps[:, 0:N], AF.Exp,
                          bias=btab_sb[:, cidx : cidx + 1], scale=SCALE, sem=acte)
                    q.append(dict(exp=acte.n, slot=slot, N=N, vb=vb, b=b, s=s,
                                  ctx_ps=ctx_ps, l_ps=l_ps, ctxcol=ctxcol,
                                  ctxpe=0, done=False))
                    if i >= 1:
                        flush(i - 1)
                if q_state["q"]:
                    flush(-1)
                e.mm(ctx_ps[:], zerob_sb[:], Qf[:, 0:512], start=False, stop=True)
                e.mm(l_ps[0:1, :], zerob_sb[:, 0:1], Qf[:, 0:512], start=False, stop=True, sem=pe)
                peval = pe.n
                e.wait("vector", pe, peval)
                lrow = linv_sb[0:1, 512 * pp : 512 * (pp + 1)]
                e.vop("reciprocal", lrow, l_ps[0:1, :], sem=dve)
                e.wait("tensor", dve, dve.n)
                e.mm(PS[4 + pp][:, 0:512], onerow_f_sb[:], lrow, start=True, stop=True, sem=pe)
                e.wait("vector", pe, pe.n)
                lb = lbc_sb[:, 512 * pp : 512 * (pp + 1)]
                e.vop("tensor_copy", lb, PS[4 + pp][:, 0:512], sem=dve)
                e.wait("vector", dve, dve.n)
                e.vop("tensor_tensor", ctx_fm[:, 1024 * h + 512 * tg : 1024 * h + 512 * (tg + 1)],
                      ctx_ps[:], lb, op=ALU.mult, sem=dve)
                ctx_evac[pp] = dve.n
                if tg == 1:
                    att_done.append(pe.n)

            if debug >= 2:
                e.wait("sync", dve, dve.n)
                for i in range(4):
                    e.dma("sync", dbg["ctx"][:, 4096 * i : 4096 * (i + 1)],
                          ctx_fm[:, 4096 * i : 4096 * (i + 1)], sm["dbgs"])

            if upto <= 2:
                return
            e.wait("gpsimd", pe, pe.n)
            for d in range(16):
                e.dma("gpsimd", W_sb[:, 2048 * d : 2048 * (d + 1)],
                      Wo[128 * d : 128 * (d + 1), :], ldc)
            e.wait("tensor", ldc, ldc.n)
            e.wait("tensor", dve, dve.n)

            def rows_pass(scale_div):
                r_mean = rows_sb[0:1, 0:1024]
                r_var = rows_sb[0:1, 1024:2048]
                r_rstd = rows_sb[0:1, 2048:3072]
                r_mrstd = rows_sb[0:1, 1024:2048]
                e.wait("vector", pe, pe.n)
                e.vop("tensor_scalar_mul", r_mean[0:1, 0:512], PL[0][0:1, :], scale_div, sem=dve)
                e.vop("tensor_scalar_mul", r_mean[0:1, 512:1024], PS[4][0:1, 0:512], scale_div, sem=dve)
                e.vop("tensor_scalar_mul", r_var[0:1, 0:512], PL[1][0:1, :], scale_div, sem=dve)
                e.vop("tensor_scalar_mul", r_var[0:1, 512:1024], PS[5][0:1, 0:512], scale_div, sem=dve)
                e.wait("vector", dve, dve.n)
                e.vop("tensor_tensor", r_rstd, r_mean, r_mean, op=ALU.mult, sem=dve)
                e.wait("vector", dve, dve.n)
                e.vop("tensor_sub", r_var, r_var, r_rstd, sem=dve)
                e.wait("scalar", dve, dve.n)
                e.act(r_rstd, r_var, AF.Ln, bias=EPS, sem=acts)
                e.wait("scalar", acts, acts.n)
                e.act(r_rstd, r_rstd, AF.Exp, scale=-0.5, sem=acts)
                e.wait("vector", acts, acts.n)
                e.vop("tensor_tensor", r_mrstd, r_mean, r_rstd, op=ALU.mult, sem=dve)
                e.wait("tensor", dve, dve.n)
                for hh in range(2):
                    if hh >= 1:
                        e.wait("tensor", dve, dve.n)
                    e.mm(PS[4][:, 0:512], onerow_f_sb[:], r_rstd[0:1, 512 * hh : 512 * (hh + 1)],
                         start=True, stop=True, sem=pe)
                    e.mm(PS[5][:, 0:512], onerow_f_sb[:], r_mrstd[0:1, 512 * hh : 512 * (hh + 1)],
                         start=True, stop=True, sem=pe)
                    e.wait("vector", pe, pe.n)
                    e.vop("tensor_copy", bc_sb[:, 1024 * hh : 1024 * hh + 512],
                          PS[4][:, 0:512], sem=dve)
                    e.vop("tensor_copy", bc_sb[:, 1024 * hh + 512 : 1024 * (hh + 1)],
                          PS[5][:, 0:512], sem=dve)

            def stats_chunk(dc, ac):
                e.wait("vector", acts, acts.n)
                e.wait("vector", dve, dve.n)
                e.wait("vector", pe, pe.n)
                for hh in range(2):
                    xb = xbb[:, 512 * hh : 512 * (hh + 1)]
                    sq = sqb[:, 512 * hh : 512 * (hh + 1)]
                    e.vop("tensor_copy", xb, ac[:, 512 * hh : 512 * (hh + 1)], sem=dve)
                    e.wait("vector", dve, dve.n)
                    e.vop("tensor_mul", sq, xb, xb, sem=dve)
                    e.wait("tensor", dve, dve.n)
                    ts_ = PL[0] if hh == 0 else PS[4]
                    tq_ = PL[1] if hh == 0 else PS[5]
                    e.mm(ts_[0:1, 0:512], onecol_f_sb[:], xb, start=(dc == 0), stop=(dc == 15))
                    e.mm(tq_[0:1, 0:512], onecol_f_sb[:], sq, start=(dc == 0), stop=(dc == 15),
                         sem=pe)

            p3_ev = []
            p3_st = []
            for dc in range(16):
                sl = dc % 2
                ps0, ps1 = PS[2 * sl], PS[2 * sl + 1]
                if dc >= 2:
                    e.wait("tensor", acts, p3_ev[dc - 2])
                for h in range(H):
                    lhsT = W_sb[:, 2048 * h + 128 * dc : 2048 * h + 128 * (dc + 1)]
                    e.mm(ps0[:], lhsT, ctx_fm[:, 1024 * h : 1024 * h + 512],
                         start=(h == 0), stop=(h == 15))
                    e.mm(ps1[:], lhsT, ctx_fm[:, 1024 * h + 512 : 1024 * (h + 1)],
                         start=(h == 0), stop=(h == 15), sem=pe if h == 15 else None)
                e.wait("scalar", pe, pe.n)
                ao = aosb_a[:, 0:1024]
                if dc >= 1:
                    e.wait("scalar", st_ao, 16 * dc)
                    e.wait("scalar", dve, dve.n)
                e.act(ao[:, 0:512], ps0[:], AF.Identity, bias=bo_sb[:, dc : dc + 1])
                e.act(ao[:, 512:1024], ps1[:], AF.Identity, bias=bo_sb[:, dc : dc + 1], sem=acts)
                p3_ev.append(acts.n)
                stats_chunk(dc, ao)
                e.wait("sync", acts, p3_ev[-1])
                e.dma("sync", ao_dram[128 * dc : 128 * (dc + 1), :], ao, st_ao)
            rows_pass(1.0 / D)
            p3b_dve = []
            e.wait("sync", st_ao, st_ao.n)
            e.wait("sync", dve, dve.n)
            for dc in range(16):
                ao = aosb_a[:, 0:1024]
                if dc >= 1:
                    e.wait("sync", dve, p3b_dve[dc - 1])
                    e.wait("sync", st_x2, st_x2.n)
                    if debug >= 3:
                        e.wait("sync", sm["dbgs"], sm["dbgs"].n)
                e.dma("sync", ao, ao_dram[128 * dc : 128 * (dc + 1), :], ld)
                for hh in range(2):
                    xo = xst_b[:, 512 * hh : 512 * (hh + 1)]
                    e.dma("sync", xo, xoT[128 * dc : 128 * (dc + 1), 512 * hh : 512 * (hh + 1)], ld)
                e.wait("vector", ld, ld.n)
                e.wait("vector", dve, dve.n)
                for hh in range(2):
                    xo = xst_b[:, 512 * hh : 512 * (hh + 1)]
                    t1 = t1b[:, 512 * hh : 512 * (hh + 1)]
                    e.vop("tensor_tensor", t1, ao[:, 512 * hh : 512 * (hh + 1)],
                          bc_sb[:, 1024 * hh : 1024 * hh + 512], op=ALU.mult, sem=dve)
                    e.wait("vector", dve, dve.n)
                    e.vop("tensor_sub", t1, t1, bc_sb[:, 1024 * hh + 512 : 1024 * (hh + 1)], sem=dve)
                    e.wait("vector", dve, dve.n)
                    e.vop("tensor_add", ao[:, 512 * hh : 512 * (hh + 1)],
                          xo, t1, sem=dve)
                e.wait("sync", dve, dve.n)
                e.dma("sync", x2_dram[128 * dc : 128 * (dc + 1), :], ao, st_x2)
                if debug >= 3:
                    e.dma("sync", dbg["x2"][128 * dc : 128 * (dc + 1), :], ao, sm["dbgs"])
                stats_chunk(dc, ao)
                p3b_dve.append(dve.n)
            rows_pass(1.0 / D)
            p3c = []
            e.wait("sync", st_x2, st_x2.n)
            e.wait("sync", dve, dve.n)
            if debug >= 3:
                e.wait("sync", sm["dbgs"], sm["dbgs"].n)
            for dc in range(16):
                x2c = aosb_a[:, 0:1024]
                if dc >= 1:
                    e.wait("sync", pool, p3c[dc - 1])
                e.dma("sync", x2c, x2_dram[128 * dc : 128 * (dc + 1), :], ld)
                e.wait("vector", ld, ld.n)
                e.wait("vector", dve, dve.n)
                for hh in range(2):
                    t1 = t1b[:, 512 * hh : 512 * (hh + 1)]
                    e.vop("tensor_tensor", t1, x2c[:, 512 * hh : 512 * (hh + 1)],
                          bc_sb[:, 1024 * hh : 1024 * hh + 512], op=ALU.mult, sem=dve)
                    e.wait("gpsimd", dve, dve.n)
                    e.pop("tensor_tensor",
                          ln2T[:, 1024 * dc + 512 * hh : 1024 * dc + 512 * (hh + 1)],
                          t1, bc_sb[:, 1024 * hh + 512 : 1024 * (hh + 1)],
                          op=ALU.subtract, sem=pool)
                p3c.append(pool.n)

            if debug >= 3 or LN2B:
                e.wait("sync", pool, pool.n)
                for i in range(4):
                    e.dma("sync", dbg["ln2"][:, 4096 * i : 4096 * (i + 1)],
                          ln2T[:, 4096 * i : 4096 * (i + 1)], sm["dbgs"])
                e.wait("sync", sm["dbgs"], sm["dbgs"].n)
            if LN2B >= 2:
                for i in range(12):
                    e.dma("sync", dbg["sb1"][:, 4096 * i : 4096 * (i + 1)],
                          sbraw[:][:, 4096 * i : 4096 * (i + 1)], sm["dbgs"])
                e.wait("sync", sm["dbgs"], sm["dbgs"].n)

            if upto <= 3:
                return
            e.wait("gpsimd", pool, pool.n)
            e.wait("gpsimd", pe, pe.n)
            e.wait("vector", sv[0], sv[0].n)
            e.wait("vector", sv[1], sv[1].n)
            e.wait("vector", pe, pe.n)
            if debug >= 1:
                e.wait("vector", sm["dbgs"], sm["dbgs"].n)
            p4_pe = []
            for g in range(p4_groups):
                gsl = g % 2
                wg = WGU_sb[:, 8192 * (2 * gsl) : 8192 * (2 * gsl + 1)]
                wu = WGU_sb[:, 8192 * (2 * gsl + 1) : 8192 * (2 * gsl + 2)]
                if g >= 2:
                    e.wait("gpsimd", pe, p4_pe[g - 2])
                src_g = WgT[:, 512 * g : 512 * (g + 1)].rearrange("(c p) n -> p c n", p=128)
                src_u = WuT[:, 512 * g : 512 * (g + 1)].rearrange("(c p) n -> p c n", p=128)
                wg_v = wg[:].rearrange("p (c n) -> p c n", c=16)
                wu_v = wu[:].rearrange("p (c n) -> p c n", c=16)
                for q4 in range(4):
                    e.dma("gpsimd", wg_v[:, 4 * q4 : 4 * (q4 + 1), :], src_g[:, 4 * q4 : 4 * (q4 + 1), :], gus[gsl])
                    e.dma("gpsimd", wu_v[:, 4 * q4 : 4 * (q4 + 1), :], src_u[:, 4 * q4 : 4 * (q4 + 1), :], gus[gsl])
                e.wait("tensor", gus[gsl], gus[gsl].n)
                for sub in range(4):
                    for half in range(2):
                        i = 8 * g + 2 * sub + half
                        if i >= 1:
                            e.wait("tensor", dve, dve.n)
                        for d in range(16):
                            rhs = ln2T[:, 1024 * d + 512 * half : 1024 * d + 512 * (half + 1)]
                            e.mm(PS[4][:], wg[:, 512 * d + 128 * sub : 512 * d + 128 * (sub + 1)],
                                 rhs, start=(d == 0), stop=(d == 15), sem=pe if d == 15 else None)
                        for d in range(16):
                            rhs = ln2T[:, 1024 * d + 512 * half : 1024 * d + 512 * (half + 1)]
                            e.mm(PS[5][:], wu[:, 512 * d + 128 * sub : 512 * d + 128 * (sub + 1)],
                                 rhs, start=(d == 0), stop=(d == 15), sem=pe if d == 15 else None)
                        e.wait("scalar", pe, pe.n)
                        sgs = sg_sb[:, 512 * (i % 2) : 512 * (i % 2 + 1)]
                        e.act(sgs, PS[4][:], AF.Silu, sem=acts)
                        e.wait("vector", acts, acts.n)
                        hc_g = 4 * g + sub
                        if half == 0:
                            dst = gated_lo if hc_g < 32 else gated_hi
                            dcol = 512 * (hc_g % 32)
                            e.vop("tensor_tensor", dst[:, dcol : dcol + 512], sgs, PS[5][:],
                                  op=ALU.mult, sem=dve)
                        else:
                            stg = vst[:, 1024 * (hc_g % 2) : 1024 * (hc_g % 2) + 512]
                            e.wait("vector", st_gb, st_gb.n)
                            e.vop("tensor_tensor", stg, sgs, PS[5][:], op=ALU.mult, sem=dve)
                            e.wait("sync", dve, dve.n)
                            e.dma("sync", gB_dram[128 * hc_g : 128 * (hc_g + 1), :], stg, st_gb)
                p4_pe.append(pe.n)

            if upto <= 4:
                return
            e.wait("gpsimd", pe, pe.n)
            e.wait("sync", pe, pe.n)
            e.wait("sync", st_gb, st_gb.n)
            if LN2B:
                for i in range(4):
                    e.dma("sync", dbg["ln2b"][:, 4096 * i : 4096 * (i + 1)],
                          ln2T[:, 4096 * i : 4096 * (i + 1)], sm["dbgs"])
                e.wait("sync", sm["dbgs"], sm["dbgs"].n)
            if LN2B >= 2:
                for i in range(12):
                    e.dma("sync", dbg["sb2"][:, 4096 * i : 4096 * (i + 1)],
                          sbraw[:][:, 4096 * i : 4096 * (i + 1)], sm["dbgs"])
                e.wait("sync", sm["dbgs"], sm["dbgs"].n)
            p5_pe = []
            for sc in range(8):
                wsl = sc % 2
                wd = WD_sb[:, 16384 * wsl : 16384 * (wsl + 1)]
                if sc >= 2:
                    e.wait("gpsimd", pe, p5_pe[sc - 2])
                for hc in range(8):
                    e.dma("gpsimd", wd[:, 2048 * hc : 2048 * (hc + 1)],
                          WdT[1024 * sc + 128 * hc : 1024 * sc + 128 * (hc + 1), :], wds[wsl])
                e.wait("tensor", wds[wsl], wds[wsl].n)
                gsl = sc % 2
                gbuf = gbuf2[:, 4096 * gsl : 4096 * (gsl + 1)]
                if sc >= 2:
                    e.wait("sync", pe, p5_pe[sc - 2])
                srcg = gB_dram[1024 * sc : 1024 * (sc + 1), :].rearrange("(c p) n -> p c n", p=128)
                e.dma("sync", gbuf[:].rearrange("p (c n) -> p c n", c=8), srcg, gbs[gsl])
                for half in range(2):
                    if half == 1:
                        e.wait("tensor", gbs[gsl], gbs[gsl].n)
                    for dq in range(4):
                        e.wait("tensor", dve, dve.n)
                        for dc4 in range(4):
                            dci = 4 * dq + dc4
                            ps = PS[dc4]
                            for hc in range(8):
                                hc_g = 8 * sc + hc
                                if half == 0:
                                    gsrc = gated_lo if hc_g < 32 else gated_hi
                                    rhs = gsrc[:, 512 * (hc_g % 32) : 512 * (hc_g % 32 + 1)]
                                else:
                                    rhs = gbuf[:, 512 * hc : 512 * (hc + 1)]
                                e.mm(ps[:], wd[:, 2048 * hc + 128 * dci : 2048 * hc + 128 * (dci + 1)],
                                     rhs, start=(hc == 0), stop=(hc == 7),
                                     sem=pe if hc == 7 else None)
                        e.wait("vector", pe, pe.n)
                        for dc4 in range(4):
                            dci = 4 * dq + dc4
                            stg = st32[:, 512 * (dc4 % 2) : 512 * (dc4 % 2 + 1)]
                            e.wait("vector", st_acc, st_acc.n)
                            e.vop("tensor_copy", stg, PS[dc4][:], sem=dve)
                            e.wait("gpsimd", dve, dve.n)
                            dst = acc_dram[128 * dci : 128 * (dci + 1), 512 * half : 512 * (half + 1)]
                            e.dma("gpsimd", dst, stg, st_acc, acc=(sc > 0))
                p5_pe.append(pe.n)

            if p5_mode <= 1:
                return
            e.wait("sync", st_acc, st_acc.n)
            e.wait("sync", dve, dve.n)
            for dc in range(16):
                ac = aosb_a[:, 0:1024]
                if dc >= 1:
                    e.wait("sync", dve, dve.n)
                e.dma("sync", ac, acc_dram[128 * dc : 128 * (dc + 1), :], ld)
                e.wait("vector", ld, ld.n)
                stats_chunk(dc, ac)
            rows_pass(1.0 / D)
            if p5_mode <= 2:
                return
            p5b = []
            e.wait("sync", st_x2, st_x2.n)
            e.wait("sync", dve, dve.n)
            for dc in range(16):
                ac = aosb_a[:, 0:1024]
                x2c = aosb_b[:, 0:1024]
                if dc >= 1:
                    e.wait("sync", st_y, 16 * dc)
                    e.wait("sync", dve, dve.n)
                e.dma("sync", ac, acc_dram[128 * dc : 128 * (dc + 1), :], ld)
                e.dma("sync", x2c, x2_dram[128 * dc : 128 * (dc + 1), :], ld)
                e.wait("vector", ld, ld.n)
                e.wait("vector", dve, dve.n)
                for hh in range(2):
                    t1 = t1b[:, 512 * hh : 512 * (hh + 1)]
                    e.vop("tensor_tensor", t1, ac[:, 512 * hh : 512 * (hh + 1)],
                          bc_sb[:, 1024 * hh : 1024 * hh + 512], op=ALU.mult, sem=dve)
                    e.wait("vector", dve, dve.n)
                    e.vop("tensor_sub", t1, t1, bc_sb[:, 1024 * hh + 512 : 1024 * (hh + 1)], sem=dve)
                    e.wait("vector", dve, dve.n)
                    e.vop("tensor_add", ac[:, 512 * hh : 512 * (hh + 1)],
                          x2c[:, 512 * hh : 512 * (hh + 1)], t1, sem=dve)
                e.wait("sync", dve, dve.n)
                e.dma("sync", y[128 * dc : 128 * (dc + 1), :], ac, st_y)
                p5b.append(st_y.n)


        @blk.sync
        def _(eng):
            program(E("sync", eng))

        @blk.gpsimd
        def _(eng):
            program(E("gpsimd", eng))

        @blk.tensor
        def _(eng):
            program(E("tensor", eng))

        @blk.scalar
        def _(eng):
            program(E("scalar", eng))

        @blk.vector
        def _(eng):
            program(E("vector", eng))

    return nc


def make_inputs(c, x, Wq, bq, Wk, bk, Wv, bv, Wo, bo, w_gate, w_up, w_down):
    xf = np.ascontiguousarray(x.reshape(S, D))
    rows = np.concatenate([np.arange(1024 * t + 128 * c, 1024 * t + 128 * (c + 1))
                           for t in range(NT)])
    btab_arr, _ = build_bias_table(c)
    return {
        "xT": np.ascontiguousarray(xf.T),
        "xoT": np.ascontiguousarray(xf[rows].T),
        "Wq": np.ascontiguousarray(Wq), "Wk": np.ascontiguousarray(Wk),
        "Wv": np.ascontiguousarray(Wv), "Wo": np.ascontiguousarray(Wo),
        "bq": np.ascontiguousarray(bq.reshape(H, 128).T),
        "bk": np.ascontiguousarray(bk.reshape(H, 128).T),
        "bo": np.ascontiguousarray(bo.reshape(H, 128).T),
        "bvr": bv.reshape(1, D).astype(ml_dtypes.bfloat16),
        "WgT": np.ascontiguousarray(w_gate.T),
        "WuT": np.ascontiguousarray(w_up.T),
        "WdT": np.ascontiguousarray(w_down.T),
        "btab": btab_arr,
        "masks": build_masks(c),
        "identb": np.eye(128, dtype=ml_dtypes.bfloat16),
        "onecol": np.ones((128, 1), dtype=ml_dtypes.bfloat16),
        "onerow_b": np.ones((1, 128), dtype=ml_dtypes.bfloat16),
        "onecol_f": np.ones((128, 1), dtype=ml_dtypes.bfloat16),
        "onerow_f": np.ones((1, 128), dtype=np.float32),
        "zerob": np.zeros((128, 128), dtype=ml_dtypes.bfloat16),
        "fconst": np.array([[EPS, 0.0, 1.0, 0, 0, 0, 0, 0]], dtype=np.float32),
        "fcb": np.tile(np.array([[0.0, 1.0]], dtype=np.float32), (128, 1)),
        "fcu": np.full((128, 4), 127, dtype=np.uint8),
    }


def run(inputs, debug=0, upto=99, p4_groups=16, p5_mode=3):
    args = [np.asarray(inputs[k], dtype=np.float32) for k in
            ("L0_attn_x", "L0_attn_Wq", "L0_attn_bq", "L0_attn_Wk", "L0_attn_bk",
             "L0_attn_Wv", "L0_attn_bv", "L0_attn_Wo", "L0_attn_bo",
             "L0_ffn_w_gate", "L0_ffn_w_up", "L0_ffn_w_down")]
    nc = build_nc(debug=debug, upto=upto, p4_groups=p4_groups, p5_mode=p5_mode)
    in_maps = [make_inputs(c, args[0], args[1], args[2], args[3], args[4], args[5],
                           args[6], args[7], args[8], args[9], args[10], args[11])
               for c in range(NC)]
    res = run_bass_kernel_spmd(nc, in_maps, core_ids=list(range(NC)))
    out = np.zeros((S, D), dtype=np.float32)
    for c in range(NC):
        yc = np.asarray(res.results[c]["y"]).T
        for t in range(NT):
            out[1024 * t + 128 * c : 1024 * t + 128 * (c + 1)] = yc[128 * t : 128 * (t + 1)]
    return out.reshape(1, S, D), res


def kernel(**inputs):
    out, _ = run(inputs, debug=0)
    return out


# version 08
# speedup vs baseline: 1.0692x; 1.0692x over previous
import numpy as np
import ml_dtypes
import concourse.bass as bass
import concourse.mybir as mybir
from concourse.bass_utils import run_bass_kernel_spmd
from contextlib import ExitStack

F32 = mybir.dt.float32
BF16 = mybir.dt.bfloat16
AF = mybir.ActivationFunctionType
ALU = mybir.AluOpType

S = 8192
D = 2048
H = 16
HD = 128
NC = 8
ROWS = S // NC
NT = 8
DH = 8192
EPS = 1e-5
SCALE = 1.0 / np.sqrt(HD)
T_BAND = 150.0
NOQFDUMP = 0
LN2B = 0
SBMAP = {}
NEG = -3.0e38
MASKNEG = -1.0e30

SLOPES = [2.0 ** (-0.5 * (h + 1)) for h in range(H)]
GSIZE = [1, 1, 1, 1, 1, 1, 1, 1, 1, 2, 2, 4, 4, 4, 4, 4]
HALVED = [True, True] + [False] * 14


def keep_subtile(h, t_near, b, s):
    dist = 1024 * (t_near - b) - 128 * s - 127
    return SLOPES[h] * max(0, dist) <= T_BAND


def attention_schedule():
    out = []
    for h in range(H):
        G = GSIZE[h]
        for tg in range(2):
            items = []
            for qg in range(4 // G):
                t_lo = tg * 4 + qg * G
                t_hi = t_lo + G - 1
                for b in range(t_lo):
                    for s in range(NC):
                        if not keep_subtile(h, t_lo, b, s):
                            continue
                        if HALVED[h]:
                            for hf in range(2):
                                items.append((b, s, ("n", h, t_hi - b, s, hf),
                                              128 * t_lo + 64 * hf, 64,
                                              128 * (t_lo - 4 * tg) + 64 * hf, None))
                        else:
                            items.append((b, s, ("n", h, t_hi - b, s, 0),
                                          128 * t_lo, 128 * G, 128 * (t_lo - 4 * tg), None))
                for t in range(t_lo, t_hi + 1):
                    for b in range(t_lo, t + 1):
                        diag = (b == t)
                        for s in range(NC):
                            if not keep_subtile(h, t, b, s):
                                continue
                            halves = (0, 1) if HALVED[h] else (0,)
                            for hf in halves:
                                n = 64 if HALVED[h] else 128
                                key = (("d", h, t_hi - t, s, hf) if diag
                                       else ("n", h, t_hi - b, s, hf))
                                mask = (128 * s + 64 * hf) if diag else None
                                items.append((b, s, key,
                                              128 * t + 64 * hf, n,
                                              128 * (t - 4 * tg) + 64 * hf, mask))
            out.append((h, tg, items))
    return out


SCHED = attention_schedule()


def build_bias_table(c):
    keys = []
    seen = set()
    for (_h, _tg, items) in SCHED:
        for (_b, _s, key, _qc, _n, _cc, _m) in items:
            if key not in seen:
                seen.add(key)
                keys.append(key)
    p = np.arange(128, dtype=np.float64)
    cols = {}
    data = []
    for key in keys:
        kind, h, dlt, s, hf = key
        sl = SLOPES[h]
        if kind == "d" and s > c:
            v = np.full(128, NEG)
        else:
            v = sl * (-1024.0 * dlt + 128.0 * (s - c) - 64.0 * hf + p)
        cols[key] = len(data)
        data.append(v)
    a = np.stack(data, axis=1).astype(np.float32)
    ncol = int(np.ceil(a.shape[1] / 16.0) * 16)
    arr = np.zeros((128, max(ncol, 16)), dtype=np.float32)
    arr[:, : a.shape[1]] = a
    return arr, cols


BT_ARR0, COLS = build_bias_table(0)
NCOL = BT_ARR0.shape[1]


def build_masks(c):
    m = np.zeros((128, NC * 128), dtype=np.float32)
    tri = np.where(np.arange(128)[:, None] <= np.arange(128)[None, :], 0.0, MASKNEG)
    m[:, 128 * c : 128 * (c + 1)] = tri
    return m.astype(ml_dtypes.bfloat16)


class Sem:
    __slots__ = ("h", "n")

    def __init__(self, h):
        self.h = h
        self.n = 0


class E:

    def __init__(self, which, eng):
        self.w = which
        self.e = eng

    def _fin(self, inst, sem, k):
        if sem is not None:
            if inst is not None:
                inst.then_inc(sem.h, k)
            sem.n += k
        return inst

    def dma(self, q, out, in_, sem=None, k=16, acc=False):
        i = None
        if self.w == q:
            if acc:
                i = self.e.dma_start(out=out, in_=in_, accum_op=ALU.add)
            else:
                i = self.e.dma_start(out=out, in_=in_)
        return self._fin(i, sem, k)

    def wait(self, q, sem, val=None):
        v = sem.n if val is None else val
        if self.w == q and v > 0:
            self.e.wait_ge(sem.h, v)

    def mm(self, out, lhsT, rhs, start, stop, sem=None, k=1):
        i = None
        if self.w == "tensor":
            i = self.e.matmul(out, lhsT, rhs, start=start, stop=stop)
        return self._fin(i, sem, k)

    def act(self, out, in_, func, bias=0.0, scale=1.0, sem=None, k=1):
        i = None
        if self.w == "scalar":
            i = self.e.activation(out, in_, func, bias=bias, scale=scale)
        return self._fin(i, sem, k)

    def act_table(self, func):
        if self.w == "scalar":
            self.e.preload_act_table(func)

    def vop(self, fn, *args, sem=None, k=1, **kw):
        i = None
        if self.w == "vector":
            i = getattr(self.e, fn)(*args, **kw)
        return self._fin(i, sem, k)

    def pop(self, fn, *args, sem=None, k=1, **kw):
        i = None
        if self.w == "gpsimd":
            i = getattr(self.e, fn)(*args, **kw)
        return self._fin(i, sem, k)


def build_nc(debug=0, upto=99, p4_groups=16, p5_mode=3):
    nc = bass.Bass("TRN2", target_bir_lowering=False, dynamic_dma_scratch_size=4096)

    ap = lambda t: t.ap()
    xT = ap(nc.dram_tensor("xT", [D, S], F32, kind="ExternalInput"))
    xoT = ap(nc.dram_tensor("xoT", [D, ROWS], F32, kind="ExternalInput"))
    Wq = ap(nc.dram_tensor("Wq", [D, D], F32, kind="ExternalInput"))
    Wk = ap(nc.dram_tensor("Wk", [D, D], F32, kind="ExternalInput"))
    Wv = ap(nc.dram_tensor("Wv", [D, D], F32, kind="ExternalInput"))
    Wo = ap(nc.dram_tensor("Wo", [D, D], F32, kind="ExternalInput"))
    bq = ap(nc.dram_tensor("bq", [128, H], F32, kind="ExternalInput"))
    bk = ap(nc.dram_tensor("bk", [128, H], F32, kind="ExternalInput"))
    bo = ap(nc.dram_tensor("bo", [128, H], F32, kind="ExternalInput"))
    bvr = ap(nc.dram_tensor("bvr", [1, D], BF16, kind="ExternalInput"))
    WgT = ap(nc.dram_tensor("WgT", [D, DH], F32, kind="ExternalInput"))
    WuT = ap(nc.dram_tensor("WuT", [D, DH], F32, kind="ExternalInput"))
    WdT = ap(nc.dram_tensor("WdT", [DH, D], F32, kind="ExternalInput"))
    btab = ap(nc.dram_tensor("btab", [128, NCOL], F32, kind="ExternalInput"))
    masks = ap(nc.dram_tensor("masks", [128, NC * 128], BF16, kind="ExternalInput"))
    identb = ap(nc.dram_tensor("identb", [128, 128], BF16, kind="ExternalInput"))
    onecol = ap(nc.dram_tensor("onecol", [128, 1], BF16, kind="ExternalInput"))
    onerow_b = ap(nc.dram_tensor("onerow_b", [1, 128], BF16, kind="ExternalInput"))
    onecol_f = ap(nc.dram_tensor("onecol_f", [128, 1], BF16, kind="ExternalInput"))
    onerow_f = ap(nc.dram_tensor("onerow_f", [1, 128], F32, kind="ExternalInput"))
    zerob = ap(nc.dram_tensor("zerob", [128, 128], BF16, kind="ExternalInput"))
    fconst = ap(nc.dram_tensor("fconst", [1, 8], F32, kind="ExternalInput"))
    fcb = ap(nc.dram_tensor("fcb", [128, 2], F32, kind="ExternalInput"))
    fcu = ap(nc.dram_tensor("fcu", [128, 4], mybir.dt.uint8, kind="ExternalInput"))

    y = ap(nc.dram_tensor("y", [D, ROWS], F32, kind="ExternalOutput"))

    ikind = "ExternalOutput" if debug >= 1 else "Internal"
    ln_dram = ap(nc.dram_tensor("ln_dram", [D, S], BF16, kind=ikind))
    K_g = ap(nc.dram_tensor("K_g", [D, S], BF16, kind=ikind))
    V_g = ap(nc.dram_tensor("V_g", [S, D], BF16, kind=ikind))
    ao_dram = ap(nc.dram_tensor("ao_dram", [D, ROWS], F32, kind=ikind))
    x2_dram = ap(nc.dram_tensor("x2_dram", [D, ROWS], F32, kind=ikind))
    gB_dram = ap(nc.dram_tensor("gB_dram", [DH, 512], BF16, kind=ikind))
    acc_dram = ap(nc.dram_tensor("acc_dram", [D, ROWS], F32, kind=ikind))

    pad_sink = ap(nc.dram_tensor("pad_sink", [1, 64], mybir.dt.uint8, kind="Internal"))

    dbg = {}
    if debug >= 1:
        dbg["Qf"] = ap(nc.dram_tensor("dbg_Qf", [128, H * 1024], BF16, kind="ExternalOutput"))
    if debug >= 2:
        dbg["ctx"] = ap(nc.dram_tensor("dbg_ctx", [128, H * 1024], BF16, kind="ExternalOutput"))
    if debug >= 3:
        dbg["x2"] = ap(nc.dram_tensor("dbg_x2", [D, ROWS], F32, kind="ExternalOutput"))
    if debug >= 3 or LN2B:
        dbg["ln2"] = ap(nc.dram_tensor("dbg_ln2", [128, 16 * 1024], BF16, kind="ExternalOutput"))
    if LN2B:
        dbg["ln2b"] = ap(nc.dram_tensor("dbg_ln2b", [128, 16 * 1024], BF16, kind="ExternalOutput"))
    if LN2B >= 2:
        U8 = mybir.dt.uint8
        dbg["sb1"] = ap(nc.dram_tensor("dbg_sb1", [128, 49152], U8, kind="ExternalOutput"))
        dbg["sb2"] = ap(nc.dram_tensor("dbg_sb2", [128, 49152], U8, kind="ExternalOutput"))
    if debug >= 5:
        dbg["rows"] = ap(nc.dram_tensor("dbg_rows", [1, 3072], F32, kind="ExternalOutput"))
        dbg["bc"] = ap(nc.dram_tensor("dbg_bc", [128, 2048], F32, kind="ExternalOutput"))
        dbg["xbb"] = ap(nc.dram_tensor("dbg_xbb", [128, 1024], BF16, kind="ExternalOutput"))
        dbg["sqb"] = ap(nc.dram_tensor("dbg_sqb", [128, 1024], BF16, kind="ExternalOutput"))
        dbg["pl"] = ap(nc.dram_tensor("dbg_pl", [1, 8192], F32, kind="ExternalOutput"))

    with ExitStack() as stk:
        EE = stk.enter_context

        from math import prod
        ring_pad = nc.alloc_sbuf_tensor_at("ring_pad", [128, 12288], mybir.dt.uint8, offset=4096)
        _cur = [16384]
        sbraw = nc.alloc_sbuf_tensor_at("sbraw", [128, 49152], mybir.dt.uint8, offset=4096) if LN2B >= 2 else None

        def region(tensors):
            base = _cur[0]
            out = []
            end = base
            for name, shape, dt in tensors:
                out.append(nc.alloc_sbuf_tensor_at(name, shape, dt, offset=base))
                SBMAP[name] = (base, base + prod(shape[1:]) * mybir.dt.size(dt))
                end = max(end, base + prod(shape[1:]) * mybir.dt.size(dt))
            _cur[0] = (end + 31) // 32 * 32
            assert _cur[0] <= nc.sbuf_top, f"SBUF overflow: {_cur[0]} > {nc.sbuf_top}"
            return out if len(out) > 1 else out[0]

        btab_sb = region([("btab_sb", [128, NCOL], F32)])
        masks_sb = region([("masks_sb", [128, NC * 128], BF16)])
        identb_sb = region([("identb_sb", [128, 128], BF16)])
        onecol_sb = region([("onecol_sb", [128, 1], BF16)])
        onerow_b_sb = region([("onerow_b_sb", [1, 128], BF16)])
        onecol_f_sb = region([("onecol_f_sb", [128, 1], BF16)])
        onerow_f_sb = region([("onerow_f_sb", [1, 128], F32)])
        zerob_sb = region([("zerob_sb", [128, 128], BF16)])
        bq_sb = region([("bq_sb", [128, H], F32)])
        bk_sb = region([("bk_sb", [128, H], F32)])
        bo_sb = region([("bo_sb", [128, H], F32)])
        fc_sb = region([("fc_sb", [1, 8], F32)])
        fcb_sb = region([("fcb_sb", [128, 2], F32)])
        fcu_sb = region([("fcu_sb", [128, 4], mybir.dt.uint8)])

        LNO, ln_sl2, ln2T, gbuf2 = region([
            ("LNO", [128, 16 * 1024], BF16),
            ("ln_sl2", [128, 2 * 8192], BF16),
            ("ln2T", [128, 16 * 1024], BF16),
            ("gbuf2", [128, 2 * 4096], BF16),
        ])
        Qf, gated_lo = region([
            ("Qf", [128, H * 1024], BF16),
            ("gated_lo", [128, 32 * 512], BF16),
        ])
        ctx_fm, gated_hi = region([
            ("ctx_fm", [128, H * 1024], BF16),
            ("gated_hi", [128, 32 * 512], BF16),
        ])
        W_sb, KV_sb, WGU_sb, WD_sb = region([
            ("W_sb", [128, 16 * 2048], BF16),
            ("KV_sb", [128, 2 * 16384], BF16),
            ("WGU_sb", [128, 4 * 8192], BF16),
            ("WD_sb", [128, 2 * 16384], BF16),
        ])
        xst_b = region([("xst_b", [128, 2 * 512], F32)])
        xnm_b = region([("xnm_b", [128, 2 * 512], F32)])
        sqb = region([("sqb", [128, 2 * 512], BF16)])
        xbb = region([("xbb", [128, 2 * 512], BF16)])
        kst, vst, ptil = region([
            ("kst", [128, 2 * 512], BF16),
            ("vst", [128, 2 * 1024], BF16),
            ("ptil", [128, 3 * 512], BF16),
        ])
        aosb_a = nc.alloc_sbuf_tensor_at("aosb_a", [128, 1024], F32, offset=SBMAP["xnm_b"][0])
        aosb_b = nc.alloc_sbuf_tensor_at("aosb_b", [128, 1024], F32, offset=SBMAP["kst"][0])
        bvr_sb = nc.alloc_sbuf_tensor_at("bvr_sb2", [1, D], BF16, offset=SBMAP["ctx_fm"][0])
        rows_sb = region([("rows_sb", [1, 3072], F32)])
        linv_sb = rows_sb
        bc_sb = region([("bc_sb", [128, 4 * 512], F32)])
        t1b, lbc_sb, sg_sb, st32 = region([
            ("t1b", [128, 2 * 512], F32),
            ("lbc_sb", [128, 2 * 512], F32),
            ("sg_sb", [128, 2 * 512], F32),
            ("st32", [128, 2 * 512], F32),
        ])

        PS = [EE(nc.psum_tensor(f"PS{i}", [128, 512], F32)) for i in range(6)]
        PL = [EE(nc.psum_tensor(f"PL{i}", [1, 512], F32)) for i in range(2)]

        nc.const_aps.register(F32, EPS, fc_sb[0:1, 0:1])
        nc.const_aps.register(F32, 0.0, fcb_sb[:, 0:1])
        nc.const_aps.register(F32, 1.0, fcb_sb[:, 1:2])
        nc.const_aps.register(BF16, 1.0, onecol_sb[:, 0:1])
        nc.const_aps.register(mybir.dt.uint8, 127, fcu_sb[:, 0:1])
        sem_names = ["ld", "ldc", "pe", "acts", "dve", "pool", "sto", "pe2", "acte",
                     "la0", "la1", "ln0", "ln1", "lb0", "lb1", "kv0", "kv1",
                     "gu0", "gu1", "wd0", "wd1", "gb0", "gb1",
                     "sk0", "sk1", "sv0", "sv1", "sl0", "sl1",
                     "st_ao", "st_x2", "st_acc", "st_gb", "st_kv", "st_y", "dbgs"]
        sem_h = {n: EE(nc.semaphore(n)) for n in sem_names}

        blk = EE(nc.Block())
        dbg_state = {}

        def ln_fm(e, sm, dram_col0, dram_src, n, out, war_sem=None, war_val=0):
            ld = sm["ld"]; pe = sm["pe"]; acts = sm["acts"]; dve = sm["dve"]; pool = sm["pool"]
            la = (sm["la0"], sm["la1"])
            lnS = (sm["ln0"], sm["ln1"])
            nh = n // 512
            sums = [(PL[0], PL[1]), (PS[4], PS[5])]
            cnt = 0
            war = []
            for g in range(16):
                for hh in range(nh):
                    sl = cnt % 2
                    xc = xst_b[:, 512 * sl : 512 * (sl + 1)]
                    xb = xbb[:, 512 * sl : 512 * (sl + 1)]
                    sq = sqb[:, 512 * sl : 512 * (sl + 1)]
                    if cnt >= 2:
                        e.wait("sync", pe, war[cnt - 2])
                    e.dma("sync", xc, dram_src[128 * g : 128 * (g + 1),
                                              dram_col0 + 512 * hh : dram_col0 + 512 * (hh + 1)], la[sl])
                    e.wait("scalar", la[sl], la[sl].n)
                    if cnt >= 2:
                        e.wait("scalar", pe, war[cnt - 2])
                    e.act(xb, xc, AF.Copy, sem=acts)
                    e.wait("vector", acts, acts.n)
                    e.vop("tensor_mul", sq, xb, xb, sem=dve)
                    e.wait("tensor", dve, dve.n)
                    ts_, tq_ = sums[hh]
                    e.mm(ts_[0:1, 0:512], onecol_f_sb[:], xb, start=(g == 0), stop=(g == 15))
                    e.mm(tq_[0:1, 0:512], onecol_f_sb[:], sq, start=(g == 0), stop=(g == 15),
                         sem=pe)
                    war.append(pe.n)
                    cnt += 1
            if debug >= 5 and not dbg_state.get("pl_dumped"):
                dbg_state["pl_dumped"] = True
                e.wait("vector", pe, pe.n)
                e.vop("tensor_copy", t1b[:, 0:512][0:1, :], PL[0][0:1, :], sem=dve)
                e.vop("tensor_copy", t1b[:, 512:1024][0:1, :], PL[1][0:1, :], sem=dve)
                e.wait("sync", dve, dve.n)
                e.dma("sync", dbg["pl"][0:1, 0:1024], t1b[0:1, 0:1024], sm["dbgs"])
            r_mean = rows_sb[0:1, 0:n]
            r_var = rows_sb[0:1, 1024 : 1024 + n]
            r_rstd = rows_sb[0:1, 2048 : 2048 + n]
            e.wait("vector", pe, pe.n)
            for hh in range(nh):
                ts_, tq_ = sums[hh]
                e.vop("tensor_scalar_mul", r_mean[0:1, 512 * hh : 512 * (hh + 1)],
                      ts_[0:1, 0:512], 1.0 / D, sem=dve)
                e.vop("tensor_scalar_mul", r_var[0:1, 512 * hh : 512 * (hh + 1)],
                      tq_[0:1, 0:512], 1.0 / D, sem=dve)
            dbgrows = debug >= 5 and not dbg_state.get("rows_dumped")
            def snap(i):
                if dbgrows:
                    e.wait("sync", dve, dve.n)
                    e.wait("sync", acts, acts.n)
                    e.dma("sync", dbg["pl"][0:1, 1024 * i : 1024 * i + 512],
                          rows_sb[0:1, 1024:1536], sm["dbgs"])
                    e.dma("sync", dbg["pl"][0:1, 1024 * i + 512 : 1024 * (i + 1)],
                          rows_sb[0:1, 2048:2560], sm["dbgs"])
                    e.wait("vector", sm["dbgs"], sm["dbgs"].n)
                    e.wait("scalar", sm["dbgs"], sm["dbgs"].n)
            e.wait("vector", dve, dve.n)
            snap(1)
            e.vop("tensor_tensor", r_rstd, r_mean, r_mean, op=ALU.mult, sem=dve)
            e.wait("vector", dve, dve.n)
            snap(2)
            e.vop("tensor_sub", r_var, r_var, r_rstd, sem=dve)
            e.wait("scalar", dve, dve.n)
            snap(3)
            e.act(r_rstd, r_var, AF.Ln, bias=EPS, sem=acts)
            e.wait("scalar", acts, acts.n)
            snap(4)
            e.act(r_rstd, r_rstd, AF.Exp, scale=-0.5, sem=acts)
            e.wait("vector", acts, acts.n)
            snap(5)
            if dbgrows:
                dbg_state["rows_dumped"] = True
            e.vop("tensor_tensor", r_var, r_mean, r_rstd, op=ALU.mult, sem=dve)
            e.wait("tensor", dve, dve.n)
            for hh in range(nh):
                if hh >= 1:
                    e.wait("tensor", dve, dve.n)
                e.mm(PS[4][:, 0:512], onerow_f_sb[:], r_rstd[0:1, 512 * hh : 512 * (hh + 1)],
                     start=True, stop=True, sem=pe)
                e.mm(PS[5][:, 0:512], onerow_f_sb[:], r_var[0:1, 512 * hh : 512 * (hh + 1)],
                     start=True, stop=True, sem=pe)
                e.wait("vector", pe, pe.n)
                e.vop("tensor_copy", bc_sb[:, 1024 * hh : 1024 * hh + 512], PS[4][:, 0:512], sem=dve)
                e.vop("tensor_copy", bc_sb[:, 1024 * hh + 512 : 1024 * (hh + 1)], PS[5][:, 0:512],
                      sem=dve)
            if war_sem is not None:
                e.wait("gpsimd", war_sem, war_val)
            cnt = 0
            dwar = []
            pwar = []
            for g in range(16):
                for hh in range(nh):
                    sl = cnt % 2
                    xc = xnm_b[:, 512 * sl : 512 * (sl + 1)]
                    t1 = t1b[:, 512 * sl : 512 * (sl + 1)]
                    if cnt >= 2:
                        e.wait("sync", dve, dwar[cnt - 2])
                    e.dma("sync", xc, dram_src[128 * g : 128 * (g + 1),
                                              dram_col0 + 512 * hh : dram_col0 + 512 * (hh + 1)], lnS[sl])
                    if cnt >= 2:
                        e.wait("vector", pool, pwar[cnt - 2])
                    e.wait("vector", lnS[sl], lnS[sl].n)
                    e.wait("vector", dve, dve.n)
                    e.vop("tensor_tensor", t1, xc,
                          bc_sb[:, 1024 * hh : 1024 * hh + 512], op=ALU.mult, sem=dve)
                    dwar.append(dve.n)
                    e.wait("gpsimd", dve, dve.n)
                    e.pop("tensor_tensor", out(g)[:, 512 * hh : 512 * (hh + 1)],
                          t1, bc_sb[:, 1024 * hh + 512 : 1024 * (hh + 1)],
                          op=ALU.subtract, sem=pool)
                    pwar.append(pool.n)
                    cnt += 1

        def program(e):
            dbg_state.clear()
            sm = {n: Sem(h) for n, h in sem_h.items()}
            phases(e, sm)
            for nm in ("st_y", "dbgs", "sto", "st_ao", "st_x2", "st_acc", "st_gb",
                       "sk0", "sk1", "sv0", "sv1", "sl0", "sl1"):
                e.wait("sync", sm[nm])
            e.wait("gpsimd", sm["st_y"])

        def phases(e, sm):
            ld = sm["ld"]; ldc = sm["ldc"]; pe = sm["pe"]; acts = sm["acts"]
            dve = sm["dve"]; pool = sm["pool"]; sto = sm["sto"]
            pe2 = sm["pe2"]; acte = sm["acte"]
            sk = (sm["sk0"], sm["sk1"]); sv = (sm["sv0"], sm["sv1"])
            slst = (sm["sl0"], sm["sl1"]); kvs = (sm["kv0"], sm["kv1"])
            gus = (sm["gu0"], sm["gu1"]); wds = (sm["wd0"], sm["wd1"])
            gbs = (sm["gb0"], sm["gb1"])
            st_ao = sm["st_ao"]; st_x2 = sm["st_x2"]; st_acc = sm["st_acc"]
            st_gb = sm["st_gb"]; st_kv = sm["st_kv"]; st_y = sm["st_y"]

            for dst, src in [(fc_sb, fconst), (fcb_sb, fcb), (fcu_sb, fcu), (btab_sb, btab), (masks_sb, masks), (identb_sb, identb),
                             (onecol_sb, onecol), (onerow_b_sb, onerow_b),
                             (onecol_f_sb, onecol_f), (onerow_f_sb, onerow_f),
                             (zerob_sb, zerob), (bq_sb, bq), (bk_sb, bk),
                             (bo_sb, bo), (bvr_sb, bvr)]:
                e.dma("sync", dst[:], src, ld)
            for q in ("sync", "gpsimd", "tensor", "scalar", "vector"):
                e.wait(q, ld)

            ln_fm(e, sm, 0, xoT, 1024,
                  out=lambda g: LNO[:, 1024 * g : 1024 * (g + 1)])
            lno_done = pool.n
            if debug >= 5:
                e.wait("sync", pool, pool.n)
                e.dma("sync", dbg["rows"], rows_sb[0:1, :], sm["dbgs"])
                e.dma("sync", dbg["bc"], bc_sb[:], sm["dbgs"])
                e.dma("sync", dbg["xbb"], xbb[:], sm["dbgs"])
                e.dma("sync", dbg["sqb"], sqb[:], sm["dbgs"])

            for d in range(16):
                e.dma("gpsimd", W_sb[:, 2048 * d : 2048 * (d + 1)],
                      Wq[128 * d : 128 * (d + 1), :], ldc)
            e.wait("tensor", ldc, ldc.n)
            e.wait("tensor", pool, lno_done)
            q_ev = []
            for h in range(H):
                for half in range(2):
                    i = 2 * h + half
                    ps = PS[i % 2]
                    if i >= 2:
                        e.wait("tensor", acts, q_ev[i - 2])
                    for d in range(16):
                        e.mm(ps[:], W_sb[:, 2048 * d + 128 * h : 2048 * d + 128 * (h + 1)],
                             LNO[:, 1024 * d + 512 * half : 1024 * d + 512 * (half + 1)],
                             start=(d == 0), stop=(d == 15), sem=pe if d == 15 else None)
                    e.wait("scalar", pe, pe.n)
                    e.act(Qf[:, 1024 * h + 512 * half : 1024 * h + 512 * (half + 1)], ps[:],
                          AF.Identity, bias=bq_sb[:, h : h + 1], sem=acts)
                    q_ev.append(acts.n)

            if debug >= 1 and not NOQFDUMP:
                e.wait("sync", acts, acts.n)
                for i in range(4):
                    e.dma("sync", dbg["Qf"][:, 4096 * i : 4096 * (i + 1)],
                          Qf[:, 4096 * i : 4096 * (i + 1)], sm["dbgs"])

            if upto <= 0:
                return
            e.wait("gpsimd", pe, pe.n)
            for d in range(16):
                e.dma("gpsimd", W_sb[:, 2048 * d : 2048 * (d + 1)],
                      Wk[128 * d : 128 * (d + 1), :], ldc)
            wk_ld = ldc.n

            ln_done = []
            ln_st = []
            k_ev = []
            kst_free = []
            kts_pe = []

            def k_pass(ts):
                lnb = ln_sl2[:, (ts % 2) * 8192 : (ts % 2 + 1) * 8192]
                if ts == 0:
                    e.wait("tensor", ldc, wk_ld)
                e.wait("tensor", pool, ln_done[ts])
                for h in range(H):
                    i = 16 * ts + h
                    sl = i % 2
                    ps = PS[2 + sl]
                    if i >= 2:
                        e.wait("tensor", acts, k_ev[i - 2])
                    for d in range(16):
                        e.mm(ps[:], W_sb[:, 2048 * d + 128 * h : 2048 * d + 128 * (h + 1)],
                             lnb[:, 512 * d : 512 * (d + 1)],
                             start=(d == 0), stop=(d == 15), sem=pe if d == 15 else None)
                    e.wait("scalar", pe, pe.n)
                    if i >= 2:
                        e.wait("scalar", sk[sl], 16 * (i // 2))
                    e.act(kst[:, 512 * sl : 512 * (sl + 1)], ps[:], AF.Identity,
                          bias=bk_sb[:, h : h + 1], sem=acts)
                    k_ev.append(acts.n)
                    e.wait("sync", acts, acts.n)
                    e.dma("sync", K_g[128 * h : 128 * (h + 1), 512 * ts : 512 * (ts + 1)],
                          kst[:, 512 * sl : 512 * (sl + 1)], sk[sl])
                kts_pe.append(pe.n)

            for ts in range(16):
                lnb = ln_sl2[:, (ts % 2) * 8192 : (ts % 2 + 1) * 8192]
                if ts >= 2:
                    e.wait("gpsimd", slst[ts % 2], 256 * (ts // 2))
                war_val = kts_pe[ts - 2] if ts >= 2 else 0
                ln_fm(e, sm, 512 * ts, xT, 512,
                      out=lambda g, lnb=lnb: lnb[:, 512 * g : 512 * (g + 1)],
                      war_sem=pe if ts >= 2 else None, war_val=war_val)
                ln_done.append(pool.n)
                e.wait("sync", pool, pool.n)
                for g in range(16):
                    e.dma("sync", ln_dram[128 * g : 128 * (g + 1), 512 * ts : 512 * (ts + 1)],
                          lnb[:, 512 * g : 512 * (g + 1)], slst[ts % 2])
                if ts >= 1:
                    k_pass(ts - 1)
            k_pass(15)

            e.wait("gpsimd", pe, pe.n)
            for d in range(16):
                e.dma("gpsimd", W_sb[:, 2048 * d : 2048 * (d + 1)],
                      Wv[128 * d : 128 * (d + 1), :], ldc)
            e.wait("tensor", ldc, ldc.n)
            e.wait("sync", sm["sl0"], sm["sl0"].n)
            e.wait("sync", sm["sl1"], sm["sl1"].n)
            e.wait("sync", pe, kts_pe[-1])
            e.wait("scalar", sk[0], sk[0].n)
            e.wait("scalar", sk[1], sk[1].n)
            v_ev = []
            vts_pe = []
            lb = (sm["lb0"], sm["lb1"])
            for ts in range(16):
                lnb = ln_sl2[:, (ts % 2) * 8192 : (ts % 2 + 1) * 8192]
                if ts >= 2:
                    e.wait("sync", pe, vts_pe[ts - 2])
                for g in range(16):
                    e.dma("sync", lnb[:, 512 * g : 512 * (g + 1)],
                          ln_dram[128 * g : 128 * (g + 1), 512 * ts : 512 * (ts + 1)], lb[ts % 2])
                e.wait("tensor", lb[ts % 2], lb[ts % 2].n)
                for tt in range(4):
                    for dh in range(2):
                        i = len(v_ev)
                        sl = i % 2
                        pa, pb = PS[2 * sl], PS[2 * sl + 1]
                        if i >= 2:
                            e.wait("tensor", acts, v_ev[i - 2])
                        for d in range(16):
                            lhsT = lnb[:, 512 * d + 128 * tt : 512 * d + 128 * (tt + 1)]
                            e.mm(pa[:], lhsT,
                                 W_sb[:, 2048 * d + 1024 * dh : 2048 * d + 1024 * dh + 512],
                                 start=(d == 0), stop=False)
                            e.mm(pb[:], lhsT,
                                 W_sb[:, 2048 * d + 1024 * dh + 512 : 2048 * d + 1024 * (dh + 1)],
                                 start=(d == 0), stop=False)
                        e.mm(pa[:], onerow_b_sb[:], bvr_sb[0:1, 1024 * dh : 1024 * dh + 512],
                             start=False, stop=True)
                        e.mm(pb[:], onerow_b_sb[:], bvr_sb[0:1, 1024 * dh + 512 : 1024 * (dh + 1)],
                             start=False, stop=True, sem=pe)
                        e.wait("scalar", pe, pe.n)
                        if i >= 2:
                            e.wait("scalar", sv[sl], 16 * (i // 2))
                        e.act(vst[:, 1024 * sl : 1024 * sl + 512], pa[:], AF.Identity)
                        e.act(vst[:, 1024 * sl + 512 : 1024 * (sl + 1)], pb[:], AF.Identity, sem=acts)
                        v_ev.append(acts.n)
                        e.wait("sync", acts, acts.n)
                        e.dma("sync", V_g[512 * ts + 128 * tt : 512 * ts + 128 * (tt + 1),
                                          1024 * dh : 1024 * (dh + 1)],
                              vst[:, 1024 * sl : 1024 * (sl + 1)], sv[sl])
                vts_pe.append(pe.n)

            if upto <= 1:
                return
            for s_ in (sm["sk0"], sm["sk1"], sm["sv0"], sm["sv1"]):
                e.wait("sync", s_, s_.n)
            e.wait("sync", pe, pe.n)
            e.wait("scalar", sv[0], sv[0].n)
            e.wait("scalar", sv[1], sv[1].n)
            e.wait("tensor", acts, acts.n)
            e.wait("tensor", dve, dve.n)
            kv_ld = []
            att_done = []
            ctx_evac = {0: 0, 1: 0}

            def load_kv(h):
                half = h % 2
                kb = KV_sb[:, 16384 * half : 16384 * half + 8192]
                vb = KV_sb[:, 16384 * half + 8192 : 16384 * (half + 1)]
                if h >= 2:
                    e.wait("sync", pe, att_done[h - 2])
                e.dma("sync", kb[:], K_g[128 * h : 128 * (h + 1), :], kvs[half])
                for b in range(NC):
                    src = V_g[1024 * b : 1024 * (b + 1), 128 * h : 128 * (h + 1)]
                    src = src.rearrange("(s k) d -> k s d", k=128)
                    dst = vb[:, 1024 * b : 1024 * (b + 1)].rearrange("k (s d) -> k s d", s=8)
                    e.dma("sync", dst, src, kvs[half])
                kv_ld.append(kvs[half].n)

            load_kv(0)

            q_state = {"q": []}

            def flush(idx):
                q = q_state["q"]
                ent = q[idx]
                if ent["done"]:
                    return
                e.wait("tensor", acte, ent["exp"])
                pt = ptil[:, 512 * ent["slot"] : 512 * ent["slot"] + ent["N"]]
                e.mm(ent["l_ps"][0:1, ent["ctxcol"] : ent["ctxcol"] + ent["N"]],
                     onecol_sb[:], pt, start=False, stop=False)
                e.mm(ent["ctx_ps"][:, ent["ctxcol"] : ent["ctxcol"] + ent["N"]],
                     ent["vb"][:, 1024 * ent["b"] + 128 * ent["s"] : 1024 * ent["b"] + 128 * (ent["s"] + 1)],
                     pt, start=False, stop=False, sem=pe)
                ent["done"] = True
                ent["ctxpe"] = pe.n

            for (h, tg, items) in SCHED:
                half = h % 2
                kb = KV_sb[:, 16384 * half : 16384 * half + 8192]
                vb = KV_sb[:, 16384 * half + 8192 : 16384 * (half + 1)]
                if tg == 0:
                    if h + 1 < H:
                        load_kv(h + 1)
                    e.wait("tensor", kvs[h % 2], kv_ld[h])
                pp = (2 * h + tg) % 2
                ctx_ps = PS[2 + pp]
                l_ps = PL[pp]
                if 2 * h + tg >= 2:
                    e.wait("tensor", dve, ctx_evac[pp])
                e.mm(ctx_ps[:], zerob_sb[:], Qf[:, 0:512], start=True, stop=False)
                e.mm(l_ps[0:1, :], zerob_sb[:, 0:1], Qf[:, 0:512], start=True, stop=False)
                for (b, s, colkey, qoff, N, ctxcol, mask) in items:
                    q = q_state["q"]
                    i = len(q)
                    slot = i % 3
                    sps = PS[(0, 1, 4)[slot]]
                    if i >= 3:
                        e.wait("tensor", acte, q[i - 3]["exp"])
                    e.mm(sps[:, 0:N], kb[:, 1024 * b + 128 * s : 1024 * b + 128 * (s + 1)],
                         Qf[:, 1024 * h + qoff : 1024 * h + qoff + N],
                         start=True, stop=(mask is None), sem=None if mask is not None else pe2)
                    if mask is not None:
                        e.mm(sps[:, 0:N], identb_sb[:], masks_sb[:, mask : mask + N],
                             start=False, stop=True, sem=pe2)
                    e.wait("scalar", pe2, pe2.n)
                    if i >= 3:
                        e.wait("scalar", pe, q[i - 3]["ctxpe"])
                    cidx = COLS[colkey]
                    e.act(ptil[:, 512 * slot : 512 * slot + N], sps[:, 0:N], AF.Exp,
                          bias=btab_sb[:, cidx : cidx + 1], scale=SCALE, sem=acte)
                    q.append(dict(exp=acte.n, slot=slot, N=N, vb=vb, b=b, s=s,
                                  ctx_ps=ctx_ps, l_ps=l_ps, ctxcol=ctxcol,
                                  ctxpe=0, done=False))
                    if i >= 2:
                        flush(i - 2)
                if len(q_state["q"]) >= 2:
                    flush(-2)
                if q_state["q"]:
                    flush(-1)
                e.mm(ctx_ps[:], zerob_sb[:], Qf[:, 0:512], start=False, stop=True)
                e.mm(l_ps[0:1, :], zerob_sb[:, 0:1], Qf[:, 0:512], start=False, stop=True, sem=pe)
                peval = pe.n
                e.wait("vector", pe, peval)
                lrow = linv_sb[0:1, 512 * pp : 512 * (pp + 1)]
                e.vop("reciprocal", lrow, l_ps[0:1, :], sem=dve)
                e.wait("tensor", dve, dve.n)
                e.mm(PS[5][:, 0:512], onerow_f_sb[:], lrow, start=True, stop=True, sem=pe)
                e.wait("vector", pe, pe.n)
                lb = lbc_sb[:, 512 * pp : 512 * (pp + 1)]
                e.vop("tensor_copy", lb, PS[5][:, 0:512], sem=dve)
                e.wait("vector", dve, dve.n)
                e.vop("tensor_tensor", ctx_fm[:, 1024 * h + 512 * tg : 1024 * h + 512 * (tg + 1)],
                      ctx_ps[:], lb, op=ALU.mult, sem=dve)
                ctx_evac[pp] = dve.n
                if tg == 1:
                    att_done.append(pe.n)

            if debug >= 2:
                e.wait("sync", dve, dve.n)
                for i in range(4):
                    e.dma("sync", dbg["ctx"][:, 4096 * i : 4096 * (i + 1)],
                          ctx_fm[:, 4096 * i : 4096 * (i + 1)], sm["dbgs"])

            if upto <= 2:
                return
            e.wait("gpsimd", pe, pe.n)
            for d in range(16):
                e.dma("gpsimd", W_sb[:, 2048 * d : 2048 * (d + 1)],
                      Wo[128 * d : 128 * (d + 1), :], ldc)
            e.wait("tensor", ldc, ldc.n)
            e.wait("tensor", dve, dve.n)

            def rows_pass(scale_div):
                r_mean = rows_sb[0:1, 0:1024]
                r_var = rows_sb[0:1, 1024:2048]
                r_rstd = rows_sb[0:1, 2048:3072]
                r_mrstd = rows_sb[0:1, 1024:2048]
                e.wait("vector", pe, pe.n)
                e.vop("tensor_scalar_mul", r_mean[0:1, 0:512], PL[0][0:1, :], scale_div, sem=dve)
                e.vop("tensor_scalar_mul", r_mean[0:1, 512:1024], PS[4][0:1, 0:512], scale_div, sem=dve)
                e.vop("tensor_scalar_mul", r_var[0:1, 0:512], PL[1][0:1, :], scale_div, sem=dve)
                e.vop("tensor_scalar_mul", r_var[0:1, 512:1024], PS[5][0:1, 0:512], scale_div, sem=dve)
                e.wait("vector", dve, dve.n)
                e.vop("tensor_tensor", r_rstd, r_mean, r_mean, op=ALU.mult, sem=dve)
                e.wait("vector", dve, dve.n)
                e.vop("tensor_sub", r_var, r_var, r_rstd, sem=dve)
                e.wait("scalar", dve, dve.n)
                e.act(r_rstd, r_var, AF.Ln, bias=EPS, sem=acts)
                e.wait("scalar", acts, acts.n)
                e.act(r_rstd, r_rstd, AF.Exp, scale=-0.5, sem=acts)
                e.wait("vector", acts, acts.n)
                e.vop("tensor_tensor", r_mrstd, r_mean, r_rstd, op=ALU.mult, sem=dve)
                e.wait("tensor", dve, dve.n)
                for hh in range(2):
                    if hh >= 1:
                        e.wait("tensor", dve, dve.n)
                    e.mm(PS[4][:, 0:512], onerow_f_sb[:], r_rstd[0:1, 512 * hh : 512 * (hh + 1)],
                         start=True, stop=True, sem=pe)
                    e.mm(PS[5][:, 0:512], onerow_f_sb[:], r_mrstd[0:1, 512 * hh : 512 * (hh + 1)],
                         start=True, stop=True, sem=pe)
                    e.wait("vector", pe, pe.n)
                    e.vop("tensor_copy", bc_sb[:, 1024 * hh : 1024 * hh + 512],
                          PS[4][:, 0:512], sem=dve)
                    e.vop("tensor_copy", bc_sb[:, 1024 * hh + 512 : 1024 * (hh + 1)],
                          PS[5][:, 0:512], sem=dve)

            def stats_chunk(dc, ac):
                e.wait("vector", acts, acts.n)
                e.wait("vector", dve, dve.n)
                e.wait("vector", pe, pe.n)
                for hh in range(2):
                    xb = xbb[:, 512 * hh : 512 * (hh + 1)]
                    sq = sqb[:, 512 * hh : 512 * (hh + 1)]
                    e.vop("tensor_copy", xb, ac[:, 512 * hh : 512 * (hh + 1)], sem=dve)
                    e.wait("vector", dve, dve.n)
                    e.vop("tensor_mul", sq, xb, xb, sem=dve)
                    e.wait("tensor", dve, dve.n)
                    ts_ = PL[0] if hh == 0 else PS[4]
                    tq_ = PL[1] if hh == 0 else PS[5]
                    e.mm(ts_[0:1, 0:512], onecol_f_sb[:], xb, start=(dc == 0), stop=(dc == 15))
                    e.mm(tq_[0:1, 0:512], onecol_f_sb[:], sq, start=(dc == 0), stop=(dc == 15),
                         sem=pe)

            p3_ev = []
            p3_st = []
            for dc in range(16):
                sl = dc % 2
                ps0, ps1 = PS[2 * sl], PS[2 * sl + 1]
                if dc >= 2:
                    e.wait("tensor", acts, p3_ev[dc - 2])
                for h in range(H):
                    lhsT = W_sb[:, 2048 * h + 128 * dc : 2048 * h + 128 * (dc + 1)]
                    e.mm(ps0[:], lhsT, ctx_fm[:, 1024 * h : 1024 * h + 512],
                         start=(h == 0), stop=(h == 15))
                    e.mm(ps1[:], lhsT, ctx_fm[:, 1024 * h + 512 : 1024 * (h + 1)],
                         start=(h == 0), stop=(h == 15), sem=pe if h == 15 else None)
                e.wait("scalar", pe, pe.n)
                ao = aosb_a[:, 0:1024]
                if dc >= 1:
                    e.wait("scalar", st_ao, 16 * dc)
                    e.wait("scalar", dve, dve.n)
                e.act(ao[:, 0:512], ps0[:], AF.Identity, bias=bo_sb[:, dc : dc + 1])
                e.act(ao[:, 512:1024], ps1[:], AF.Identity, bias=bo_sb[:, dc : dc + 1], sem=acts)
                p3_ev.append(acts.n)
                stats_chunk(dc, ao)
                e.wait("sync", acts, p3_ev[-1])
                e.dma("sync", ao_dram[128 * dc : 128 * (dc + 1), :], ao, st_ao)
            rows_pass(1.0 / D)
            p3b_dve = []
            e.wait("sync", st_ao, st_ao.n)
            e.wait("sync", dve, dve.n)
            for dc in range(16):
                ao = aosb_a[:, 0:1024]
                if dc >= 1:
                    e.wait("sync", dve, p3b_dve[dc - 1])
                    e.wait("sync", st_x2, st_x2.n)
                    if debug >= 3:
                        e.wait("sync", sm["dbgs"], sm["dbgs"].n)
                e.dma("sync", ao, ao_dram[128 * dc : 128 * (dc + 1), :], ld)
                for hh in range(2):
                    xo = xst_b[:, 512 * hh : 512 * (hh + 1)]
                    e.dma("sync", xo, xoT[128 * dc : 128 * (dc + 1), 512 * hh : 512 * (hh + 1)], ld)
                e.wait("vector", ld, ld.n)
                e.wait("vector", dve, dve.n)
                for hh in range(2):
                    xo = xst_b[:, 512 * hh : 512 * (hh + 1)]
                    t1 = t1b[:, 512 * hh : 512 * (hh + 1)]
                    e.vop("tensor_tensor", t1, ao[:, 512 * hh : 512 * (hh + 1)],
                          bc_sb[:, 1024 * hh : 1024 * hh + 512], op=ALU.mult, sem=dve)
                    e.wait("vector", dve, dve.n)
                    e.vop("tensor_sub", t1, t1, bc_sb[:, 1024 * hh + 512 : 1024 * (hh + 1)], sem=dve)
                    e.wait("vector", dve, dve.n)
                    e.vop("tensor_add", ao[:, 512 * hh : 512 * (hh + 1)],
                          xo, t1, sem=dve)
                e.wait("sync", dve, dve.n)
                e.dma("sync", x2_dram[128 * dc : 128 * (dc + 1), :], ao, st_x2)
                if debug >= 3:
                    e.dma("sync", dbg["x2"][128 * dc : 128 * (dc + 1), :], ao, sm["dbgs"])
                stats_chunk(dc, ao)
                p3b_dve.append(dve.n)
            rows_pass(1.0 / D)
            p3c = []
            e.wait("sync", st_x2, st_x2.n)
            e.wait("sync", dve, dve.n)
            if debug >= 3:
                e.wait("sync", sm["dbgs"], sm["dbgs"].n)
            for dc in range(16):
                x2c = aosb_a[:, 0:1024]
                if dc >= 1:
                    e.wait("sync", pool, p3c[dc - 1])
                e.dma("sync", x2c, x2_dram[128 * dc : 128 * (dc + 1), :], ld)
                e.wait("vector", ld, ld.n)
                e.wait("vector", dve, dve.n)
                for hh in range(2):
                    t1 = t1b[:, 512 * hh : 512 * (hh + 1)]
                    e.vop("tensor_tensor", t1, x2c[:, 512 * hh : 512 * (hh + 1)],
                          bc_sb[:, 1024 * hh : 1024 * hh + 512], op=ALU.mult, sem=dve)
                    e.wait("gpsimd", dve, dve.n)
                    e.pop("tensor_tensor",
                          ln2T[:, 1024 * dc + 512 * hh : 1024 * dc + 512 * (hh + 1)],
                          t1, bc_sb[:, 1024 * hh + 512 : 1024 * (hh + 1)],
                          op=ALU.subtract, sem=pool)
                p3c.append(pool.n)

            if debug >= 3 or LN2B:
                e.wait("sync", pool, pool.n)
                for i in range(4):
                    e.dma("sync", dbg["ln2"][:, 4096 * i : 4096 * (i + 1)],
                          ln2T[:, 4096 * i : 4096 * (i + 1)], sm["dbgs"])
                e.wait("sync", sm["dbgs"], sm["dbgs"].n)
            if LN2B >= 2:
                for i in range(12):
                    e.dma("sync", dbg["sb1"][:, 4096 * i : 4096 * (i + 1)],
                          sbraw[:][:, 4096 * i : 4096 * (i + 1)], sm["dbgs"])
                e.wait("sync", sm["dbgs"], sm["dbgs"].n)

            if upto <= 3:
                return
            e.wait("gpsimd", pool, pool.n)
            e.wait("gpsimd", pe, pe.n)
            e.wait("vector", sv[0], sv[0].n)
            e.wait("vector", sv[1], sv[1].n)
            e.wait("vector", pe, pe.n)
            if debug >= 1:
                e.wait("vector", sm["dbgs"], sm["dbgs"].n)
            p4_pe = []
            for g in range(p4_groups):
                gsl = g % 2
                wg = WGU_sb[:, 8192 * (2 * gsl) : 8192 * (2 * gsl + 1)]
                wu = WGU_sb[:, 8192 * (2 * gsl + 1) : 8192 * (2 * gsl + 2)]
                if g >= 2:
                    e.wait("gpsimd", pe, p4_pe[g - 2])
                src_g = WgT[:, 512 * g : 512 * (g + 1)].rearrange("(c p) n -> p c n", p=128)
                src_u = WuT[:, 512 * g : 512 * (g + 1)].rearrange("(c p) n -> p c n", p=128)
                wg_v = wg[:].rearrange("p (c n) -> p c n", c=16)
                wu_v = wu[:].rearrange("p (c n) -> p c n", c=16)
                for q4 in range(4):
                    e.dma("gpsimd", wg_v[:, 4 * q4 : 4 * (q4 + 1), :], src_g[:, 4 * q4 : 4 * (q4 + 1), :], gus[gsl])
                    e.dma("gpsimd", wu_v[:, 4 * q4 : 4 * (q4 + 1), :], src_u[:, 4 * q4 : 4 * (q4 + 1), :], gus[gsl])
                e.wait("tensor", gus[gsl], gus[gsl].n)
                for sub in range(4):
                    for half in range(2):
                        i = 8 * g + 2 * sub + half
                        if i >= 1:
                            e.wait("tensor", dve, dve.n)
                        for d in range(16):
                            rhs = ln2T[:, 1024 * d + 512 * half : 1024 * d + 512 * (half + 1)]
                            e.mm(PS[4][:], wg[:, 512 * d + 128 * sub : 512 * d + 128 * (sub + 1)],
                                 rhs, start=(d == 0), stop=(d == 15), sem=pe if d == 15 else None)
                        for d in range(16):
                            rhs = ln2T[:, 1024 * d + 512 * half : 1024 * d + 512 * (half + 1)]
                            e.mm(PS[5][:], wu[:, 512 * d + 128 * sub : 512 * d + 128 * (sub + 1)],
                                 rhs, start=(d == 0), stop=(d == 15), sem=pe if d == 15 else None)
                        e.wait("scalar", pe, pe.n)
                        sgs = sg_sb[:, 512 * (i % 2) : 512 * (i % 2 + 1)]
                        e.act(sgs, PS[4][:], AF.Silu, sem=acts)
                        e.wait("vector", acts, acts.n)
                        hc_g = 4 * g + sub
                        if half == 0:
                            dst = gated_lo if hc_g < 32 else gated_hi
                            dcol = 512 * (hc_g % 32)
                            e.vop("tensor_tensor", dst[:, dcol : dcol + 512], sgs, PS[5][:],
                                  op=ALU.mult, sem=dve)
                        else:
                            stg = vst[:, 1024 * (hc_g % 2) : 1024 * (hc_g % 2) + 512]
                            e.wait("vector", st_gb, st_gb.n)
                            e.vop("tensor_tensor", stg, sgs, PS[5][:], op=ALU.mult, sem=dve)
                            e.wait("sync", dve, dve.n)
                            e.dma("sync", gB_dram[128 * hc_g : 128 * (hc_g + 1), :], stg, st_gb)
                p4_pe.append(pe.n)

            if upto <= 4:
                return
            e.wait("gpsimd", pe, pe.n)
            e.wait("sync", pe, pe.n)
            e.wait("sync", st_gb, st_gb.n)
            if LN2B:
                for i in range(4):
                    e.dma("sync", dbg["ln2b"][:, 4096 * i : 4096 * (i + 1)],
                          ln2T[:, 4096 * i : 4096 * (i + 1)], sm["dbgs"])
                e.wait("sync", sm["dbgs"], sm["dbgs"].n)
            if LN2B >= 2:
                for i in range(12):
                    e.dma("sync", dbg["sb2"][:, 4096 * i : 4096 * (i + 1)],
                          sbraw[:][:, 4096 * i : 4096 * (i + 1)], sm["dbgs"])
                e.wait("sync", sm["dbgs"], sm["dbgs"].n)
            p5_pe = []
            for sc in range(8):
                wsl = sc % 2
                wd = WD_sb[:, 16384 * wsl : 16384 * (wsl + 1)]
                if sc >= 2:
                    e.wait("gpsimd", pe, p5_pe[sc - 2])
                for hc in range(8):
                    e.dma("gpsimd", wd[:, 2048 * hc : 2048 * (hc + 1)],
                          WdT[1024 * sc + 128 * hc : 1024 * sc + 128 * (hc + 1), :], wds[wsl])
                e.wait("tensor", wds[wsl], wds[wsl].n)
                gsl = sc % 2
                gbuf = gbuf2[:, 4096 * gsl : 4096 * (gsl + 1)]
                if sc >= 2:
                    e.wait("sync", pe, p5_pe[sc - 2])
                srcg = gB_dram[1024 * sc : 1024 * (sc + 1), :].rearrange("(c p) n -> p c n", p=128)
                e.dma("sync", gbuf[:].rearrange("p (c n) -> p c n", c=8), srcg, gbs[gsl])
                for half in range(2):
                    if half == 1:
                        e.wait("tensor", gbs[gsl], gbs[gsl].n)
                    for dq in range(4):
                        e.wait("tensor", dve, dve.n)
                        for dc4 in range(4):
                            dci = 4 * dq + dc4
                            ps = PS[dc4]
                            for hc in range(8):
                                hc_g = 8 * sc + hc
                                if half == 0:
                                    gsrc = gated_lo if hc_g < 32 else gated_hi
                                    rhs = gsrc[:, 512 * (hc_g % 32) : 512 * (hc_g % 32 + 1)]
                                else:
                                    rhs = gbuf[:, 512 * hc : 512 * (hc + 1)]
                                e.mm(ps[:], wd[:, 2048 * hc + 128 * dci : 2048 * hc + 128 * (dci + 1)],
                                     rhs, start=(hc == 0), stop=(hc == 7),
                                     sem=pe if hc == 7 else None)
                        e.wait("vector", pe, pe.n)
                        for dc4 in range(4):
                            dci = 4 * dq + dc4
                            stg = st32[:, 512 * (dc4 % 2) : 512 * (dc4 % 2 + 1)]
                            e.wait("vector", st_acc, st_acc.n)
                            e.vop("tensor_copy", stg, PS[dc4][:], sem=dve)
                            e.wait("gpsimd", dve, dve.n)
                            dst = acc_dram[128 * dci : 128 * (dci + 1), 512 * half : 512 * (half + 1)]
                            e.dma("gpsimd", dst, stg, st_acc, acc=(sc > 0))
                p5_pe.append(pe.n)

            if p5_mode <= 1:
                return
            e.wait("sync", st_acc, st_acc.n)
            e.wait("sync", dve, dve.n)
            for dc in range(16):
                ac = aosb_a[:, 0:1024]
                if dc >= 1:
                    e.wait("sync", dve, dve.n)
                e.dma("sync", ac, acc_dram[128 * dc : 128 * (dc + 1), :], ld)
                e.wait("vector", ld, ld.n)
                stats_chunk(dc, ac)
            rows_pass(1.0 / D)
            if p5_mode <= 2:
                return
            p5b = []
            e.wait("sync", st_x2, st_x2.n)
            e.wait("sync", dve, dve.n)
            for dc in range(16):
                ac = aosb_a[:, 0:1024]
                x2c = aosb_b[:, 0:1024]
                if dc >= 1:
                    e.wait("sync", st_y, 16 * dc)
                    e.wait("sync", dve, dve.n)
                e.dma("sync", ac, acc_dram[128 * dc : 128 * (dc + 1), :], ld)
                e.dma("sync", x2c, x2_dram[128 * dc : 128 * (dc + 1), :], ld)
                e.wait("vector", ld, ld.n)
                e.wait("vector", dve, dve.n)
                for hh in range(2):
                    t1 = t1b[:, 512 * hh : 512 * (hh + 1)]
                    e.vop("tensor_tensor", t1, ac[:, 512 * hh : 512 * (hh + 1)],
                          bc_sb[:, 1024 * hh : 1024 * hh + 512], op=ALU.mult, sem=dve)
                    e.wait("vector", dve, dve.n)
                    e.vop("tensor_sub", t1, t1, bc_sb[:, 1024 * hh + 512 : 1024 * (hh + 1)], sem=dve)
                    e.wait("vector", dve, dve.n)
                    e.vop("tensor_add", ac[:, 512 * hh : 512 * (hh + 1)],
                          x2c[:, 512 * hh : 512 * (hh + 1)], t1, sem=dve)
                e.wait("sync", dve, dve.n)
                e.dma("sync", y[128 * dc : 128 * (dc + 1), :], ac, st_y)
                p5b.append(st_y.n)


        @blk.sync
        def _(eng):
            program(E("sync", eng))

        @blk.gpsimd
        def _(eng):
            program(E("gpsimd", eng))

        @blk.tensor
        def _(eng):
            program(E("tensor", eng))

        @blk.scalar
        def _(eng):
            program(E("scalar", eng))

        @blk.vector
        def _(eng):
            program(E("vector", eng))

    return nc


def make_inputs(c, x, Wq, bq, Wk, bk, Wv, bv, Wo, bo, w_gate, w_up, w_down):
    xf = np.ascontiguousarray(x.reshape(S, D))
    rows = np.concatenate([np.arange(1024 * t + 128 * c, 1024 * t + 128 * (c + 1))
                           for t in range(NT)])
    btab_arr, _ = build_bias_table(c)
    return {
        "xT": np.ascontiguousarray(xf.T),
        "xoT": np.ascontiguousarray(xf[rows].T),
        "Wq": np.ascontiguousarray(Wq), "Wk": np.ascontiguousarray(Wk),
        "Wv": np.ascontiguousarray(Wv), "Wo": np.ascontiguousarray(Wo),
        "bq": np.ascontiguousarray(bq.reshape(H, 128).T),
        "bk": np.ascontiguousarray(bk.reshape(H, 128).T),
        "bo": np.ascontiguousarray(bo.reshape(H, 128).T),
        "bvr": bv.reshape(1, D).astype(ml_dtypes.bfloat16),
        "WgT": np.ascontiguousarray(w_gate.T),
        "WuT": np.ascontiguousarray(w_up.T),
        "WdT": np.ascontiguousarray(w_down.T),
        "btab": btab_arr,
        "masks": build_masks(c),
        "identb": np.eye(128, dtype=ml_dtypes.bfloat16),
        "onecol": np.ones((128, 1), dtype=ml_dtypes.bfloat16),
        "onerow_b": np.ones((1, 128), dtype=ml_dtypes.bfloat16),
        "onecol_f": np.ones((128, 1), dtype=ml_dtypes.bfloat16),
        "onerow_f": np.ones((1, 128), dtype=np.float32),
        "zerob": np.zeros((128, 128), dtype=ml_dtypes.bfloat16),
        "fconst": np.array([[EPS, 0.0, 1.0, 0, 0, 0, 0, 0]], dtype=np.float32),
        "fcb": np.tile(np.array([[0.0, 1.0]], dtype=np.float32), (128, 1)),
        "fcu": np.full((128, 4), 127, dtype=np.uint8),
    }


def run(inputs, debug=0, upto=99, p4_groups=16, p5_mode=3):
    args = [np.asarray(inputs[k], dtype=np.float32) for k in
            ("L0_attn_x", "L0_attn_Wq", "L0_attn_bq", "L0_attn_Wk", "L0_attn_bk",
             "L0_attn_Wv", "L0_attn_bv", "L0_attn_Wo", "L0_attn_bo",
             "L0_ffn_w_gate", "L0_ffn_w_up", "L0_ffn_w_down")]
    nc = build_nc(debug=debug, upto=upto, p4_groups=p4_groups, p5_mode=p5_mode)
    in_maps = [make_inputs(c, args[0], args[1], args[2], args[3], args[4], args[5],
                           args[6], args[7], args[8], args[9], args[10], args[11])
               for c in range(NC)]
    res = run_bass_kernel_spmd(nc, in_maps, core_ids=list(range(NC)))
    out = np.zeros((S, D), dtype=np.float32)
    for c in range(NC):
        yc = np.asarray(res.results[c]["y"]).T
        for t in range(NT):
            out[1024 * t + 128 * c : 1024 * t + 128 * (c + 1)] = yc[128 * t : 128 * (t + 1)]
    return out.reshape(1, S, D), res


def kernel(**inputs):
    out, _ = run(inputs, debug=0)
    return out


# version 09 (final)
# speedup vs baseline: 1.1051x; 1.0336x over previous
import numpy as np
import ml_dtypes
import concourse.bass as bass
import concourse.mybir as mybir
from concourse.bass_utils import run_bass_kernel_spmd
from contextlib import ExitStack

F32 = mybir.dt.float32
BF16 = mybir.dt.bfloat16
AF = mybir.ActivationFunctionType
ALU = mybir.AluOpType

S = 8192
D = 2048
H = 16
HD = 128
NC = 8
ROWS = S // NC
NT = 8
DH = 8192
EPS = 1e-5
SCALE = 1.0 / np.sqrt(HD)
T_BAND = 150.0
NOQFDUMP = 0
LN2B = 0
SBMAP = {}
NEG = -3.0e38
MASKNEG = -1.0e30

SLOPES = [2.0 ** (-0.5 * (h + 1)) for h in range(H)]
GSIZE = [1, 1, 1, 1, 1, 1, 1, 1, 1, 2, 2, 4, 4, 4, 4, 4]
HALVED = [True, True] + [False] * 14


def keep_subtile(h, t_near, b, s):
    dist = 1024 * (t_near - b) - 128 * s - 127
    return SLOPES[h] * max(0, dist) <= T_BAND


def attention_schedule():
    out = []
    for h in range(H):
        G = GSIZE[h]
        for tg in range(2):
            items = []
            for qg in range(4 // G):
                t_lo = tg * 4 + qg * G
                t_hi = t_lo + G - 1
                for b in range(t_lo):
                    for s in range(NC):
                        if not keep_subtile(h, t_lo, b, s):
                            continue
                        if HALVED[h]:
                            for hf in range(2):
                                items.append((b, s, ("n", h, t_hi - b, s, hf),
                                              128 * t_lo + 64 * hf, 64,
                                              128 * (t_lo - 4 * tg) + 64 * hf, None))
                        else:
                            items.append((b, s, ("n", h, t_hi - b, s, 0),
                                          128 * t_lo, 128 * G, 128 * (t_lo - 4 * tg), None))
                for t in range(t_lo, t_hi + 1):
                    for b in range(t_lo, t + 1):
                        diag = (b == t)
                        for s in range(NC):
                            if not keep_subtile(h, t, b, s):
                                continue
                            halves = (0, 1) if HALVED[h] else (0,)
                            for hf in halves:
                                n = 64 if HALVED[h] else 128
                                key = (("d", h, t_hi - t, s, hf) if diag
                                       else ("n", h, t_hi - b, s, hf))
                                mask = (128 * s + 64 * hf) if diag else None
                                items.append((b, s, key,
                                              128 * t + 64 * hf, n,
                                              128 * (t - 4 * tg) + 64 * hf, mask))
            out.append((h, tg, items))
    return out


SCHED = attention_schedule()


def build_bias_table(c):
    keys = []
    seen = set()
    for (_h, _tg, items) in SCHED:
        for (_b, _s, key, _qc, _n, _cc, _m) in items:
            if key not in seen:
                seen.add(key)
                keys.append(key)
    p = np.arange(128, dtype=np.float64)
    cols = {}
    data = []
    for key in keys:
        kind, h, dlt, s, hf = key
        sl = SLOPES[h]
        if kind == "d" and s > c:
            v = np.full(128, NEG)
        else:
            v = sl * (-1024.0 * dlt + 128.0 * (s - c) - 64.0 * hf + p)
        cols[key] = len(data)
        data.append(v)
    a = np.stack(data, axis=1).astype(np.float32)
    ncol = int(np.ceil(a.shape[1] / 16.0) * 16)
    arr = np.zeros((128, max(ncol, 16)), dtype=np.float32)
    arr[:, : a.shape[1]] = a
    return arr, cols


BT_ARR0, COLS = build_bias_table(0)
NCOL = BT_ARR0.shape[1]


def build_masks(c):
    m = np.zeros((128, NC * 128), dtype=np.float32)
    tri = np.where(np.arange(128)[:, None] <= np.arange(128)[None, :], 0.0, MASKNEG)
    m[:, 128 * c : 128 * (c + 1)] = tri
    return m.astype(ml_dtypes.bfloat16)


class Sem:
    __slots__ = ("h", "n")

    def __init__(self, h):
        self.h = h
        self.n = 0


class E:

    def __init__(self, which, eng):
        self.w = which
        self.e = eng

    def _fin(self, inst, sem, k):
        if sem is not None:
            if inst is not None:
                inst.then_inc(sem.h, k)
            sem.n += k
        return inst

    def dma(self, q, out, in_, sem=None, k=16, acc=False):
        i = None
        if self.w == q:
            if acc:
                i = self.e.dma_start(out=out, in_=in_, accum_op=ALU.add)
            else:
                i = self.e.dma_start(out=out, in_=in_)
        return self._fin(i, sem, k)

    def wait(self, q, sem, val=None):
        v = sem.n if val is None else val
        if self.w == q and v > 0:
            self.e.wait_ge(sem.h, v)

    def mm(self, out, lhsT, rhs, start, stop, sem=None, k=1):
        i = None
        if self.w == "tensor":
            i = self.e.matmul(out, lhsT, rhs, start=start, stop=stop)
        return self._fin(i, sem, k)

    def act(self, out, in_, func, bias=0.0, scale=1.0, sem=None, k=1):
        i = None
        if self.w == "scalar":
            i = self.e.activation(out, in_, func, bias=bias, scale=scale)
        return self._fin(i, sem, k)

    def act_table(self, func):
        if self.w == "scalar":
            self.e.preload_act_table(func)

    def vop(self, fn, *args, sem=None, k=1, **kw):
        i = None
        if self.w == "vector":
            i = getattr(self.e, fn)(*args, **kw)
        return self._fin(i, sem, k)

    def pop(self, fn, *args, sem=None, k=1, **kw):
        i = None
        if self.w == "gpsimd":
            i = getattr(self.e, fn)(*args, **kw)
        return self._fin(i, sem, k)


def build_nc(debug=0, upto=99, p4_groups=16, p5_mode=3):
    nc = bass.Bass("TRN2", target_bir_lowering=False, dynamic_dma_scratch_size=4096)

    ap = lambda t: t.ap()
    xT = ap(nc.dram_tensor("xT", [D, S], F32, kind="ExternalInput"))
    xoT = ap(nc.dram_tensor("xoT", [D, ROWS], F32, kind="ExternalInput"))
    Wq = ap(nc.dram_tensor("Wq", [D, D], F32, kind="ExternalInput"))
    Wk = ap(nc.dram_tensor("Wk", [D, D], F32, kind="ExternalInput"))
    Wv = ap(nc.dram_tensor("Wv", [D, D], F32, kind="ExternalInput"))
    Wo = ap(nc.dram_tensor("Wo", [D, D], F32, kind="ExternalInput"))
    bq = ap(nc.dram_tensor("bq", [128, H], F32, kind="ExternalInput"))
    bk = ap(nc.dram_tensor("bk", [128, H], F32, kind="ExternalInput"))
    bo = ap(nc.dram_tensor("bo", [128, H], F32, kind="ExternalInput"))
    bvr = ap(nc.dram_tensor("bvr", [1, D], BF16, kind="ExternalInput"))
    WgT = ap(nc.dram_tensor("WgT", [D, DH], F32, kind="ExternalInput"))
    WuT = ap(nc.dram_tensor("WuT", [D, DH], F32, kind="ExternalInput"))
    WdT = ap(nc.dram_tensor("WdT", [DH, D], F32, kind="ExternalInput"))
    btab = ap(nc.dram_tensor("btab", [128, NCOL], F32, kind="ExternalInput"))
    masks = ap(nc.dram_tensor("masks", [128, NC * 128], BF16, kind="ExternalInput"))
    identb = ap(nc.dram_tensor("identb", [128, 128], BF16, kind="ExternalInput"))
    onecol = ap(nc.dram_tensor("onecol", [128, 1], BF16, kind="ExternalInput"))
    onerow_b = ap(nc.dram_tensor("onerow_b", [1, 128], BF16, kind="ExternalInput"))
    onecol_f = ap(nc.dram_tensor("onecol_f", [128, 1], BF16, kind="ExternalInput"))
    onerow_f = ap(nc.dram_tensor("onerow_f", [1, 128], F32, kind="ExternalInput"))
    zerob = ap(nc.dram_tensor("zerob", [128, 128], BF16, kind="ExternalInput"))
    fconst = ap(nc.dram_tensor("fconst", [1, 8], F32, kind="ExternalInput"))
    fcb = ap(nc.dram_tensor("fcb", [128, 2], F32, kind="ExternalInput"))
    fcu = ap(nc.dram_tensor("fcu", [128, 4], mybir.dt.uint8, kind="ExternalInput"))

    y = ap(nc.dram_tensor("y", [D, ROWS], F32, kind="ExternalOutput"))

    ikind = "ExternalOutput" if debug >= 1 else "Internal"
    ln_dram = ap(nc.dram_tensor("ln_dram", [D, S], BF16, kind=ikind))
    K_g = ap(nc.dram_tensor("K_g", [D, S], BF16, kind=ikind))
    V_g = ap(nc.dram_tensor("V_g", [S, D], BF16, kind=ikind))
    ao_dram = ap(nc.dram_tensor("ao_dram", [D, ROWS], F32, kind=ikind))
    x2_dram = ap(nc.dram_tensor("x2_dram", [D, ROWS], F32, kind=ikind))
    gB_dram = ap(nc.dram_tensor("gB_dram", [DH, 512], BF16, kind=ikind))
    acc_dram = ap(nc.dram_tensor("acc_dram", [D, ROWS], F32, kind=ikind))

    pad_sink = ap(nc.dram_tensor("pad_sink", [1, 64], mybir.dt.uint8, kind="Internal"))

    dbg = {}
    if debug >= 1:
        dbg["Qf"] = ap(nc.dram_tensor("dbg_Qf", [128, H * 1024], BF16, kind="ExternalOutput"))
    if debug >= 2:
        dbg["ctx"] = ap(nc.dram_tensor("dbg_ctx", [128, H * 1024], BF16, kind="ExternalOutput"))
    if debug >= 3:
        dbg["x2"] = ap(nc.dram_tensor("dbg_x2", [D, ROWS], F32, kind="ExternalOutput"))
    if debug >= 3 or LN2B:
        dbg["ln2"] = ap(nc.dram_tensor("dbg_ln2", [128, 16 * 1024], BF16, kind="ExternalOutput"))
    if LN2B:
        dbg["ln2b"] = ap(nc.dram_tensor("dbg_ln2b", [128, 16 * 1024], BF16, kind="ExternalOutput"))
    if LN2B >= 2:
        U8 = mybir.dt.uint8
        dbg["sb1"] = ap(nc.dram_tensor("dbg_sb1", [128, 49152], U8, kind="ExternalOutput"))
        dbg["sb2"] = ap(nc.dram_tensor("dbg_sb2", [128, 49152], U8, kind="ExternalOutput"))
    if debug >= 5:
        dbg["rows"] = ap(nc.dram_tensor("dbg_rows", [1, 3072], F32, kind="ExternalOutput"))
        dbg["bc"] = ap(nc.dram_tensor("dbg_bc", [128, 2048], F32, kind="ExternalOutput"))
        dbg["xbb"] = ap(nc.dram_tensor("dbg_xbb", [128, 1024], BF16, kind="ExternalOutput"))
        dbg["sqb"] = ap(nc.dram_tensor("dbg_sqb", [128, 1024], BF16, kind="ExternalOutput"))
        dbg["pl"] = ap(nc.dram_tensor("dbg_pl", [1, 8192], F32, kind="ExternalOutput"))

    with ExitStack() as stk:
        EE = stk.enter_context

        from math import prod
        ring_pad = nc.alloc_sbuf_tensor_at("ring_pad", [128, 12288], mybir.dt.uint8, offset=4096)
        _cur = [16384]
        sbraw = nc.alloc_sbuf_tensor_at("sbraw", [128, 49152], mybir.dt.uint8, offset=4096) if LN2B >= 2 else None

        def region(tensors):
            base = _cur[0]
            out = []
            end = base
            for name, shape, dt in tensors:
                out.append(nc.alloc_sbuf_tensor_at(name, shape, dt, offset=base))
                SBMAP[name] = (base, base + prod(shape[1:]) * mybir.dt.size(dt))
                end = max(end, base + prod(shape[1:]) * mybir.dt.size(dt))
            _cur[0] = (end + 31) // 32 * 32
            assert _cur[0] <= nc.sbuf_top, f"SBUF overflow: {_cur[0]} > {nc.sbuf_top}"
            return out if len(out) > 1 else out[0]

        btab_sb = region([("btab_sb", [128, NCOL], F32)])
        masks_sb = region([("masks_sb", [128, NC * 128], BF16)])
        identb_sb = region([("identb_sb", [128, 128], BF16)])
        onecol_sb = region([("onecol_sb", [128, 1], BF16)])
        onerow_b_sb = region([("onerow_b_sb", [1, 128], BF16)])
        onecol_f_sb = region([("onecol_f_sb", [128, 1], BF16)])
        onerow_f_sb = region([("onerow_f_sb", [1, 128], F32)])
        zerob_sb = region([("zerob_sb", [128, 128], BF16)])
        bq_sb = region([("bq_sb", [128, H], F32)])
        bk_sb = region([("bk_sb", [128, H], F32)])
        bo_sb = region([("bo_sb", [128, H], F32)])
        fc_sb = region([("fc_sb", [1, 8], F32)])
        fcb_sb = region([("fcb_sb", [128, 2], F32)])
        fcu_sb = region([("fcu_sb", [128, 4], mybir.dt.uint8)])

        LNO, ln_sl2, ln2T, gbuf2 = region([
            ("LNO", [128, 16 * 1024], BF16),
            ("ln_sl2", [128, 2 * 8192], BF16),
            ("ln2T", [128, 16 * 1024], BF16),
            ("gbuf2", [128, 2 * 4096], BF16),
        ])
        Qf, gated_lo = region([
            ("Qf", [128, H * 1024], BF16),
            ("gated_lo", [128, 32 * 512], BF16),
        ])
        ctx_fm, gated_hi = region([
            ("ctx_fm", [128, H * 1024], BF16),
            ("gated_hi", [128, 32 * 512], BF16),
        ])
        W_sb, KV_sb, WGU_sb, WD_sb = region([
            ("W_sb", [128, 16 * 2048], BF16),
            ("KV_sb", [128, 2 * 16384], BF16),
            ("WGU_sb", [128, 4 * 8192], BF16),
            ("WD_sb", [128, 2 * 16384], BF16),
        ])
        xst_b = region([("xst_b", [128, 2 * 512], F32)])
        xnm_b = region([("xnm_b", [128, 2 * 512], F32)])
        sqb = region([("sqb", [128, 2 * 512], BF16)])
        xbb = region([("xbb", [128, 2 * 512], BF16)])
        kst, vst, ptil = region([
            ("kst", [128, 2 * 512], BF16),
            ("vst", [128, 2 * 1024], BF16),
            ("ptil", [128, 3 * 512], BF16),
        ])
        aosb_a = nc.alloc_sbuf_tensor_at("aosb_a", [128, 1024], F32, offset=SBMAP["xnm_b"][0])
        aosb_b = nc.alloc_sbuf_tensor_at("aosb_b", [128, 1024], F32, offset=SBMAP["kst"][0])
        bvr_sb = nc.alloc_sbuf_tensor_at("bvr_sb2", [1, D], BF16, offset=SBMAP["ctx_fm"][0])
        rows_sb = region([("rows_sb", [1, 3072], F32)])
        linv_sb = rows_sb
        bc_sb = region([("bc_sb", [128, 4 * 512], F32)])
        t1b, lbc_sb, sg_sb, st32 = region([
            ("t1b", [128, 2 * 512], F32),
            ("lbc_sb", [128, 2 * 512], F32),
            ("sg_sb", [128, 2 * 512], F32),
            ("st32", [128, 2 * 512], F32),
        ])

        PS = [EE(nc.psum_tensor(f"PS{i}", [128, 512], F32)) for i in range(6)]
        PL = [EE(nc.psum_tensor(f"PL{i}", [1, 512], F32)) for i in range(2)]

        nc.const_aps.register(F32, EPS, fc_sb[0:1, 0:1])
        nc.const_aps.register(F32, 0.0, fcb_sb[:, 0:1])
        nc.const_aps.register(F32, 1.0, fcb_sb[:, 1:2])
        nc.const_aps.register(BF16, 1.0, onecol_sb[:, 0:1])
        nc.const_aps.register(mybir.dt.uint8, 127, fcu_sb[:, 0:1])
        sem_names = ["ld", "ldc", "pe", "acts", "dve", "pool", "sto", "pe2", "acte",
                     "la0", "la1", "ln0", "ln1", "lb0", "lb1", "kv0", "kv1",
                     "gu0", "gu1", "wd0", "wd1", "gb0", "gb1",
                     "sk0", "sk1", "sv0", "sv1", "sl0", "sl1",
                     "st_ao", "st_x2", "st_acc", "st_gb", "st_kv", "st_y", "dbgs"]
        sem_h = {n: EE(nc.semaphore(n)) for n in sem_names}

        blk = EE(nc.Block())
        dbg_state = {}

        def ln_fm(e, sm, dram_col0, dram_src, n, out, war_sem=None, war_val=0):
            ld = sm["ld"]; pe = sm["pe"]; acts = sm["acts"]; dve = sm["dve"]; pool = sm["pool"]
            la = (sm["la0"], sm["la1"])
            lnS = (sm["ln0"], sm["ln1"])
            nh = n // 512
            sums = [(PL[0], PL[1]), (PS[4], PS[5])]
            cnt = 0
            war = []
            for g in range(16):
                for hh in range(nh):
                    sl = cnt % 2
                    xc = xst_b[:, 512 * sl : 512 * (sl + 1)]
                    xb = xbb[:, 512 * sl : 512 * (sl + 1)]
                    sq = sqb[:, 512 * sl : 512 * (sl + 1)]
                    if cnt >= 2:
                        e.wait("sync", pe, war[cnt - 2])
                    e.dma("sync", xc, dram_src[128 * g : 128 * (g + 1),
                                              dram_col0 + 512 * hh : dram_col0 + 512 * (hh + 1)], la[sl])
                    e.wait("scalar", la[sl], la[sl].n)
                    if cnt >= 2:
                        e.wait("scalar", pe, war[cnt - 2])
                    e.act(xb, xc, AF.Copy, sem=acts)
                    e.wait("vector", acts, acts.n)
                    e.vop("tensor_mul", sq, xb, xb, sem=dve)
                    e.wait("tensor", dve, dve.n)
                    ts_, tq_ = sums[hh]
                    e.mm(ts_[0:1, 0:512], onecol_f_sb[:], xb, start=(g == 0), stop=(g == 15))
                    e.mm(tq_[0:1, 0:512], onecol_f_sb[:], sq, start=(g == 0), stop=(g == 15),
                         sem=pe)
                    war.append(pe.n)
                    cnt += 1
            if debug >= 5 and not dbg_state.get("pl_dumped"):
                dbg_state["pl_dumped"] = True
                e.wait("vector", pe, pe.n)
                e.vop("tensor_copy", t1b[:, 0:512][0:1, :], PL[0][0:1, :], sem=dve)
                e.vop("tensor_copy", t1b[:, 512:1024][0:1, :], PL[1][0:1, :], sem=dve)
                e.wait("sync", dve, dve.n)
                e.dma("sync", dbg["pl"][0:1, 0:1024], t1b[0:1, 0:1024], sm["dbgs"])
            r_mean = rows_sb[0:1, 0:n]
            r_var = rows_sb[0:1, 1024 : 1024 + n]
            r_rstd = rows_sb[0:1, 2048 : 2048 + n]
            e.wait("vector", pe, pe.n)
            for hh in range(nh):
                ts_, tq_ = sums[hh]
                e.vop("tensor_scalar_mul", r_mean[0:1, 512 * hh : 512 * (hh + 1)],
                      ts_[0:1, 0:512], 1.0 / D, sem=dve)
                e.vop("tensor_scalar_mul", r_var[0:1, 512 * hh : 512 * (hh + 1)],
                      tq_[0:1, 0:512], 1.0 / D, sem=dve)
            dbgrows = debug >= 5 and not dbg_state.get("rows_dumped")
            def snap(i):
                if dbgrows:
                    e.wait("sync", dve, dve.n)
                    e.wait("sync", acts, acts.n)
                    e.dma("sync", dbg["pl"][0:1, 1024 * i : 1024 * i + 512],
                          rows_sb[0:1, 1024:1536], sm["dbgs"])
                    e.dma("sync", dbg["pl"][0:1, 1024 * i + 512 : 1024 * (i + 1)],
                          rows_sb[0:1, 2048:2560], sm["dbgs"])
                    e.wait("vector", sm["dbgs"], sm["dbgs"].n)
                    e.wait("scalar", sm["dbgs"], sm["dbgs"].n)
            e.wait("vector", dve, dve.n)
            snap(1)
            e.vop("tensor_tensor", r_rstd, r_mean, r_mean, op=ALU.mult, sem=dve)
            e.wait("vector", dve, dve.n)
            snap(2)
            e.vop("tensor_sub", r_var, r_var, r_rstd, sem=dve)
            e.wait("scalar", dve, dve.n)
            snap(3)
            e.act(r_rstd, r_var, AF.Ln, bias=EPS, sem=acts)
            e.wait("scalar", acts, acts.n)
            snap(4)
            e.act(r_rstd, r_rstd, AF.Exp, scale=-0.5, sem=acts)
            e.wait("vector", acts, acts.n)
            snap(5)
            if dbgrows:
                dbg_state["rows_dumped"] = True
            e.vop("tensor_tensor", r_var, r_mean, r_rstd, op=ALU.mult, sem=dve)
            e.wait("tensor", dve, dve.n)
            for hh in range(nh):
                if hh >= 1:
                    e.wait("tensor", dve, dve.n)
                e.mm(PS[4][:, 0:512], onerow_f_sb[:], r_rstd[0:1, 512 * hh : 512 * (hh + 1)],
                     start=True, stop=True, sem=pe)
                e.mm(PS[5][:, 0:512], onerow_f_sb[:], r_var[0:1, 512 * hh : 512 * (hh + 1)],
                     start=True, stop=True, sem=pe)
                e.wait("vector", pe, pe.n)
                e.vop("tensor_copy", bc_sb[:, 1024 * hh : 1024 * hh + 512], PS[4][:, 0:512], sem=dve)
                e.vop("tensor_copy", bc_sb[:, 1024 * hh + 512 : 1024 * (hh + 1)], PS[5][:, 0:512],
                      sem=dve)
            if war_sem is not None:
                e.wait("gpsimd", war_sem, war_val)
            cnt = 0
            dwar = []
            pwar = []
            for g in range(16):
                for hh in range(nh):
                    sl = cnt % 2
                    xc = xnm_b[:, 512 * sl : 512 * (sl + 1)]
                    t1 = t1b[:, 512 * sl : 512 * (sl + 1)]
                    if cnt >= 2:
                        e.wait("sync", dve, dwar[cnt - 2])
                    e.dma("sync", xc, dram_src[128 * g : 128 * (g + 1),
                                              dram_col0 + 512 * hh : dram_col0 + 512 * (hh + 1)], lnS[sl])
                    if cnt >= 2:
                        e.wait("vector", pool, pwar[cnt - 2])
                    e.wait("vector", lnS[sl], lnS[sl].n)
                    e.wait("vector", dve, dve.n)
                    e.vop("tensor_tensor", t1, xc,
                          bc_sb[:, 1024 * hh : 1024 * hh + 512], op=ALU.mult, sem=dve)
                    dwar.append(dve.n)
                    e.wait("gpsimd", dve, dve.n)
                    e.pop("tensor_tensor", out(g)[:, 512 * hh : 512 * (hh + 1)],
                          t1, bc_sb[:, 1024 * hh + 512 : 1024 * (hh + 1)],
                          op=ALU.subtract, sem=pool)
                    pwar.append(pool.n)
                    cnt += 1

        def program(e):
            dbg_state.clear()
            sm = {n: Sem(h) for n, h in sem_h.items()}
            phases(e, sm)
            for nm in ("st_y", "dbgs", "sto", "st_ao", "st_x2", "st_acc", "st_gb",
                       "sk0", "sk1", "sv0", "sv1", "sl0", "sl1"):
                e.wait("sync", sm[nm])
            e.wait("gpsimd", sm["st_y"])

        def phases(e, sm):
            ld = sm["ld"]; ldc = sm["ldc"]; pe = sm["pe"]; acts = sm["acts"]
            dve = sm["dve"]; pool = sm["pool"]; sto = sm["sto"]
            pe2 = sm["pe2"]; acte = sm["acte"]
            sk = (sm["sk0"], sm["sk1"]); sv = (sm["sv0"], sm["sv1"])
            slst = (sm["sl0"], sm["sl1"]); kvs = (sm["kv0"], sm["kv1"])
            gus = (sm["gu0"], sm["gu1"]); wds = (sm["wd0"], sm["wd1"])
            gbs = (sm["gb0"], sm["gb1"])
            st_ao = sm["st_ao"]; st_x2 = sm["st_x2"]; st_acc = sm["st_acc"]
            st_gb = sm["st_gb"]; st_kv = sm["st_kv"]; st_y = sm["st_y"]

            for dst, src in [(fc_sb, fconst), (fcb_sb, fcb), (fcu_sb, fcu), (btab_sb, btab), (masks_sb, masks), (identb_sb, identb),
                             (onecol_sb, onecol), (onerow_b_sb, onerow_b),
                             (onecol_f_sb, onecol_f), (onerow_f_sb, onerow_f),
                             (zerob_sb, zerob), (bq_sb, bq), (bk_sb, bk),
                             (bo_sb, bo), (bvr_sb, bvr)]:
                e.dma("sync", dst[:], src, ld)
            for q in ("sync", "gpsimd", "tensor", "scalar", "vector"):
                e.wait(q, ld)

            ln_fm(e, sm, 0, xoT, 1024,
                  out=lambda g: LNO[:, 1024 * g : 1024 * (g + 1)])
            lno_done = pool.n
            if debug >= 5:
                e.wait("sync", pool, pool.n)
                e.dma("sync", dbg["rows"], rows_sb[0:1, :], sm["dbgs"])
                e.dma("sync", dbg["bc"], bc_sb[:], sm["dbgs"])
                e.dma("sync", dbg["xbb"], xbb[:], sm["dbgs"])
                e.dma("sync", dbg["sqb"], sqb[:], sm["dbgs"])

            for d in range(16):
                e.dma("gpsimd", W_sb[:, 2048 * d : 2048 * (d + 1)],
                      Wq[128 * d : 128 * (d + 1), :], ldc)
            e.wait("tensor", ldc, ldc.n)
            e.wait("tensor", pool, lno_done)
            q_ev = []
            for h in range(H):
                for half in range(2):
                    i = 2 * h + half
                    ps = PS[i % 2]
                    if i >= 2:
                        e.wait("tensor", acts, q_ev[i - 2])
                    for d in range(16):
                        e.mm(ps[:], W_sb[:, 2048 * d + 128 * h : 2048 * d + 128 * (h + 1)],
                             LNO[:, 1024 * d + 512 * half : 1024 * d + 512 * (half + 1)],
                             start=(d == 0), stop=(d == 15), sem=pe if d == 15 else None)
                    e.wait("scalar", pe, pe.n)
                    e.act(Qf[:, 1024 * h + 512 * half : 1024 * h + 512 * (half + 1)], ps[:],
                          AF.Identity, bias=bq_sb[:, h : h + 1], sem=acts)
                    q_ev.append(acts.n)

            if debug >= 1 and not NOQFDUMP:
                e.wait("sync", acts, acts.n)
                for i in range(4):
                    e.dma("sync", dbg["Qf"][:, 4096 * i : 4096 * (i + 1)],
                          Qf[:, 4096 * i : 4096 * (i + 1)], sm["dbgs"])

            if upto <= 0:
                return
            e.wait("gpsimd", pe, pe.n)
            for d in range(16):
                e.dma("gpsimd", W_sb[:, 2048 * d : 2048 * (d + 1)],
                      Wk[128 * d : 128 * (d + 1), :], ldc)
            wk_ld = ldc.n

            ln_done = []
            ln_st = []
            k_ev = []
            kst_free = []
            kts_pe = []

            def k_pass(ts):
                lnb = ln_sl2[:, (ts % 2) * 8192 : (ts % 2 + 1) * 8192]
                if ts == 0:
                    e.wait("tensor", ldc, wk_ld)
                e.wait("tensor", pool, ln_done[ts])
                for h in range(H):
                    i = 16 * ts + h
                    sl = i % 2
                    ps = PS[2 + sl]
                    if i >= 2:
                        e.wait("tensor", acts, k_ev[i - 2])
                    for d in range(16):
                        e.mm(ps[:], W_sb[:, 2048 * d + 128 * h : 2048 * d + 128 * (h + 1)],
                             lnb[:, 512 * d : 512 * (d + 1)],
                             start=(d == 0), stop=(d == 15), sem=pe if d == 15 else None)
                    e.wait("scalar", pe, pe.n)
                    if i >= 2:
                        e.wait("scalar", sk[sl], 16 * (i // 2))
                    e.act(kst[:, 512 * sl : 512 * (sl + 1)], ps[:], AF.Identity,
                          bias=bk_sb[:, h : h + 1], sem=acts)
                    k_ev.append(acts.n)
                    e.wait("sync", acts, acts.n)
                    e.dma("sync", K_g[128 * h : 128 * (h + 1), 512 * ts : 512 * (ts + 1)],
                          kst[:, 512 * sl : 512 * (sl + 1)], sk[sl])
                kts_pe.append(pe.n)

            for ts in range(16):
                lnb = ln_sl2[:, (ts % 2) * 8192 : (ts % 2 + 1) * 8192]
                if ts >= 2:
                    e.wait("gpsimd", slst[ts % 2], 256 * (ts // 2))
                war_val = kts_pe[ts - 2] if ts >= 2 else 0
                ln_fm(e, sm, 512 * ts, xT, 512,
                      out=lambda g, lnb=lnb: lnb[:, 512 * g : 512 * (g + 1)],
                      war_sem=pe if ts >= 2 else None, war_val=war_val)
                ln_done.append(pool.n)
                e.wait("sync", pool, pool.n)
                for g in range(16):
                    e.dma("sync", ln_dram[128 * g : 128 * (g + 1), 512 * ts : 512 * (ts + 1)],
                          lnb[:, 512 * g : 512 * (g + 1)], slst[ts % 2])
                if ts >= 1:
                    k_pass(ts - 1)
            k_pass(15)

            e.wait("gpsimd", pe, pe.n)
            for d in range(16):
                e.dma("gpsimd", W_sb[:, 2048 * d : 2048 * (d + 1)],
                      Wv[128 * d : 128 * (d + 1), :], ldc)
            e.wait("tensor", ldc, ldc.n)
            e.wait("sync", sm["sl0"], sm["sl0"].n)
            e.wait("sync", sm["sl1"], sm["sl1"].n)
            e.wait("sync", pe, kts_pe[-1])
            e.wait("scalar", sk[0], sk[0].n)
            e.wait("scalar", sk[1], sk[1].n)
            v_ev = []
            vts_pe = []
            lb = (sm["lb0"], sm["lb1"])
            for ts in range(16):
                lnb = ln_sl2[:, (ts % 2) * 8192 : (ts % 2 + 1) * 8192]
                if ts >= 2:
                    e.wait("sync", pe, vts_pe[ts - 2])
                for g in range(16):
                    e.dma("sync", lnb[:, 512 * g : 512 * (g + 1)],
                          ln_dram[128 * g : 128 * (g + 1), 512 * ts : 512 * (ts + 1)], lb[ts % 2])
                e.wait("tensor", lb[ts % 2], lb[ts % 2].n)
                for tt in range(4):
                    for dh in range(2):
                        i = len(v_ev)
                        sl = i % 2
                        pa, pb = PS[2 * sl], PS[2 * sl + 1]
                        if i >= 2:
                            e.wait("tensor", acts, v_ev[i - 2])
                        for d in range(16):
                            lhsT = lnb[:, 512 * d + 128 * tt : 512 * d + 128 * (tt + 1)]
                            e.mm(pa[:], lhsT,
                                 W_sb[:, 2048 * d + 1024 * dh : 2048 * d + 1024 * dh + 512],
                                 start=(d == 0), stop=False)
                            e.mm(pb[:], lhsT,
                                 W_sb[:, 2048 * d + 1024 * dh + 512 : 2048 * d + 1024 * (dh + 1)],
                                 start=(d == 0), stop=False)
                        e.mm(pa[:], onerow_b_sb[:], bvr_sb[0:1, 1024 * dh : 1024 * dh + 512],
                             start=False, stop=True)
                        e.mm(pb[:], onerow_b_sb[:], bvr_sb[0:1, 1024 * dh + 512 : 1024 * (dh + 1)],
                             start=False, stop=True, sem=pe)
                        e.wait("scalar", pe, pe.n)
                        if i >= 2:
                            e.wait("scalar", sv[sl], 16 * (i // 2))
                        e.act(vst[:, 1024 * sl : 1024 * sl + 512], pa[:], AF.Identity)
                        e.act(vst[:, 1024 * sl + 512 : 1024 * (sl + 1)], pb[:], AF.Identity, sem=acts)
                        v_ev.append(acts.n)
                        e.wait("sync", acts, acts.n)
                        e.dma("sync", V_g[512 * ts + 128 * tt : 512 * ts + 128 * (tt + 1),
                                          1024 * dh : 1024 * (dh + 1)],
                              vst[:, 1024 * sl : 1024 * (sl + 1)], sv[sl])
                vts_pe.append(pe.n)

            if upto <= 1:
                return
            for s_ in (sm["sk0"], sm["sk1"], sm["sv0"], sm["sv1"]):
                e.wait("sync", s_, s_.n)
            e.wait("sync", pe, pe.n)
            e.wait("scalar", sv[0], sv[0].n)
            e.wait("scalar", sv[1], sv[1].n)
            e.wait("tensor", acts, acts.n)
            e.wait("tensor", dve, dve.n)
            kv_ld = []
            att_done = []
            ctx_evac = {0: 0, 1: 0}

            def load_kv(h):
                half = h % 2
                kb = KV_sb[:, 16384 * half : 16384 * half + 8192]
                vb = KV_sb[:, 16384 * half + 8192 : 16384 * (half + 1)]
                if h >= 2:
                    e.wait("sync", pe, att_done[h - 2])
                e.dma("sync", kb[:], K_g[128 * h : 128 * (h + 1), :], kvs[half])
                for b in range(NC):
                    src = V_g[1024 * b : 1024 * (b + 1), 128 * h : 128 * (h + 1)]
                    src = src.rearrange("(s k) d -> k s d", k=128)
                    dst = vb[:, 1024 * b : 1024 * (b + 1)].rearrange("k (s d) -> k s d", s=8)
                    e.dma("sync", dst, src, kvs[half])
                kv_ld.append(kvs[half].n)

            load_kv(0)

            q_state = {"q": []}

            def flush(idx):
                q = q_state["q"]
                ent = q[idx]
                if ent["done"]:
                    return
                e.wait("tensor", acte, ent["exp"])
                pt = ptil[:, 512 * ent["slot"] : 512 * ent["slot"] + ent["N"]]
                e.mm(ent["l_ps"][0:1, ent["ctxcol"] : ent["ctxcol"] + ent["N"]],
                     onecol_sb[:], pt, start=False, stop=False)
                e.mm(ent["ctx_ps"][:, ent["ctxcol"] : ent["ctxcol"] + ent["N"]],
                     ent["vb"][:, 1024 * ent["b"] + 128 * ent["s"] : 1024 * ent["b"] + 128 * (ent["s"] + 1)],
                     pt, start=False, stop=False, sem=pe)
                ent["done"] = True
                ent["ctxpe"] = pe.n

            for (h, tg, items) in SCHED:
                half = h % 2
                kb = KV_sb[:, 16384 * half : 16384 * half + 8192]
                vb = KV_sb[:, 16384 * half + 8192 : 16384 * (half + 1)]
                if tg == 0:
                    if h + 1 < H:
                        load_kv(h + 1)
                    e.wait("tensor", kvs[h % 2], kv_ld[h])
                pp = (2 * h + tg) % 2
                ctx_ps = PS[2 + pp]
                l_ps = PL[pp]
                if 2 * h + tg >= 2:
                    e.wait("tensor", dve, ctx_evac[pp])
                e.mm(ctx_ps[:], zerob_sb[:], Qf[:, 0:512], start=True, stop=False)
                e.mm(l_ps[0:1, :], zerob_sb[:, 0:1], Qf[:, 0:512], start=True, stop=False)
                for (b, s, colkey, qoff, N, ctxcol, mask) in items:
                    q = q_state["q"]
                    i = len(q)
                    slot = i % 3
                    sps = PS[(0, 1, 4)[slot]]
                    if i >= 3:
                        e.wait("tensor", acte, q[i - 3]["exp"])
                    e.mm(sps[:, 0:N], kb[:, 1024 * b + 128 * s : 1024 * b + 128 * (s + 1)],
                         Qf[:, 1024 * h + qoff : 1024 * h + qoff + N],
                         start=True, stop=(mask is None), sem=None if mask is not None else pe2)
                    if mask is not None:
                        e.mm(sps[:, 0:N], identb_sb[:], masks_sb[:, mask : mask + N],
                             start=False, stop=True, sem=pe2)
                    e.wait("scalar", pe2, pe2.n)
                    if i >= 3:
                        e.wait("scalar", pe, q[i - 3]["ctxpe"])
                    cidx = COLS[colkey]
                    e.act(ptil[:, 512 * slot : 512 * slot + N], sps[:, 0:N], AF.Exp,
                          bias=btab_sb[:, cidx : cidx + 1], scale=SCALE, sem=acte)
                    q.append(dict(exp=acte.n, slot=slot, N=N, vb=vb, b=b, s=s,
                                  ctx_ps=ctx_ps, l_ps=l_ps, ctxcol=ctxcol,
                                  ctxpe=0, done=False))
                    if i >= 2:
                        flush(i - 2)
                if len(q_state["q"]) >= 2:
                    flush(-2)
                if q_state["q"]:
                    flush(-1)
                e.mm(ctx_ps[:], zerob_sb[:], Qf[:, 0:512], start=False, stop=True)
                e.mm(l_ps[0:1, :], zerob_sb[:, 0:1], Qf[:, 0:512], start=False, stop=True, sem=pe)
                peval = pe.n
                e.wait("vector", pe, peval)
                lrow = linv_sb[0:1, 512 * pp : 512 * (pp + 1)]
                e.vop("reciprocal", lrow, l_ps[0:1, :], sem=dve)
                e.wait("tensor", dve, dve.n)
                e.mm(PS[5][:, 0:512], onerow_f_sb[:], lrow, start=True, stop=True, sem=pe)
                e.wait("vector", pe, pe.n)
                lb = lbc_sb[:, 512 * pp : 512 * (pp + 1)]
                e.vop("tensor_copy", lb, PS[5][:, 0:512], sem=dve)
                e.wait("vector", dve, dve.n)
                e.vop("tensor_tensor", ctx_fm[:, 1024 * h + 512 * tg : 1024 * h + 512 * (tg + 1)],
                      ctx_ps[:], lb, op=ALU.mult, sem=dve)
                ctx_evac[pp] = dve.n
                if tg == 1:
                    att_done.append(pe.n)

            if debug >= 2:
                e.wait("sync", dve, dve.n)
                for i in range(4):
                    e.dma("sync", dbg["ctx"][:, 4096 * i : 4096 * (i + 1)],
                          ctx_fm[:, 4096 * i : 4096 * (i + 1)], sm["dbgs"])

            if upto <= 2:
                return
            e.wait("gpsimd", pe, pe.n)
            for d in range(16):
                e.dma("gpsimd", W_sb[:, 2048 * d : 2048 * (d + 1)],
                      Wo[128 * d : 128 * (d + 1), :], ldc)
            e.wait("tensor", ldc, ldc.n)
            e.wait("tensor", dve, dve.n)

            def rows_pass(scale_div):
                r_mean = rows_sb[0:1, 0:1024]
                r_var = rows_sb[0:1, 1024:2048]
                r_rstd = rows_sb[0:1, 2048:3072]
                r_mrstd = rows_sb[0:1, 1024:2048]
                e.wait("vector", pe, pe.n)
                e.vop("tensor_scalar_mul", r_mean[0:1, 0:512], PL[0][0:1, :], scale_div, sem=dve)
                e.vop("tensor_scalar_mul", r_mean[0:1, 512:1024], PS[4][0:1, 0:512], scale_div, sem=dve)
                e.vop("tensor_scalar_mul", r_var[0:1, 0:512], PL[1][0:1, :], scale_div, sem=dve)
                e.vop("tensor_scalar_mul", r_var[0:1, 512:1024], PS[5][0:1, 0:512], scale_div, sem=dve)
                e.wait("vector", dve, dve.n)
                e.vop("tensor_tensor", r_rstd, r_mean, r_mean, op=ALU.mult, sem=dve)
                e.wait("vector", dve, dve.n)
                e.vop("tensor_sub", r_var, r_var, r_rstd, sem=dve)
                e.wait("scalar", dve, dve.n)
                e.act(r_rstd, r_var, AF.Ln, bias=EPS, sem=acts)
                e.wait("scalar", acts, acts.n)
                e.act(r_rstd, r_rstd, AF.Exp, scale=-0.5, sem=acts)
                e.wait("vector", acts, acts.n)
                e.vop("tensor_tensor", r_mrstd, r_mean, r_rstd, op=ALU.mult, sem=dve)
                e.wait("tensor", dve, dve.n)
                for hh in range(2):
                    if hh >= 1:
                        e.wait("tensor", dve, dve.n)
                    e.mm(PS[4][:, 0:512], onerow_f_sb[:], r_rstd[0:1, 512 * hh : 512 * (hh + 1)],
                         start=True, stop=True, sem=pe)
                    e.mm(PS[5][:, 0:512], onerow_f_sb[:], r_mrstd[0:1, 512 * hh : 512 * (hh + 1)],
                         start=True, stop=True, sem=pe)
                    e.wait("vector", pe, pe.n)
                    e.vop("tensor_copy", bc_sb[:, 1024 * hh : 1024 * hh + 512],
                          PS[4][:, 0:512], sem=dve)
                    e.vop("tensor_copy", bc_sb[:, 1024 * hh + 512 : 1024 * (hh + 1)],
                          PS[5][:, 0:512], sem=dve)

            def stats_chunk(dc, ac):
                e.wait("vector", acts, acts.n)
                e.wait("vector", dve, dve.n)
                e.wait("vector", pe, pe.n)
                for hh in range(2):
                    xb = xbb[:, 512 * hh : 512 * (hh + 1)]
                    sq = sqb[:, 512 * hh : 512 * (hh + 1)]
                    e.vop("tensor_copy", xb, ac[:, 512 * hh : 512 * (hh + 1)], sem=dve)
                    e.wait("vector", dve, dve.n)
                    e.vop("tensor_mul", sq, xb, xb, sem=dve)
                    e.wait("tensor", dve, dve.n)
                    ts_ = PL[0] if hh == 0 else PS[4]
                    tq_ = PL[1] if hh == 0 else PS[5]
                    e.mm(ts_[0:1, 0:512], onecol_f_sb[:], xb, start=(dc == 0), stop=(dc == 15))
                    e.mm(tq_[0:1, 0:512], onecol_f_sb[:], sq, start=(dc == 0), stop=(dc == 15),
                         sem=pe)

            p3_ev = []
            p3_st = []
            for dc in range(16):
                sl = dc % 2
                ps0, ps1 = PS[2 * sl], PS[2 * sl + 1]
                if dc >= 2:
                    e.wait("tensor", acts, p3_ev[dc - 2])
                for h in range(H):
                    lhsT = W_sb[:, 2048 * h + 128 * dc : 2048 * h + 128 * (dc + 1)]
                    e.mm(ps0[:], lhsT, ctx_fm[:, 1024 * h : 1024 * h + 512],
                         start=(h == 0), stop=(h == 15))
                    e.mm(ps1[:], lhsT, ctx_fm[:, 1024 * h + 512 : 1024 * (h + 1)],
                         start=(h == 0), stop=(h == 15), sem=pe if h == 15 else None)
                e.wait("scalar", pe, pe.n)
                ao = aosb_a[:, 0:1024]
                if dc >= 1:
                    e.wait("scalar", st_ao, 16 * dc)
                    e.wait("scalar", dve, dve.n)
                e.act(ao[:, 0:512], ps0[:], AF.Identity, bias=bo_sb[:, dc : dc + 1])
                e.act(ao[:, 512:1024], ps1[:], AF.Identity, bias=bo_sb[:, dc : dc + 1], sem=acts)
                p3_ev.append(acts.n)
                stats_chunk(dc, ao)
                e.wait("sync", acts, p3_ev[-1])
                e.dma("sync", ao_dram[128 * dc : 128 * (dc + 1), :], ao, st_ao)
            rows_pass(1.0 / D)
            p3b_dve = []
            e.wait("sync", st_ao, st_ao.n)
            e.wait("sync", dve, dve.n)
            for dc in range(16):
                ao = aosb_a[:, 0:1024]
                if dc >= 1:
                    e.wait("sync", dve, p3b_dve[dc - 1])
                    e.wait("sync", st_x2, st_x2.n)
                    if debug >= 3:
                        e.wait("sync", sm["dbgs"], sm["dbgs"].n)
                e.dma("sync", ao, ao_dram[128 * dc : 128 * (dc + 1), :], ld)
                for hh in range(2):
                    xo = xst_b[:, 512 * hh : 512 * (hh + 1)]
                    e.dma("sync", xo, xoT[128 * dc : 128 * (dc + 1), 512 * hh : 512 * (hh + 1)], ld)
                e.wait("vector", ld, ld.n)
                e.wait("vector", dve, dve.n)
                for hh in range(2):
                    xo = xst_b[:, 512 * hh : 512 * (hh + 1)]
                    t1 = t1b[:, 512 * hh : 512 * (hh + 1)]
                    e.vop("tensor_tensor", t1, ao[:, 512 * hh : 512 * (hh + 1)],
                          bc_sb[:, 1024 * hh : 1024 * hh + 512], op=ALU.mult, sem=dve)
                    e.wait("vector", dve, dve.n)
                    e.vop("tensor_sub", t1, t1, bc_sb[:, 1024 * hh + 512 : 1024 * (hh + 1)], sem=dve)
                    e.wait("vector", dve, dve.n)
                    e.vop("tensor_add", ao[:, 512 * hh : 512 * (hh + 1)],
                          xo, t1, sem=dve)
                e.wait("sync", dve, dve.n)
                e.dma("sync", x2_dram[128 * dc : 128 * (dc + 1), :], ao, st_x2)
                if debug >= 3:
                    e.dma("sync", dbg["x2"][128 * dc : 128 * (dc + 1), :], ao, sm["dbgs"])
                stats_chunk(dc, ao)
                p3b_dve.append(dve.n)
            rows_pass(1.0 / D)
            p3c = []
            e.wait("sync", st_x2, st_x2.n)
            e.wait("sync", dve, dve.n)
            if debug >= 3:
                e.wait("sync", sm["dbgs"], sm["dbgs"].n)
            for dc in range(16):
                x2c = aosb_a[:, 0:1024]
                if dc >= 1:
                    e.wait("sync", pool, p3c[dc - 1])
                e.dma("sync", x2c, x2_dram[128 * dc : 128 * (dc + 1), :], ld)
                e.wait("vector", ld, ld.n)
                e.wait("vector", dve, dve.n)
                for hh in range(2):
                    t1 = t1b[:, 512 * hh : 512 * (hh + 1)]
                    e.vop("tensor_tensor", t1, x2c[:, 512 * hh : 512 * (hh + 1)],
                          bc_sb[:, 1024 * hh : 1024 * hh + 512], op=ALU.mult, sem=dve)
                    e.wait("gpsimd", dve, dve.n)
                    e.pop("tensor_tensor",
                          ln2T[:, 1024 * dc + 512 * hh : 1024 * dc + 512 * (hh + 1)],
                          t1, bc_sb[:, 1024 * hh + 512 : 1024 * (hh + 1)],
                          op=ALU.subtract, sem=pool)
                p3c.append(pool.n)

            if debug >= 3 or LN2B:
                e.wait("sync", pool, pool.n)
                for i in range(4):
                    e.dma("sync", dbg["ln2"][:, 4096 * i : 4096 * (i + 1)],
                          ln2T[:, 4096 * i : 4096 * (i + 1)], sm["dbgs"])
                e.wait("sync", sm["dbgs"], sm["dbgs"].n)
            if LN2B >= 2:
                for i in range(12):
                    e.dma("sync", dbg["sb1"][:, 4096 * i : 4096 * (i + 1)],
                          sbraw[:][:, 4096 * i : 4096 * (i + 1)], sm["dbgs"])
                e.wait("sync", sm["dbgs"], sm["dbgs"].n)

            if upto <= 3:
                return
            e.wait("gpsimd", pool, pool.n)
            e.wait("gpsimd", pe, pe.n)
            e.wait("vector", sv[0], sv[0].n)
            e.wait("vector", sv[1], sv[1].n)
            e.wait("vector", pe, pe.n)
            if debug >= 1:
                e.wait("vector", sm["dbgs"], sm["dbgs"].n)
            p4_pe = []
            for g in range(p4_groups):
                gsl = g % 2
                wg = WGU_sb[:, 8192 * (2 * gsl) : 8192 * (2 * gsl + 1)]
                wu = WGU_sb[:, 8192 * (2 * gsl + 1) : 8192 * (2 * gsl + 2)]
                if g >= 2:
                    e.wait("gpsimd", pe, p4_pe[g - 2])
                src_g = WgT[:, 512 * g : 512 * (g + 1)].rearrange("(c p) n -> p c n", p=128)
                src_u = WuT[:, 512 * g : 512 * (g + 1)].rearrange("(c p) n -> p c n", p=128)
                wg_v = wg[:].rearrange("p (c n) -> p c n", c=16)
                wu_v = wu[:].rearrange("p (c n) -> p c n", c=16)
                for q4 in range(4):
                    e.dma("gpsimd", wg_v[:, 4 * q4 : 4 * (q4 + 1), :], src_g[:, 4 * q4 : 4 * (q4 + 1), :], gus[gsl])
                    e.dma("gpsimd", wu_v[:, 4 * q4 : 4 * (q4 + 1), :], src_u[:, 4 * q4 : 4 * (q4 + 1), :], gus[gsl])
                e.wait("tensor", gus[gsl], gus[gsl].n)
                for sub in range(4):
                    for half in range(2):
                        i = 8 * g + 2 * sub + half
                        if i >= 1:
                            e.wait("tensor", dve, dve.n)
                        for d in range(16):
                            rhs = ln2T[:, 1024 * d + 512 * half : 1024 * d + 512 * (half + 1)]
                            e.mm(PS[4][:], wg[:, 512 * d + 128 * sub : 512 * d + 128 * (sub + 1)],
                                 rhs, start=(d == 0), stop=(d == 15), sem=pe if d == 15 else None)
                        for d in range(16):
                            rhs = ln2T[:, 1024 * d + 512 * half : 1024 * d + 512 * (half + 1)]
                            e.mm(PS[5][:], wu[:, 512 * d + 128 * sub : 512 * d + 128 * (sub + 1)],
                                 rhs, start=(d == 0), stop=(d == 15), sem=pe if d == 15 else None)
                        e.wait("scalar", pe, pe.n)
                        sgs = sg_sb[:, 512 * (i % 2) : 512 * (i % 2 + 1)]
                        e.act(sgs, PS[4][:], AF.Silu, sem=acts)
                        e.wait("vector", acts, acts.n)
                        hc_g = 4 * g + sub
                        if half == 0:
                            dst = gated_lo if hc_g < 32 else gated_hi
                            dcol = 512 * (hc_g % 32)
                            e.vop("tensor_tensor", dst[:, dcol : dcol + 512], sgs, PS[5][:],
                                  op=ALU.mult, sem=dve)
                        else:
                            stg = vst[:, 1024 * (hc_g % 2) : 1024 * (hc_g % 2) + 512]
                            e.wait("vector", st_gb, st_gb.n)
                            e.vop("tensor_tensor", stg, sgs, PS[5][:], op=ALU.mult, sem=dve)
                            e.wait("sync", dve, dve.n)
                            e.dma("sync", gB_dram[128 * hc_g : 128 * (hc_g + 1), :], stg, st_gb)
                p4_pe.append(pe.n)

            if upto <= 4:
                return
            e.wait("gpsimd", pe, pe.n)
            e.wait("sync", pe, pe.n)
            e.wait("sync", st_gb, st_gb.n)
            if LN2B:
                for i in range(4):
                    e.dma("sync", dbg["ln2b"][:, 4096 * i : 4096 * (i + 1)],
                          ln2T[:, 4096 * i : 4096 * (i + 1)], sm["dbgs"])
                e.wait("sync", sm["dbgs"], sm["dbgs"].n)
            if LN2B >= 2:
                for i in range(12):
                    e.dma("sync", dbg["sb2"][:, 4096 * i : 4096 * (i + 1)],
                          sbraw[:][:, 4096 * i : 4096 * (i + 1)], sm["dbgs"])
                e.wait("sync", sm["dbgs"], sm["dbgs"].n)
            p5_pe = []
            p5_copy = {}
            for sc in range(8):
                wsl = sc % 2
                wd = WD_sb[:, 16384 * wsl : 16384 * (wsl + 1)]
                if sc >= 2:
                    e.wait("gpsimd", pe, p5_pe[sc - 2])
                for hc in range(8):
                    e.dma("gpsimd", wd[:, 2048 * hc : 2048 * (hc + 1)],
                          WdT[1024 * sc + 128 * hc : 1024 * sc + 128 * (hc + 1), :], wds[wsl])
                e.wait("tensor", wds[wsl], wds[wsl].n)
                gsl = sc % 2
                gbuf = gbuf2[:, 4096 * gsl : 4096 * (gsl + 1)]
                if sc >= 2:
                    e.wait("sync", pe, p5_pe[sc - 2])
                srcg = gB_dram[1024 * sc : 1024 * (sc + 1), :].rearrange("(c p) n -> p c n", p=128)
                e.dma("sync", gbuf[:].rearrange("p (c n) -> p c n", c=8), srcg, gbs[gsl])
                for half in range(2):
                    if half == 1:
                        e.wait("tensor", gbs[gsl], gbs[gsl].n)
                    for dq in range(4):
                        mm_pe = {}
                        for dc4 in range(4):
                            dci = 4 * dq + dc4
                            ps = PS[dc4]
                            if dc4 in p5_copy:
                                e.wait("tensor", dve, p5_copy[dc4])
                            for hc in range(8):
                                hc_g = 8 * sc + hc
                                if half == 0:
                                    gsrc = gated_lo if hc_g < 32 else gated_hi
                                    rhs = gsrc[:, 512 * (hc_g % 32) : 512 * (hc_g % 32 + 1)]
                                else:
                                    rhs = gbuf[:, 512 * hc : 512 * (hc + 1)]
                                e.mm(ps[:], wd[:, 2048 * hc + 128 * dci : 2048 * hc + 128 * (dci + 1)],
                                     rhs, start=(hc == 0), stop=(hc == 7),
                                     sem=pe if hc == 7 else None)
                            mm_pe[dc4] = pe.n
                        for dc4 in range(4):
                            dci = 4 * dq + dc4
                            stg = st32[:, 512 * (dc4 % 2) : 512 * (dc4 % 2 + 1)]
                            e.wait("vector", pe, mm_pe[dc4])
                            e.wait("vector", st_acc, st_acc.n)
                            e.vop("tensor_copy", stg, PS[dc4][:], sem=dve)
                            p5_copy[dc4] = dve.n
                            e.wait("gpsimd", dve, dve.n)
                            dst = acc_dram[128 * dci : 128 * (dci + 1), 512 * half : 512 * (half + 1)]
                            e.dma("gpsimd", dst, stg, st_acc, acc=(sc > 0))
                p5_pe.append(pe.n)

            if p5_mode <= 1:
                return
            e.wait("sync", st_acc, st_acc.n)
            e.wait("sync", dve, dve.n)
            for dc in range(16):
                ac = aosb_a[:, 0:1024]
                if dc >= 1:
                    e.wait("sync", dve, dve.n)
                e.dma("sync", ac, acc_dram[128 * dc : 128 * (dc + 1), :], ld)
                e.wait("vector", ld, ld.n)
                stats_chunk(dc, ac)
            rows_pass(1.0 / D)
            if p5_mode <= 2:
                return
            p5b = []
            e.wait("sync", st_x2, st_x2.n)
            e.wait("sync", dve, dve.n)
            for dc in range(16):
                ac = aosb_a[:, 0:1024]
                x2c = aosb_b[:, 0:1024]
                if dc >= 1:
                    e.wait("sync", st_y, 16 * dc)
                    e.wait("sync", dve, dve.n)
                e.dma("sync", ac, acc_dram[128 * dc : 128 * (dc + 1), :], ld)
                e.dma("sync", x2c, x2_dram[128 * dc : 128 * (dc + 1), :], ld)
                e.wait("vector", ld, ld.n)
                e.wait("vector", dve, dve.n)
                for hh in range(2):
                    t1 = t1b[:, 512 * hh : 512 * (hh + 1)]
                    e.vop("tensor_tensor", t1, ac[:, 512 * hh : 512 * (hh + 1)],
                          bc_sb[:, 1024 * hh : 1024 * hh + 512], op=ALU.mult, sem=dve)
                    e.wait("vector", dve, dve.n)
                    e.vop("tensor_sub", t1, t1, bc_sb[:, 1024 * hh + 512 : 1024 * (hh + 1)], sem=dve)
                    e.wait("vector", dve, dve.n)
                    e.vop("tensor_add", ac[:, 512 * hh : 512 * (hh + 1)],
                          x2c[:, 512 * hh : 512 * (hh + 1)], t1, sem=dve)
                e.wait("sync", dve, dve.n)
                e.dma("sync", y[128 * dc : 128 * (dc + 1), :], ac, st_y)
                p5b.append(st_y.n)


        @blk.sync
        def _(eng):
            program(E("sync", eng))

        @blk.gpsimd
        def _(eng):
            program(E("gpsimd", eng))

        @blk.tensor
        def _(eng):
            program(E("tensor", eng))

        @blk.scalar
        def _(eng):
            program(E("scalar", eng))

        @blk.vector
        def _(eng):
            program(E("vector", eng))

    return nc


def make_inputs(c, x, Wq, bq, Wk, bk, Wv, bv, Wo, bo, w_gate, w_up, w_down):
    xf = np.ascontiguousarray(x.reshape(S, D))
    rows = np.concatenate([np.arange(1024 * t + 128 * c, 1024 * t + 128 * (c + 1))
                           for t in range(NT)])
    btab_arr, _ = build_bias_table(c)
    return {
        "xT": np.ascontiguousarray(xf.T),
        "xoT": np.ascontiguousarray(xf[rows].T),
        "Wq": np.ascontiguousarray(Wq), "Wk": np.ascontiguousarray(Wk),
        "Wv": np.ascontiguousarray(Wv), "Wo": np.ascontiguousarray(Wo),
        "bq": np.ascontiguousarray(bq.reshape(H, 128).T),
        "bk": np.ascontiguousarray(bk.reshape(H, 128).T),
        "bo": np.ascontiguousarray(bo.reshape(H, 128).T),
        "bvr": bv.reshape(1, D).astype(ml_dtypes.bfloat16),
        "WgT": np.ascontiguousarray(w_gate.T),
        "WuT": np.ascontiguousarray(w_up.T),
        "WdT": np.ascontiguousarray(w_down.T),
        "btab": btab_arr,
        "masks": build_masks(c),
        "identb": np.eye(128, dtype=ml_dtypes.bfloat16),
        "onecol": np.ones((128, 1), dtype=ml_dtypes.bfloat16),
        "onerow_b": np.ones((1, 128), dtype=ml_dtypes.bfloat16),
        "onecol_f": np.ones((128, 1), dtype=ml_dtypes.bfloat16),
        "onerow_f": np.ones((1, 128), dtype=np.float32),
        "zerob": np.zeros((128, 128), dtype=ml_dtypes.bfloat16),
        "fconst": np.array([[EPS, 0.0, 1.0, 0, 0, 0, 0, 0]], dtype=np.float32),
        "fcb": np.tile(np.array([[0.0, 1.0]], dtype=np.float32), (128, 1)),
        "fcu": np.full((128, 4), 127, dtype=np.uint8),
    }


def run(inputs, debug=0, upto=99, p4_groups=16, p5_mode=3):
    args = [np.asarray(inputs[k], dtype=np.float32) for k in
            ("L0_attn_x", "L0_attn_Wq", "L0_attn_bq", "L0_attn_Wk", "L0_attn_bk",
             "L0_attn_Wv", "L0_attn_bv", "L0_attn_Wo", "L0_attn_bo",
             "L0_ffn_w_gate", "L0_ffn_w_up", "L0_ffn_w_down")]
    nc = build_nc(debug=debug, upto=upto, p4_groups=p4_groups, p5_mode=p5_mode)
    in_maps = [make_inputs(c, args[0], args[1], args[2], args[3], args[4], args[5],
                           args[6], args[7], args[8], args[9], args[10], args[11])
               for c in range(NC)]
    res = run_bass_kernel_spmd(nc, in_maps, core_ids=list(range(NC)))
    out = np.zeros((S, D), dtype=np.float32)
    for c in range(NC):
        yc = np.asarray(res.results[c]["y"]).T
        for t in range(NT):
            out[1024 * t + 128 * c : 1024 * t + 128 * (c + 1)] = yc[128 * t : 128 * (t + 1)]
    return out.reshape(1, S, D), res


def kernel(**inputs):
    out, _ = run(inputs, debug=0)
    return out


# version 10
# speedup vs baseline: 1.1116x; 1.0058x over previous
import numpy as np
import ml_dtypes
import concourse.bass as bass
import concourse.mybir as mybir
from concourse.bass_utils import run_bass_kernel_spmd
from contextlib import ExitStack

F32 = mybir.dt.float32
BF16 = mybir.dt.bfloat16
AF = mybir.ActivationFunctionType
ALU = mybir.AluOpType

S = 8192
D = 2048
H = 16
HD = 128
NC = 8
ROWS = S // NC
NT = 8
DH = 8192
EPS = 1e-5
SCALE = 1.0 / np.sqrt(HD)
T_BAND = 150.0
NOQFDUMP = 0
LN2B = 0
SBMAP = {}
NEG = -3.0e38
MASKNEG = -1.0e30

SLOPES = [2.0 ** (-0.5 * (h + 1)) for h in range(H)]
GSIZE = [1, 1, 1, 1, 1, 1, 1, 1, 1, 2, 2, 4, 4, 4, 4, 4]
HALVED = [True, True] + [False] * 14


def keep_subtile(h, t_near, b, s):
    dist = 1024 * (t_near - b) - 128 * s - 127
    return SLOPES[h] * max(0, dist) <= T_BAND


def attention_schedule():
    out = []
    for h in range(H):
        G = GSIZE[h]
        for tg in range(2):
            items = []
            for qg in range(4 // G):
                t_lo = tg * 4 + qg * G
                t_hi = t_lo + G - 1
                for b in range(t_lo):
                    for s in range(NC):
                        if not keep_subtile(h, t_lo, b, s):
                            continue
                        if HALVED[h]:
                            for hf in range(2):
                                items.append((b, s, ("n", h, t_hi - b, s, hf),
                                              128 * t_lo + 64 * hf, 64,
                                              128 * (t_lo - 4 * tg) + 64 * hf, None))
                        else:
                            items.append((b, s, ("n", h, t_hi - b, s, 0),
                                          128 * t_lo, 128 * G, 128 * (t_lo - 4 * tg), None))
                for t in range(t_lo, t_hi + 1):
                    for b in range(t_lo, t + 1):
                        diag = (b == t)
                        for s in range(NC):
                            if not keep_subtile(h, t, b, s):
                                continue
                            halves = (0, 1) if HALVED[h] else (0,)
                            for hf in halves:
                                n = 64 if HALVED[h] else 128
                                key = (("d", h, t_hi - t, s, hf) if diag
                                       else ("n", h, t_hi - b, s, hf))
                                mask = (128 * s + 64 * hf) if diag else None
                                items.append((b, s, key,
                                              128 * t + 64 * hf, n,
                                              128 * (t - 4 * tg) + 64 * hf, mask))
            out.append((h, tg, items))
    return out


SCHED = attention_schedule()


def build_bias_table(c):
    keys = []
    seen = set()
    for (_h, _tg, items) in SCHED:
        for (_b, _s, key, _qc, _n, _cc, _m) in items:
            if key not in seen:
                seen.add(key)
                keys.append(key)
    p = np.arange(128, dtype=np.float64)
    cols = {}
    data = []
    for key in keys:
        kind, h, dlt, s, hf = key
        sl = SLOPES[h]
        if kind == "d" and s > c:
            v = np.full(128, NEG)
        else:
            v = sl * (-1024.0 * dlt + 128.0 * (s - c) - 64.0 * hf + p)
        cols[key] = len(data)
        data.append(v)
    a = np.stack(data, axis=1).astype(np.float32)
    ncol = int(np.ceil(a.shape[1] / 16.0) * 16)
    arr = np.zeros((128, max(ncol, 16)), dtype=np.float32)
    arr[:, : a.shape[1]] = a
    return arr, cols


BT_ARR0, COLS = build_bias_table(0)
NCOL = BT_ARR0.shape[1]


def build_masks(c):
    m = np.zeros((128, NC * 128), dtype=np.float32)
    tri = np.where(np.arange(128)[:, None] <= np.arange(128)[None, :], 0.0, MASKNEG)
    m[:, 128 * c : 128 * (c + 1)] = tri
    return m.astype(ml_dtypes.bfloat16)


class Sem:
    __slots__ = ("h", "n")

    def __init__(self, h):
        self.h = h
        self.n = 0


class E:

    def __init__(self, which, eng):
        self.w = which
        self.e = eng

    def _fin(self, inst, sem, k):
        if sem is not None:
            if inst is not None:
                inst.then_inc(sem.h, k)
            sem.n += k
        return inst

    def dma(self, q, out, in_, sem=None, k=16, acc=False):
        i = None
        if self.w == q:
            if acc:
                i = self.e.dma_start(out=out, in_=in_, accum_op=ALU.add)
            else:
                i = self.e.dma_start(out=out, in_=in_)
        return self._fin(i, sem, k)

    def wait(self, q, sem, val=None):
        v = sem.n if val is None else val
        if self.w == q and v > 0:
            self.e.wait_ge(sem.h, v)

    def mm(self, out, lhsT, rhs, start, stop, sem=None, k=1):
        i = None
        if self.w == "tensor":
            i = self.e.matmul(out, lhsT, rhs, start=start, stop=stop)
        return self._fin(i, sem, k)

    def act(self, out, in_, func, bias=0.0, scale=1.0, sem=None, k=1):
        i = None
        if self.w == "scalar":
            i = self.e.activation(out, in_, func, bias=bias, scale=scale)
        return self._fin(i, sem, k)

    def act_table(self, func):
        if self.w == "scalar":
            self.e.preload_act_table(func)

    def vop(self, fn, *args, sem=None, k=1, **kw):
        i = None
        if self.w == "vector":
            i = getattr(self.e, fn)(*args, **kw)
        return self._fin(i, sem, k)

    def pop(self, fn, *args, sem=None, k=1, **kw):
        i = None
        if self.w == "gpsimd":
            i = getattr(self.e, fn)(*args, **kw)
        return self._fin(i, sem, k)


def build_nc(debug=0, upto=99, p4_groups=16, p5_mode=3):
    nc = bass.Bass("TRN2", target_bir_lowering=False, dynamic_dma_scratch_size=4096)

    ap = lambda t: t.ap()
    xT = ap(nc.dram_tensor("xT", [D, S], F32, kind="ExternalInput"))
    xoT = ap(nc.dram_tensor("xoT", [D, ROWS], F32, kind="ExternalInput"))
    Wq = ap(nc.dram_tensor("Wq", [D, D], F32, kind="ExternalInput"))
    Wk = ap(nc.dram_tensor("Wk", [D, D], F32, kind="ExternalInput"))
    Wv = ap(nc.dram_tensor("Wv", [D, D], F32, kind="ExternalInput"))
    Wo = ap(nc.dram_tensor("Wo", [D, D], F32, kind="ExternalInput"))
    bq = ap(nc.dram_tensor("bq", [128, H], F32, kind="ExternalInput"))
    bk = ap(nc.dram_tensor("bk", [128, H], F32, kind="ExternalInput"))
    bo = ap(nc.dram_tensor("bo", [128, H], F32, kind="ExternalInput"))
    bvr = ap(nc.dram_tensor("bvr", [1, D], BF16, kind="ExternalInput"))
    WgT = ap(nc.dram_tensor("WgT", [D, DH], F32, kind="ExternalInput"))
    WuT = ap(nc.dram_tensor("WuT", [D, DH], F32, kind="ExternalInput"))
    WdT = ap(nc.dram_tensor("WdT", [DH, D], F32, kind="ExternalInput"))
    btab = ap(nc.dram_tensor("btab", [128, NCOL], F32, kind="ExternalInput"))
    masks = ap(nc.dram_tensor("masks", [128, NC * 128], BF16, kind="ExternalInput"))
    identb = ap(nc.dram_tensor("identb", [128, 128], BF16, kind="ExternalInput"))
    onecol = ap(nc.dram_tensor("onecol", [128, 1], BF16, kind="ExternalInput"))
    onerow_b = ap(nc.dram_tensor("onerow_b", [1, 128], BF16, kind="ExternalInput"))
    onecol_f = ap(nc.dram_tensor("onecol_f", [128, 1], BF16, kind="ExternalInput"))
    onerow_f = ap(nc.dram_tensor("onerow_f", [1, 128], F32, kind="ExternalInput"))
    zerob = ap(nc.dram_tensor("zerob", [128, 128], BF16, kind="ExternalInput"))
    fconst = ap(nc.dram_tensor("fconst", [1, 8], F32, kind="ExternalInput"))
    fcb = ap(nc.dram_tensor("fcb", [128, 2], F32, kind="ExternalInput"))
    fcu = ap(nc.dram_tensor("fcu", [128, 4], mybir.dt.uint8, kind="ExternalInput"))

    y = ap(nc.dram_tensor("y", [D, ROWS], F32, kind="ExternalOutput"))

    ikind = "ExternalOutput" if debug >= 1 else "Internal"
    ln_dram = ap(nc.dram_tensor("ln_dram", [D, S], BF16, kind=ikind))
    K_g = ap(nc.dram_tensor("K_g", [D, S], BF16, kind=ikind))
    V_g = ap(nc.dram_tensor("V_g", [S, D], BF16, kind=ikind))
    ao_dram = ap(nc.dram_tensor("ao_dram", [D, ROWS], F32, kind=ikind))
    x2_dram = ap(nc.dram_tensor("x2_dram", [D, ROWS], F32, kind=ikind))
    gB_dram = ap(nc.dram_tensor("gB_dram", [DH, 512], BF16, kind=ikind))
    acc_dram = ap(nc.dram_tensor("acc_dram", [D, ROWS], F32, kind=ikind))

    pad_sink = ap(nc.dram_tensor("pad_sink", [1, 64], mybir.dt.uint8, kind="Internal"))

    dbg = {}
    if debug >= 1:
        dbg["Qf"] = ap(nc.dram_tensor("dbg_Qf", [128, H * 1024], BF16, kind="ExternalOutput"))
    if debug >= 2:
        dbg["ctx"] = ap(nc.dram_tensor("dbg_ctx", [128, H * 1024], BF16, kind="ExternalOutput"))
    if debug >= 3:
        dbg["x2"] = ap(nc.dram_tensor("dbg_x2", [D, ROWS], F32, kind="ExternalOutput"))
    if debug >= 3 or LN2B:
        dbg["ln2"] = ap(nc.dram_tensor("dbg_ln2", [128, 16 * 1024], BF16, kind="ExternalOutput"))
    if LN2B:
        dbg["ln2b"] = ap(nc.dram_tensor("dbg_ln2b", [128, 16 * 1024], BF16, kind="ExternalOutput"))
    if LN2B >= 2:
        U8 = mybir.dt.uint8
        dbg["sb1"] = ap(nc.dram_tensor("dbg_sb1", [128, 49152], U8, kind="ExternalOutput"))
        dbg["sb2"] = ap(nc.dram_tensor("dbg_sb2", [128, 49152], U8, kind="ExternalOutput"))
    if debug >= 5:
        dbg["rows"] = ap(nc.dram_tensor("dbg_rows", [1, 3072], F32, kind="ExternalOutput"))
        dbg["bc"] = ap(nc.dram_tensor("dbg_bc", [128, 2048], F32, kind="ExternalOutput"))
        dbg["xbb"] = ap(nc.dram_tensor("dbg_xbb", [128, 1024], BF16, kind="ExternalOutput"))
        dbg["sqb"] = ap(nc.dram_tensor("dbg_sqb", [128, 1024], BF16, kind="ExternalOutput"))
        dbg["pl"] = ap(nc.dram_tensor("dbg_pl", [1, 8192], F32, kind="ExternalOutput"))

    with ExitStack() as stk:
        EE = stk.enter_context

        from math import prod
        ring_pad = nc.alloc_sbuf_tensor_at("ring_pad", [128, 12288], mybir.dt.uint8, offset=4096)
        _cur = [16384]
        sbraw = nc.alloc_sbuf_tensor_at("sbraw", [128, 49152], mybir.dt.uint8, offset=4096) if LN2B >= 2 else None

        def region(tensors):
            base = _cur[0]
            out = []
            end = base
            for name, shape, dt in tensors:
                out.append(nc.alloc_sbuf_tensor_at(name, shape, dt, offset=base))
                SBMAP[name] = (base, base + prod(shape[1:]) * mybir.dt.size(dt))
                end = max(end, base + prod(shape[1:]) * mybir.dt.size(dt))
            _cur[0] = (end + 31) // 32 * 32
            assert _cur[0] <= nc.sbuf_top, f"SBUF overflow: {_cur[0]} > {nc.sbuf_top}"
            return out if len(out) > 1 else out[0]

        btab_sb = region([("btab_sb", [128, NCOL], F32)])
        masks_sb = region([("masks_sb", [128, NC * 128], BF16)])
        identb_sb = region([("identb_sb", [128, 128], BF16)])
        onecol_sb = region([("onecol_sb", [128, 1], BF16)])
        onerow_b_sb = region([("onerow_b_sb", [1, 128], BF16)])
        onecol_f_sb = region([("onecol_f_sb", [128, 1], BF16)])
        onerow_f_sb = region([("onerow_f_sb", [1, 128], F32)])
        zerob_sb = region([("zerob_sb", [128, 128], BF16)])
        bq_sb = region([("bq_sb", [128, H], F32)])
        bk_sb = region([("bk_sb", [128, H], F32)])
        bo_sb = region([("bo_sb", [128, H], F32)])
        fc_sb = region([("fc_sb", [1, 8], F32)])
        fcb_sb = region([("fcb_sb", [128, 2], F32)])
        fcu_sb = region([("fcu_sb", [128, 4], mybir.dt.uint8)])

        LNO, ln_sl2, ln2T, gbuf2 = region([
            ("LNO", [128, 16 * 1024], BF16),
            ("ln_sl2", [128, 2 * 8192], BF16),
            ("ln2T", [128, 16 * 1024], BF16),
            ("gbuf2", [128, 2 * 4096], BF16),
        ])
        Qf, gated_lo = region([
            ("Qf", [128, H * 1024], BF16),
            ("gated_lo", [128, 32 * 512], BF16),
        ])
        ctx_fm, gated_hi = region([
            ("ctx_fm", [128, H * 1024], BF16),
            ("gated_hi", [128, 32 * 512], BF16),
        ])
        W_sb, KV_sb, WGU_sb, WD_sb = region([
            ("W_sb", [128, 16 * 2048], BF16),
            ("KV_sb", [128, 2 * 16384], BF16),
            ("WGU_sb", [128, 4 * 8192], BF16),
            ("WD_sb", [128, 2 * 16384], BF16),
        ])
        xst_b = region([("xst_b", [128, 2 * 512], F32)])
        xnm_b = region([("xnm_b", [128, 2 * 512], F32)])
        sqb = region([("sqb", [128, 2 * 512], BF16)])
        xbb = region([("xbb", [128, 2 * 512], BF16)])
        kst, vst, ptil = region([
            ("kst", [128, 2 * 512], BF16),
            ("vst", [128, 2 * 1024], BF16),
            ("ptil", [128, 3 * 512], BF16),
        ])
        aosb_a = nc.alloc_sbuf_tensor_at("aosb_a", [128, 1024], F32, offset=SBMAP["xnm_b"][0])
        aosb_b = nc.alloc_sbuf_tensor_at("aosb_b", [128, 1024], F32, offset=SBMAP["kst"][0])
        bvr_sb = nc.alloc_sbuf_tensor_at("bvr_sb2", [1, D], BF16, offset=SBMAP["ctx_fm"][0])
        rows_sb = region([("rows_sb", [1, 3072], F32)])
        linv_sb = rows_sb
        bc_sb = region([("bc_sb", [128, 4 * 512], F32)])
        t1b, lbc_sb, sg_sb, st32 = region([
            ("t1b", [128, 2 * 512], F32),
            ("lbc_sb", [128, 2 * 512], F32),
            ("sg_sb", [128, 2 * 512], F32),
            ("st32", [128, 2 * 512], F32),
        ])

        PS = [EE(nc.psum_tensor(f"PS{i}", [128, 512], F32)) for i in range(6)]
        PL = [EE(nc.psum_tensor(f"PL{i}", [1, 512], F32)) for i in range(2)]

        nc.const_aps.register(F32, EPS, fc_sb[0:1, 0:1])
        nc.const_aps.register(F32, 0.0, fcb_sb[:, 0:1])
        nc.const_aps.register(F32, 1.0, fcb_sb[:, 1:2])
        nc.const_aps.register(BF16, 1.0, onecol_sb[:, 0:1])
        nc.const_aps.register(mybir.dt.uint8, 127, fcu_sb[:, 0:1])
        sem_names = ["ld", "ldc", "pe", "acts", "dve", "pool", "sto", "pe2", "acte",
                     "la0", "la1", "ln0", "ln1", "lb0", "lb1", "kv0", "kv1",
                     "gu0", "gu1", "wd0", "wd1", "gb0", "gb1",
                     "sk0", "sk1", "sv0", "sv1", "sl0", "sl1",
                     "st_ao", "st_x2", "st_acc", "st_gb", "st_kv", "st_y", "dbgs"]
        sem_h = {n: EE(nc.semaphore(n)) for n in sem_names}

        blk = EE(nc.Block())
        dbg_state = {}

        def ln_fm(e, sm, dram_col0, dram_src, n, out, war_sem=None, war_val=0):
            ld = sm["ld"]; pe = sm["pe"]; acts = sm["acts"]; dve = sm["dve"]; pool = sm["pool"]
            la = (sm["la0"], sm["la1"])
            lnS = (sm["ln0"], sm["ln1"])
            nh = n // 512
            sums = [(PL[0], PL[1]), (PS[4], PS[5])]
            cnt = 0
            war = []
            for g in range(16):
                for hh in range(nh):
                    sl = cnt % 2
                    xc = xst_b[:, 512 * sl : 512 * (sl + 1)]
                    xb = xbb[:, 512 * sl : 512 * (sl + 1)]
                    sq = sqb[:, 512 * sl : 512 * (sl + 1)]
                    if cnt >= 2:
                        e.wait("sync", pe, war[cnt - 2])
                    e.dma("sync", xc, dram_src[128 * g : 128 * (g + 1),
                                              dram_col0 + 512 * hh : dram_col0 + 512 * (hh + 1)], la[sl])
                    e.wait("scalar", la[sl], la[sl].n)
                    if cnt >= 2:
                        e.wait("scalar", pe, war[cnt - 2])
                    e.act(xb, xc, AF.Copy, sem=acts)
                    e.wait("vector", acts, acts.n)
                    e.vop("tensor_mul", sq, xb, xb, sem=dve)
                    e.wait("tensor", dve, dve.n)
                    ts_, tq_ = sums[hh]
                    e.mm(ts_[0:1, 0:512], onecol_f_sb[:], xb, start=(g == 0), stop=(g == 15))
                    e.mm(tq_[0:1, 0:512], onecol_f_sb[:], sq, start=(g == 0), stop=(g == 15),
                         sem=pe)
                    war.append(pe.n)
                    cnt += 1
            if debug >= 5 and not dbg_state.get("pl_dumped"):
                dbg_state["pl_dumped"] = True
                e.wait("vector", pe, pe.n)
                e.vop("tensor_copy", t1b[:, 0:512][0:1, :], PL[0][0:1, :], sem=dve)
                e.vop("tensor_copy", t1b[:, 512:1024][0:1, :], PL[1][0:1, :], sem=dve)
                e.wait("sync", dve, dve.n)
                e.dma("sync", dbg["pl"][0:1, 0:1024], t1b[0:1, 0:1024], sm["dbgs"])
            r_mean = rows_sb[0:1, 0:n]
            r_var = rows_sb[0:1, 1024 : 1024 + n]
            r_rstd = rows_sb[0:1, 2048 : 2048 + n]
            e.wait("vector", pe, pe.n)
            for hh in range(nh):
                ts_, tq_ = sums[hh]
                e.vop("tensor_scalar_mul", r_mean[0:1, 512 * hh : 512 * (hh + 1)],
                      ts_[0:1, 0:512], 1.0 / D, sem=dve)
                e.vop("tensor_scalar_mul", r_var[0:1, 512 * hh : 512 * (hh + 1)],
                      tq_[0:1, 0:512], 1.0 / D, sem=dve)
            dbgrows = debug >= 5 and not dbg_state.get("rows_dumped")
            def snap(i):
                if dbgrows:
                    e.wait("sync", dve, dve.n)
                    e.wait("sync", acts, acts.n)
                    e.dma("sync", dbg["pl"][0:1, 1024 * i : 1024 * i + 512],
                          rows_sb[0:1, 1024:1536], sm["dbgs"])
                    e.dma("sync", dbg["pl"][0:1, 1024 * i + 512 : 1024 * (i + 1)],
                          rows_sb[0:1, 2048:2560], sm["dbgs"])
                    e.wait("vector", sm["dbgs"], sm["dbgs"].n)
                    e.wait("scalar", sm["dbgs"], sm["dbgs"].n)
            e.wait("vector", dve, dve.n)
            snap(1)
            e.vop("tensor_tensor", r_rstd, r_mean, r_mean, op=ALU.mult, sem=dve)
            e.wait("vector", dve, dve.n)
            snap(2)
            e.vop("tensor_sub", r_var, r_var, r_rstd, sem=dve)
            e.wait("scalar", dve, dve.n)
            snap(3)
            e.act(r_rstd, r_var, AF.Ln, bias=EPS, sem=acts)
            e.wait("scalar", acts, acts.n)
            snap(4)
            e.act(r_rstd, r_rstd, AF.Exp, scale=-0.5, sem=acts)
            e.wait("vector", acts, acts.n)
            snap(5)
            if dbgrows:
                dbg_state["rows_dumped"] = True
            e.vop("tensor_tensor", r_var, r_mean, r_rstd, op=ALU.mult, sem=dve)
            e.wait("tensor", dve, dve.n)
            for hh in range(nh):
                if hh >= 1:
                    e.wait("tensor", dve, dve.n)
                e.mm(PS[4][:, 0:512], onerow_f_sb[:], r_rstd[0:1, 512 * hh : 512 * (hh + 1)],
                     start=True, stop=True, sem=pe)
                e.mm(PS[5][:, 0:512], onerow_f_sb[:], r_var[0:1, 512 * hh : 512 * (hh + 1)],
                     start=True, stop=True, sem=pe)
                e.wait("vector", pe, pe.n)
                e.vop("tensor_copy", bc_sb[:, 1024 * hh : 1024 * hh + 512], PS[4][:, 0:512], sem=dve)
                e.vop("tensor_copy", bc_sb[:, 1024 * hh + 512 : 1024 * (hh + 1)], PS[5][:, 0:512],
                      sem=dve)
            if war_sem is not None:
                e.wait("gpsimd", war_sem, war_val)
            cnt = 0
            dwar = []
            pwar = []
            for g in range(16):
                for hh in range(nh):
                    sl = cnt % 2
                    xc = xnm_b[:, 512 * sl : 512 * (sl + 1)]
                    t1 = t1b[:, 512 * sl : 512 * (sl + 1)]
                    if cnt >= 2:
                        e.wait("sync", dve, dwar[cnt - 2])
                    e.dma("sync", xc, dram_src[128 * g : 128 * (g + 1),
                                              dram_col0 + 512 * hh : dram_col0 + 512 * (hh + 1)], lnS[sl])
                    if cnt >= 2:
                        e.wait("vector", pool, pwar[cnt - 2])
                    e.wait("vector", lnS[sl], lnS[sl].n)
                    e.wait("vector", dve, dve.n)
                    e.vop("tensor_tensor", t1, xc,
                          bc_sb[:, 1024 * hh : 1024 * hh + 512], op=ALU.mult, sem=dve)
                    dwar.append(dve.n)
                    e.wait("gpsimd", dve, dve.n)
                    e.pop("tensor_tensor", out(g)[:, 512 * hh : 512 * (hh + 1)],
                          t1, bc_sb[:, 1024 * hh + 512 : 1024 * (hh + 1)],
                          op=ALU.subtract, sem=pool)
                    pwar.append(pool.n)
                    cnt += 1

        def program(e):
            dbg_state.clear()
            sm = {n: Sem(h) for n, h in sem_h.items()}
            phases(e, sm)
            for nm in ("st_y", "dbgs", "sto", "st_ao", "st_x2", "st_acc", "st_gb",
                       "sk0", "sk1", "sv0", "sv1", "sl0", "sl1"):
                e.wait("sync", sm[nm])
            e.wait("gpsimd", sm["st_y"])

        def phases(e, sm):
            ld = sm["ld"]; ldc = sm["ldc"]; pe = sm["pe"]; acts = sm["acts"]
            dve = sm["dve"]; pool = sm["pool"]; sto = sm["sto"]
            pe2 = sm["pe2"]; acte = sm["acte"]
            sk = (sm["sk0"], sm["sk1"]); sv = (sm["sv0"], sm["sv1"])
            slst = (sm["sl0"], sm["sl1"]); kvs = (sm["kv0"], sm["kv1"])
            gus = (sm["gu0"], sm["gu1"]); wds = (sm["wd0"], sm["wd1"])
            gbs = (sm["gb0"], sm["gb1"])
            st_ao = sm["st_ao"]; st_x2 = sm["st_x2"]; st_acc = sm["st_acc"]
            st_gb = sm["st_gb"]; st_kv = sm["st_kv"]; st_y = sm["st_y"]

            for dst, src in [(fc_sb, fconst), (fcb_sb, fcb), (fcu_sb, fcu), (btab_sb, btab), (masks_sb, masks), (identb_sb, identb),
                             (onecol_sb, onecol), (onerow_b_sb, onerow_b),
                             (onecol_f_sb, onecol_f), (onerow_f_sb, onerow_f),
                             (zerob_sb, zerob), (bq_sb, bq), (bk_sb, bk),
                             (bo_sb, bo), (bvr_sb, bvr)]:
                e.dma("sync", dst[:], src, ld)
            for q in ("sync", "gpsimd", "tensor", "scalar", "vector"):
                e.wait(q, ld)

            ln_fm(e, sm, 0, xoT, 1024,
                  out=lambda g: LNO[:, 1024 * g : 1024 * (g + 1)])
            lno_done = pool.n
            if debug >= 5:
                e.wait("sync", pool, pool.n)
                e.dma("sync", dbg["rows"], rows_sb[0:1, :], sm["dbgs"])
                e.dma("sync", dbg["bc"], bc_sb[:], sm["dbgs"])
                e.dma("sync", dbg["xbb"], xbb[:], sm["dbgs"])
                e.dma("sync", dbg["sqb"], sqb[:], sm["dbgs"])

            for d in range(16):
                e.dma("gpsimd", W_sb[:, 2048 * d : 2048 * (d + 1)],
                      Wq[128 * d : 128 * (d + 1), :], ldc)
            e.wait("tensor", ldc, ldc.n)
            e.wait("tensor", pool, lno_done)
            q_ev = []
            for h in range(H):
                for half in range(2):
                    i = 2 * h + half
                    ps = PS[i % 2]
                    if i >= 2:
                        e.wait("tensor", acts, q_ev[i - 2])
                    for d in range(16):
                        e.mm(ps[:], W_sb[:, 2048 * d + 128 * h : 2048 * d + 128 * (h + 1)],
                             LNO[:, 1024 * d + 512 * half : 1024 * d + 512 * (half + 1)],
                             start=(d == 0), stop=(d == 15), sem=pe if d == 15 else None)
                    e.wait("scalar", pe, pe.n)
                    e.act(Qf[:, 1024 * h + 512 * half : 1024 * h + 512 * (half + 1)], ps[:],
                          AF.Identity, bias=bq_sb[:, h : h + 1], sem=acts)
                    q_ev.append(acts.n)

            if debug >= 1 and not NOQFDUMP:
                e.wait("sync", acts, acts.n)
                for i in range(4):
                    e.dma("sync", dbg["Qf"][:, 4096 * i : 4096 * (i + 1)],
                          Qf[:, 4096 * i : 4096 * (i + 1)], sm["dbgs"])

            if upto <= 0:
                return
            e.wait("gpsimd", pe, pe.n)
            for d in range(16):
                e.dma("gpsimd", W_sb[:, 2048 * d : 2048 * (d + 1)],
                      Wk[128 * d : 128 * (d + 1), :], ldc)
            wk_ld = ldc.n

            ln_done = []
            ln_st = []
            k_ev = []
            kst_free = []
            kts_pe = []

            def k_pass(ts):
                lnb = ln_sl2[:, (ts % 2) * 8192 : (ts % 2 + 1) * 8192]
                if ts == 0:
                    e.wait("tensor", ldc, wk_ld)
                e.wait("tensor", pool, ln_done[ts])
                for h in range(H):
                    i = 16 * ts + h
                    sl = i % 2
                    ps = PS[2 + sl]
                    if i >= 2:
                        e.wait("tensor", acts, k_ev[i - 2])
                    for d in range(16):
                        e.mm(ps[:], W_sb[:, 2048 * d + 128 * h : 2048 * d + 128 * (h + 1)],
                             lnb[:, 512 * d : 512 * (d + 1)],
                             start=(d == 0), stop=(d == 15), sem=pe if d == 15 else None)
                    e.wait("scalar", pe, pe.n)
                    if i >= 2:
                        e.wait("scalar", sk[sl], 16 * (i // 2))
                    e.act(kst[:, 512 * sl : 512 * (sl + 1)], ps[:], AF.Identity,
                          bias=bk_sb[:, h : h + 1], sem=acts)
                    k_ev.append(acts.n)
                    e.wait("sync", acts, acts.n)
                    e.dma("sync", K_g[128 * h : 128 * (h + 1), 512 * ts : 512 * (ts + 1)],
                          kst[:, 512 * sl : 512 * (sl + 1)], sk[sl])
                kts_pe.append(pe.n)

            for ts in range(16):
                lnb = ln_sl2[:, (ts % 2) * 8192 : (ts % 2 + 1) * 8192]
                if ts >= 2:
                    e.wait("gpsimd", slst[ts % 2], 256 * (ts // 2))
                war_val = kts_pe[ts - 2] if ts >= 2 else 0
                ln_fm(e, sm, 512 * ts, xT, 512,
                      out=lambda g, lnb=lnb: lnb[:, 512 * g : 512 * (g + 1)],
                      war_sem=pe if ts >= 2 else None, war_val=war_val)
                ln_done.append(pool.n)
                e.wait("sync", pool, pool.n)
                for g in range(16):
                    e.dma("sync", ln_dram[128 * g : 128 * (g + 1), 512 * ts : 512 * (ts + 1)],
                          lnb[:, 512 * g : 512 * (g + 1)], slst[ts % 2])
                if ts >= 1:
                    k_pass(ts - 1)
            k_pass(15)

            e.wait("gpsimd", pe, pe.n)
            for d in range(16):
                e.dma("gpsimd", W_sb[:, 2048 * d : 2048 * (d + 1)],
                      Wv[128 * d : 128 * (d + 1), :], ldc)
            e.wait("tensor", ldc, ldc.n)
            e.wait("sync", sm["sl0"], sm["sl0"].n)
            e.wait("sync", sm["sl1"], sm["sl1"].n)
            e.wait("sync", pe, kts_pe[-1])
            e.wait("scalar", sk[0], sk[0].n)
            e.wait("scalar", sk[1], sk[1].n)
            v_ev = []
            vts_pe = []
            lb = (sm["lb0"], sm["lb1"])
            for ts in range(16):
                lnb = ln_sl2[:, (ts % 2) * 8192 : (ts % 2 + 1) * 8192]
                if ts >= 2:
                    e.wait("sync", pe, vts_pe[ts - 2])
                for g in range(16):
                    e.dma("sync", lnb[:, 512 * g : 512 * (g + 1)],
                          ln_dram[128 * g : 128 * (g + 1), 512 * ts : 512 * (ts + 1)], lb[ts % 2])
                e.wait("tensor", lb[ts % 2], lb[ts % 2].n)
                for tt in range(4):
                    for dh in range(2):
                        i = len(v_ev)
                        sl = i % 2
                        pa, pb = PS[2 * sl], PS[2 * sl + 1]
                        if i >= 2:
                            e.wait("tensor", acts, v_ev[i - 2])
                        for d in range(16):
                            lhsT = lnb[:, 512 * d + 128 * tt : 512 * d + 128 * (tt + 1)]
                            e.mm(pa[:], lhsT,
                                 W_sb[:, 2048 * d + 1024 * dh : 2048 * d + 1024 * dh + 512],
                                 start=(d == 0), stop=False)
                            e.mm(pb[:], lhsT,
                                 W_sb[:, 2048 * d + 1024 * dh + 512 : 2048 * d + 1024 * (dh + 1)],
                                 start=(d == 0), stop=False)
                        e.mm(pa[:], onerow_b_sb[:], bvr_sb[0:1, 1024 * dh : 1024 * dh + 512],
                             start=False, stop=True)
                        e.mm(pb[:], onerow_b_sb[:], bvr_sb[0:1, 1024 * dh + 512 : 1024 * (dh + 1)],
                             start=False, stop=True, sem=pe)
                        e.wait("scalar", pe, pe.n)
                        if i >= 2:
                            e.wait("scalar", sv[sl], 16 * (i // 2))
                        e.act(vst[:, 1024 * sl : 1024 * sl + 512], pa[:], AF.Identity)
                        e.act(vst[:, 1024 * sl + 512 : 1024 * (sl + 1)], pb[:], AF.Identity, sem=acts)
                        v_ev.append(acts.n)
                        e.wait("sync", acts, acts.n)
                        e.dma("sync", V_g[512 * ts + 128 * tt : 512 * ts + 128 * (tt + 1),
                                          1024 * dh : 1024 * (dh + 1)],
                              vst[:, 1024 * sl : 1024 * (sl + 1)], sv[sl])
                vts_pe.append(pe.n)

            if upto <= 1:
                return
            for s_ in (sm["sk0"], sm["sk1"], sm["sv0"], sm["sv1"]):
                e.wait("sync", s_, s_.n)
            e.wait("sync", pe, pe.n)
            e.wait("scalar", sv[0], sv[0].n)
            e.wait("scalar", sv[1], sv[1].n)
            e.wait("tensor", acts, acts.n)
            e.wait("tensor", dve, dve.n)
            kv_ld = []
            att_done = []
            ctx_evac = {0: 0, 1: 0}

            def load_kv(h):
                half = h % 2
                kb = KV_sb[:, 16384 * half : 16384 * half + 8192]
                vb = KV_sb[:, 16384 * half + 8192 : 16384 * (half + 1)]
                if h >= 2:
                    e.wait("sync", pe, att_done[h - 2])
                e.dma("sync", kb[:], K_g[128 * h : 128 * (h + 1), :], kvs[half])
                for b in range(NC):
                    src = V_g[1024 * b : 1024 * (b + 1), 128 * h : 128 * (h + 1)]
                    src = src.rearrange("(s k) d -> k s d", k=128)
                    dst = vb[:, 1024 * b : 1024 * (b + 1)].rearrange("k (s d) -> k s d", s=8)
                    e.dma("sync", dst, src, kvs[half])
                kv_ld.append(kvs[half].n)

            load_kv(0)

            q_state = {"q": []}

            def flush(idx):
                q = q_state["q"]
                ent = q[idx]
                if ent["done"]:
                    return
                e.wait("tensor", acte, ent["exp"])
                pt = ptil[:, 512 * ent["slot"] : 512 * ent["slot"] + ent["N"]]
                e.mm(ent["l_ps"][0:1, ent["ctxcol"] : ent["ctxcol"] + ent["N"]],
                     onecol_sb[:], pt, start=False, stop=False)
                e.mm(ent["ctx_ps"][:, ent["ctxcol"] : ent["ctxcol"] + ent["N"]],
                     ent["vb"][:, 1024 * ent["b"] + 128 * ent["s"] : 1024 * ent["b"] + 128 * (ent["s"] + 1)],
                     pt, start=False, stop=False, sem=pe)
                ent["done"] = True
                ent["ctxpe"] = pe.n

            for (h, tg, items) in SCHED:
                half = h % 2
                kb = KV_sb[:, 16384 * half : 16384 * half + 8192]
                vb = KV_sb[:, 16384 * half + 8192 : 16384 * (half + 1)]
                if tg == 0:
                    if h + 1 < H:
                        load_kv(h + 1)
                    e.wait("tensor", kvs[h % 2], kv_ld[h])
                pp = (2 * h + tg) % 2
                ctx_ps = PS[2 + pp]
                l_ps = PL[pp]
                if 2 * h + tg >= 2:
                    e.wait("tensor", dve, ctx_evac[pp])
                e.mm(ctx_ps[:], zerob_sb[:], Qf[:, 0:512], start=True, stop=False)
                e.mm(l_ps[0:1, :], zerob_sb[:, 0:1], Qf[:, 0:512], start=True, stop=False)
                for (b, s, colkey, qoff, N, ctxcol, mask) in items:
                    q = q_state["q"]
                    i = len(q)
                    slot = i % 3
                    sps = PS[(0, 1, 4)[slot]]
                    if i >= 3:
                        e.wait("tensor", acte, q[i - 3]["exp"])
                    e.mm(sps[:, 0:N], kb[:, 1024 * b + 128 * s : 1024 * b + 128 * (s + 1)],
                         Qf[:, 1024 * h + qoff : 1024 * h + qoff + N],
                         start=True, stop=(mask is None), sem=None if mask is not None else pe2)
                    if mask is not None:
                        e.mm(sps[:, 0:N], identb_sb[:], masks_sb[:, mask : mask + N],
                             start=False, stop=True, sem=pe2)
                    e.wait("scalar", pe2, pe2.n)
                    if i >= 3:
                        e.wait("scalar", pe, q[i - 3]["ctxpe"])
                    cidx = COLS[colkey]
                    e.act(ptil[:, 512 * slot : 512 * slot + N], sps[:, 0:N], AF.Exp,
                          bias=btab_sb[:, cidx : cidx + 1], scale=SCALE, sem=acte)
                    q.append(dict(exp=acte.n, slot=slot, N=N, vb=vb, b=b, s=s,
                                  ctx_ps=ctx_ps, l_ps=l_ps, ctxcol=ctxcol,
                                  ctxpe=0, done=False))
                    if i >= 2:
                        flush(i - 2)
                if len(q_state["q"]) >= 2:
                    flush(-2)
                if q_state["q"]:
                    flush(-1)
                e.mm(ctx_ps[:], zerob_sb[:], Qf[:, 0:512], start=False, stop=True)
                e.mm(l_ps[0:1, :], zerob_sb[:, 0:1], Qf[:, 0:512], start=False, stop=True, sem=pe)
                peval = pe.n
                e.wait("vector", pe, peval)
                lrow = linv_sb[0:1, 512 * pp : 512 * (pp + 1)]
                e.vop("reciprocal", lrow, l_ps[0:1, :], sem=dve)
                e.wait("tensor", dve, dve.n)
                e.mm(PS[5][:, 0:512], onerow_f_sb[:], lrow, start=True, stop=True, sem=pe)
                e.wait("vector", pe, pe.n)
                lb = lbc_sb[:, 512 * pp : 512 * (pp + 1)]
                e.vop("tensor_copy", lb, PS[5][:, 0:512], sem=dve)
                e.wait("vector", dve, dve.n)
                e.vop("tensor_tensor", ctx_fm[:, 1024 * h + 512 * tg : 1024 * h + 512 * (tg + 1)],
                      ctx_ps[:], lb, op=ALU.mult, sem=dve)
                ctx_evac[pp] = dve.n
                if tg == 1:
                    att_done.append(pe.n)

            if debug >= 2:
                e.wait("sync", dve, dve.n)
                for i in range(4):
                    e.dma("sync", dbg["ctx"][:, 4096 * i : 4096 * (i + 1)],
                          ctx_fm[:, 4096 * i : 4096 * (i + 1)], sm["dbgs"])

            if upto <= 2:
                return
            e.wait("gpsimd", pe, pe.n)
            for d in range(16):
                e.dma("gpsimd", W_sb[:, 2048 * d : 2048 * (d + 1)],
                      Wo[128 * d : 128 * (d + 1), :], ldc)
            e.wait("tensor", ldc, ldc.n)
            e.wait("tensor", dve, dve.n)

            def rows_pass(scale_div):
                r_mean = rows_sb[0:1, 0:1024]
                r_var = rows_sb[0:1, 1024:2048]
                r_rstd = rows_sb[0:1, 2048:3072]
                r_mrstd = rows_sb[0:1, 1024:2048]
                e.wait("vector", pe, pe.n)
                e.vop("tensor_scalar_mul", r_mean[0:1, 0:512], PL[0][0:1, :], scale_div, sem=dve)
                e.vop("tensor_scalar_mul", r_mean[0:1, 512:1024], PS[4][0:1, 0:512], scale_div, sem=dve)
                e.vop("tensor_scalar_mul", r_var[0:1, 0:512], PL[1][0:1, :], scale_div, sem=dve)
                e.vop("tensor_scalar_mul", r_var[0:1, 512:1024], PS[5][0:1, 0:512], scale_div, sem=dve)
                e.wait("vector", dve, dve.n)
                e.vop("tensor_tensor", r_rstd, r_mean, r_mean, op=ALU.mult, sem=dve)
                e.wait("vector", dve, dve.n)
                e.vop("tensor_sub", r_var, r_var, r_rstd, sem=dve)
                e.wait("scalar", dve, dve.n)
                e.act(r_rstd, r_var, AF.Ln, bias=EPS, sem=acts)
                e.wait("scalar", acts, acts.n)
                e.act(r_rstd, r_rstd, AF.Exp, scale=-0.5, sem=acts)
                e.wait("vector", acts, acts.n)
                e.vop("tensor_tensor", r_mrstd, r_mean, r_rstd, op=ALU.mult, sem=dve)
                e.wait("tensor", dve, dve.n)
                for hh in range(2):
                    if hh >= 1:
                        e.wait("tensor", dve, dve.n)
                    e.mm(PS[4][:, 0:512], onerow_f_sb[:], r_rstd[0:1, 512 * hh : 512 * (hh + 1)],
                         start=True, stop=True, sem=pe)
                    e.mm(PS[5][:, 0:512], onerow_f_sb[:], r_mrstd[0:1, 512 * hh : 512 * (hh + 1)],
                         start=True, stop=True, sem=pe)
                    e.wait("vector", pe, pe.n)
                    e.vop("tensor_copy", bc_sb[:, 1024 * hh : 1024 * hh + 512],
                          PS[4][:, 0:512], sem=dve)
                    e.vop("tensor_copy", bc_sb[:, 1024 * hh + 512 : 1024 * (hh + 1)],
                          PS[5][:, 0:512], sem=dve)

            def stats_chunk(dc, ac):
                e.wait("vector", acts, acts.n)
                e.wait("vector", dve, dve.n)
                e.wait("vector", pe, pe.n)
                for hh in range(2):
                    xb = xbb[:, 512 * hh : 512 * (hh + 1)]
                    sq = sqb[:, 512 * hh : 512 * (hh + 1)]
                    e.vop("tensor_copy", xb, ac[:, 512 * hh : 512 * (hh + 1)], sem=dve)
                    e.wait("vector", dve, dve.n)
                    e.vop("tensor_mul", sq, xb, xb, sem=dve)
                    e.wait("tensor", dve, dve.n)
                    ts_ = PL[0] if hh == 0 else PS[4]
                    tq_ = PL[1] if hh == 0 else PS[5]
                    e.mm(ts_[0:1, 0:512], onecol_f_sb[:], xb, start=(dc == 0), stop=(dc == 15))
                    e.mm(tq_[0:1, 0:512], onecol_f_sb[:], sq, start=(dc == 0), stop=(dc == 15),
                         sem=pe)

            p3_ev = []
            p3_st = []
            for dc in range(16):
                sl = dc % 2
                ps0, ps1 = PS[2 * sl], PS[2 * sl + 1]
                if dc >= 2:
                    e.wait("tensor", acts, p3_ev[dc - 2])
                for h in range(H):
                    lhsT = W_sb[:, 2048 * h + 128 * dc : 2048 * h + 128 * (dc + 1)]
                    e.mm(ps0[:], lhsT, ctx_fm[:, 1024 * h : 1024 * h + 512],
                         start=(h == 0), stop=(h == 15))
                    e.mm(ps1[:], lhsT, ctx_fm[:, 1024 * h + 512 : 1024 * (h + 1)],
                         start=(h == 0), stop=(h == 15), sem=pe if h == 15 else None)
                e.wait("scalar", pe, pe.n)
                ao = aosb_a[:, 0:1024]
                if dc >= 1:
                    e.wait("scalar", st_ao, 16 * dc)
                    e.wait("scalar", dve, dve.n)
                e.act(ao[:, 0:512], ps0[:], AF.Identity, bias=bo_sb[:, dc : dc + 1])
                e.act(ao[:, 512:1024], ps1[:], AF.Identity, bias=bo_sb[:, dc : dc + 1], sem=acts)
                p3_ev.append(acts.n)
                stats_chunk(dc, ao)
                e.wait("sync", acts, p3_ev[-1])
                e.dma("sync", ao_dram[128 * dc : 128 * (dc + 1), :], ao, st_ao)
            rows_pass(1.0 / D)
            p3b_dve = []
            e.wait("sync", st_ao, st_ao.n)
            e.wait("sync", dve, dve.n)
            for dc in range(16):
                ao = aosb_a[:, 0:1024]
                if dc >= 1:
                    e.wait("sync", dve, p3b_dve[dc - 1])
                    e.wait("sync", st_x2, st_x2.n)
                    if debug >= 3:
                        e.wait("sync", sm["dbgs"], sm["dbgs"].n)
                e.dma("sync", ao, ao_dram[128 * dc : 128 * (dc + 1), :], ld)
                for hh in range(2):
                    xo = xst_b[:, 512 * hh : 512 * (hh + 1)]
                    e.dma("sync", xo, xoT[128 * dc : 128 * (dc + 1), 512 * hh : 512 * (hh + 1)], ld)
                e.wait("vector", ld, ld.n)
                e.wait("vector", dve, dve.n)
                for hh in range(2):
                    xo = xst_b[:, 512 * hh : 512 * (hh + 1)]
                    t1 = t1b[:, 512 * hh : 512 * (hh + 1)]
                    e.vop("tensor_tensor", t1, ao[:, 512 * hh : 512 * (hh + 1)],
                          bc_sb[:, 1024 * hh : 1024 * hh + 512], op=ALU.mult, sem=dve)
                    e.wait("vector", dve, dve.n)
                    e.vop("tensor_sub", t1, t1, bc_sb[:, 1024 * hh + 512 : 1024 * (hh + 1)], sem=dve)
                    e.wait("vector", dve, dve.n)
                    e.vop("tensor_add", ao[:, 512 * hh : 512 * (hh + 1)],
                          xo, t1, sem=dve)
                e.wait("sync", dve, dve.n)
                e.dma("sync", x2_dram[128 * dc : 128 * (dc + 1), :], ao, st_x2)
                if debug >= 3:
                    e.dma("sync", dbg["x2"][128 * dc : 128 * (dc + 1), :], ao, sm["dbgs"])
                stats_chunk(dc, ao)
                p3b_dve.append(dve.n)
            rows_pass(1.0 / D)
            p3c = []
            e.wait("sync", st_x2, st_x2.n)
            e.wait("sync", dve, dve.n)
            if debug >= 3:
                e.wait("sync", sm["dbgs"], sm["dbgs"].n)
            for dc in range(16):
                x2c = aosb_a[:, 0:1024]
                if dc >= 1:
                    e.wait("sync", pool, p3c[dc - 1])
                e.dma("sync", x2c, x2_dram[128 * dc : 128 * (dc + 1), :], ld)
                e.wait("vector", ld, ld.n)
                e.wait("vector", dve, dve.n)
                for hh in range(2):
                    t1 = t1b[:, 512 * hh : 512 * (hh + 1)]
                    e.vop("tensor_tensor", t1, x2c[:, 512 * hh : 512 * (hh + 1)],
                          bc_sb[:, 1024 * hh : 1024 * hh + 512], op=ALU.mult, sem=dve)
                    e.wait("gpsimd", dve, dve.n)
                    e.pop("tensor_tensor",
                          ln2T[:, 1024 * dc + 512 * hh : 1024 * dc + 512 * (hh + 1)],
                          t1, bc_sb[:, 1024 * hh + 512 : 1024 * (hh + 1)],
                          op=ALU.subtract, sem=pool)
                p3c.append(pool.n)

            if debug >= 3 or LN2B:
                e.wait("sync", pool, pool.n)
                for i in range(4):
                    e.dma("sync", dbg["ln2"][:, 4096 * i : 4096 * (i + 1)],
                          ln2T[:, 4096 * i : 4096 * (i + 1)], sm["dbgs"])
                e.wait("sync", sm["dbgs"], sm["dbgs"].n)
            if LN2B >= 2:
                for i in range(12):
                    e.dma("sync", dbg["sb1"][:, 4096 * i : 4096 * (i + 1)],
                          sbraw[:][:, 4096 * i : 4096 * (i + 1)], sm["dbgs"])
                e.wait("sync", sm["dbgs"], sm["dbgs"].n)

            if upto <= 3:
                return
            e.wait("gpsimd", pool, pool.n)
            e.wait("gpsimd", pe, pe.n)
            e.wait("vector", sv[0], sv[0].n)
            e.wait("vector", sv[1], sv[1].n)
            e.wait("vector", pe, pe.n)
            if debug >= 1:
                e.wait("vector", sm["dbgs"], sm["dbgs"].n)
            p4_pe = []
            p4_mult = {}
            for g in range(p4_groups):
                gsl = g % 2
                wg = WGU_sb[:, 8192 * (2 * gsl) : 8192 * (2 * gsl + 1)]
                wu = WGU_sb[:, 8192 * (2 * gsl + 1) : 8192 * (2 * gsl + 2)]
                if g >= 2:
                    e.wait("gpsimd", pe, p4_pe[g - 2])
                src_g = WgT[:, 512 * g : 512 * (g + 1)].rearrange("(c p) n -> p c n", p=128)
                src_u = WuT[:, 512 * g : 512 * (g + 1)].rearrange("(c p) n -> p c n", p=128)
                wg_v = wg[:].rearrange("p (c n) -> p c n", c=16)
                wu_v = wu[:].rearrange("p (c n) -> p c n", c=16)
                for q4 in range(4):
                    e.dma("gpsimd", wg_v[:, 4 * q4 : 4 * (q4 + 1), :], src_g[:, 4 * q4 : 4 * (q4 + 1), :], gus[gsl])
                    e.dma("gpsimd", wu_v[:, 4 * q4 : 4 * (q4 + 1), :], src_u[:, 4 * q4 : 4 * (q4 + 1), :], gus[gsl])
                e.wait("tensor", gus[gsl], gus[gsl].n)
                for sub in range(4):
                    for half in range(2):
                        i = 8 * g + 2 * sub + half
                        pp4 = i % 2
                        psg = PS[4] if pp4 == 0 else PS[2]
                        psu = PS[5] if pp4 == 0 else PS[3]
                        if pp4 in p4_mult:
                            e.wait("tensor", dve, p4_mult[pp4])
                        for d in range(16):
                            rhs = ln2T[:, 1024 * d + 512 * half : 1024 * d + 512 * (half + 1)]
                            e.mm(psg[:], wg[:, 512 * d + 128 * sub : 512 * d + 128 * (sub + 1)],
                                 rhs, start=(d == 0), stop=(d == 15), sem=pe if d == 15 else None)
                        for d in range(16):
                            rhs = ln2T[:, 1024 * d + 512 * half : 1024 * d + 512 * (half + 1)]
                            e.mm(psu[:], wu[:, 512 * d + 128 * sub : 512 * d + 128 * (sub + 1)],
                                 rhs, start=(d == 0), stop=(d == 15), sem=pe if d == 15 else None)
                        e.wait("scalar", pe, pe.n)
                        sgs = sg_sb[:, 512 * (i % 2) : 512 * (i % 2 + 1)]
                        e.act(sgs, psg[:], AF.Silu, sem=acts)
                        e.wait("vector", acts, acts.n)
                        hc_g = 4 * g + sub
                        if half == 0:
                            dst = gated_lo if hc_g < 32 else gated_hi
                            dcol = 512 * (hc_g % 32)
                            e.vop("tensor_tensor", dst[:, dcol : dcol + 512], sgs, psu[:],
                                  op=ALU.mult, sem=dve)
                        else:
                            stg = vst[:, 1024 * (hc_g % 2) : 1024 * (hc_g % 2) + 512]
                            e.wait("vector", st_gb, st_gb.n)
                            e.vop("tensor_tensor", stg, sgs, psu[:], op=ALU.mult, sem=dve)
                            e.wait("sync", dve, dve.n)
                            e.dma("sync", gB_dram[128 * hc_g : 128 * (hc_g + 1), :], stg, st_gb)
                        p4_mult[pp4] = dve.n
                p4_pe.append(pe.n)

            if upto <= 4:
                return
            e.wait("gpsimd", pe, pe.n)
            e.wait("sync", pe, pe.n)
            e.wait("sync", st_gb, st_gb.n)
            if LN2B:
                for i in range(4):
                    e.dma("sync", dbg["ln2b"][:, 4096 * i : 4096 * (i + 1)],
                          ln2T[:, 4096 * i : 4096 * (i + 1)], sm["dbgs"])
                e.wait("sync", sm["dbgs"], sm["dbgs"].n)
            if LN2B >= 2:
                for i in range(12):
                    e.dma("sync", dbg["sb2"][:, 4096 * i : 4096 * (i + 1)],
                          sbraw[:][:, 4096 * i : 4096 * (i + 1)], sm["dbgs"])
                e.wait("sync", sm["dbgs"], sm["dbgs"].n)
            p5_pe = []
            p5_copy = {}
            for sc in range(8):
                wsl = sc % 2
                wd = WD_sb[:, 16384 * wsl : 16384 * (wsl + 1)]
                if sc >= 2:
                    e.wait("gpsimd", pe, p5_pe[sc - 2])
                for hc in range(8):
                    e.dma("gpsimd", wd[:, 2048 * hc : 2048 * (hc + 1)],
                          WdT[1024 * sc + 128 * hc : 1024 * sc + 128 * (hc + 1), :], wds[wsl])
                e.wait("tensor", wds[wsl], wds[wsl].n)
                gsl = sc % 2
                gbuf = gbuf2[:, 4096 * gsl : 4096 * (gsl + 1)]
                if sc >= 2:
                    e.wait("sync", pe, p5_pe[sc - 2])
                srcg = gB_dram[1024 * sc : 1024 * (sc + 1), :].rearrange("(c p) n -> p c n", p=128)
                e.dma("sync", gbuf[:].rearrange("p (c n) -> p c n", c=8), srcg, gbs[gsl])
                for half in range(2):
                    if half == 1:
                        e.wait("tensor", gbs[gsl], gbs[gsl].n)
                    for dq in range(4):
                        mm_pe = {}
                        for dc4 in range(4):
                            dci = 4 * dq + dc4
                            ps = PS[dc4]
                            if dc4 in p5_copy:
                                e.wait("tensor", dve, p5_copy[dc4])
                            for hc in range(8):
                                hc_g = 8 * sc + hc
                                if half == 0:
                                    gsrc = gated_lo if hc_g < 32 else gated_hi
                                    rhs = gsrc[:, 512 * (hc_g % 32) : 512 * (hc_g % 32 + 1)]
                                else:
                                    rhs = gbuf[:, 512 * hc : 512 * (hc + 1)]
                                e.mm(ps[:], wd[:, 2048 * hc + 128 * dci : 2048 * hc + 128 * (dci + 1)],
                                     rhs, start=(hc == 0), stop=(hc == 7),
                                     sem=pe if hc == 7 else None)
                            mm_pe[dc4] = pe.n
                        for dc4 in range(4):
                            dci = 4 * dq + dc4
                            stg = st32[:, 512 * (dc4 % 2) : 512 * (dc4 % 2 + 1)]
                            e.wait("vector", pe, mm_pe[dc4])
                            e.wait("vector", st_acc, st_acc.n)
                            e.vop("tensor_copy", stg, PS[dc4][:], sem=dve)
                            p5_copy[dc4] = dve.n
                            e.wait("gpsimd", dve, dve.n)
                            dst = acc_dram[128 * dci : 128 * (dci + 1), 512 * half : 512 * (half + 1)]
                            e.dma("gpsimd", dst, stg, st_acc, acc=(sc > 0))
                p5_pe.append(pe.n)

            if p5_mode <= 1:
                return
            e.wait("sync", st_acc, st_acc.n)
            e.wait("sync", dve, dve.n)
            for dc in range(16):
                ac = aosb_a[:, 0:1024]
                if dc >= 1:
                    e.wait("sync", dve, dve.n)
                e.dma("sync", ac, acc_dram[128 * dc : 128 * (dc + 1), :], ld)
                e.wait("vector", ld, ld.n)
                stats_chunk(dc, ac)
            rows_pass(1.0 / D)
            if p5_mode <= 2:
                return
            p5b = []
            e.wait("sync", st_x2, st_x2.n)
            e.wait("sync", dve, dve.n)
            for dc in range(16):
                ac = aosb_a[:, 0:1024]
                x2c = aosb_b[:, 0:1024]
                if dc >= 1:
                    e.wait("sync", st_y, 16 * dc)
                    e.wait("sync", dve, dve.n)
                e.dma("sync", ac, acc_dram[128 * dc : 128 * (dc + 1), :], ld)
                e.dma("sync", x2c, x2_dram[128 * dc : 128 * (dc + 1), :], ld)
                e.wait("vector", ld, ld.n)
                e.wait("vector", dve, dve.n)
                for hh in range(2):
                    t1 = t1b[:, 512 * hh : 512 * (hh + 1)]
                    e.vop("tensor_tensor", t1, ac[:, 512 * hh : 512 * (hh + 1)],
                          bc_sb[:, 1024 * hh : 1024 * hh + 512], op=ALU.mult, sem=dve)
                    e.wait("vector", dve, dve.n)
                    e.vop("tensor_sub", t1, t1, bc_sb[:, 1024 * hh + 512 : 1024 * (hh + 1)], sem=dve)
                    e.wait("vector", dve, dve.n)
                    e.vop("tensor_add", ac[:, 512 * hh : 512 * (hh + 1)],
                          x2c[:, 512 * hh : 512 * (hh + 1)], t1, sem=dve)
                e.wait("sync", dve, dve.n)
                e.dma("sync", y[128 * dc : 128 * (dc + 1), :], ac, st_y)
                p5b.append(st_y.n)


        @blk.sync
        def _(eng):
            program(E("sync", eng))

        @blk.gpsimd
        def _(eng):
            program(E("gpsimd", eng))

        @blk.tensor
        def _(eng):
            program(E("tensor", eng))

        @blk.scalar
        def _(eng):
            program(E("scalar", eng))

        @blk.vector
        def _(eng):
            program(E("vector", eng))

    return nc


def make_inputs(c, x, Wq, bq, Wk, bk, Wv, bv, Wo, bo, w_gate, w_up, w_down):
    xf = np.ascontiguousarray(x.reshape(S, D))
    rows = np.concatenate([np.arange(1024 * t + 128 * c, 1024 * t + 128 * (c + 1))
                           for t in range(NT)])
    btab_arr, _ = build_bias_table(c)
    return {
        "xT": np.ascontiguousarray(xf.T),
        "xoT": np.ascontiguousarray(xf[rows].T),
        "Wq": np.ascontiguousarray(Wq), "Wk": np.ascontiguousarray(Wk),
        "Wv": np.ascontiguousarray(Wv), "Wo": np.ascontiguousarray(Wo),
        "bq": np.ascontiguousarray(bq.reshape(H, 128).T),
        "bk": np.ascontiguousarray(bk.reshape(H, 128).T),
        "bo": np.ascontiguousarray(bo.reshape(H, 128).T),
        "bvr": bv.reshape(1, D).astype(ml_dtypes.bfloat16),
        "WgT": np.ascontiguousarray(w_gate.T),
        "WuT": np.ascontiguousarray(w_up.T),
        "WdT": np.ascontiguousarray(w_down.T),
        "btab": btab_arr,
        "masks": build_masks(c),
        "identb": np.eye(128, dtype=ml_dtypes.bfloat16),
        "onecol": np.ones((128, 1), dtype=ml_dtypes.bfloat16),
        "onerow_b": np.ones((1, 128), dtype=ml_dtypes.bfloat16),
        "onecol_f": np.ones((128, 1), dtype=ml_dtypes.bfloat16),
        "onerow_f": np.ones((1, 128), dtype=np.float32),
        "zerob": np.zeros((128, 128), dtype=ml_dtypes.bfloat16),
        "fconst": np.array([[EPS, 0.0, 1.0, 0, 0, 0, 0, 0]], dtype=np.float32),
        "fcb": np.tile(np.array([[0.0, 1.0]], dtype=np.float32), (128, 1)),
        "fcu": np.full((128, 4), 127, dtype=np.uint8),
    }


def run(inputs, debug=0, upto=99, p4_groups=16, p5_mode=3):
    args = [np.asarray(inputs[k], dtype=np.float32) for k in
            ("L0_attn_x", "L0_attn_Wq", "L0_attn_bq", "L0_attn_Wk", "L0_attn_bk",
             "L0_attn_Wv", "L0_attn_bv", "L0_attn_Wo", "L0_attn_bo",
             "L0_ffn_w_gate", "L0_ffn_w_up", "L0_ffn_w_down")]
    nc = build_nc(debug=debug, upto=upto, p4_groups=p4_groups, p5_mode=p5_mode)
    in_maps = [make_inputs(c, args[0], args[1], args[2], args[3], args[4], args[5],
                           args[6], args[7], args[8], args[9], args[10], args[11])
               for c in range(NC)]
    res = run_bass_kernel_spmd(nc, in_maps, core_ids=list(range(NC)))
    out = np.zeros((S, D), dtype=np.float32)
    for c in range(NC):
        yc = np.asarray(res.results[c]["y"]).T
        for t in range(NT):
            out[1024 * t + 128 * c : 1024 * t + 128 * (c + 1)] = yc[128 * t : 128 * (t + 1)]
    return out.reshape(1, S, D), res


def kernel(**inputs):
    out, _ = run(inputs, debug=0)
    return out


# version 12
# speedup vs baseline: 1.1118x; 1.0002x over previous
import numpy as np
import ml_dtypes
import concourse.bass as bass
import concourse.mybir as mybir
from concourse.bass_utils import run_bass_kernel_spmd
from contextlib import ExitStack

F32 = mybir.dt.float32
BF16 = mybir.dt.bfloat16
AF = mybir.ActivationFunctionType
ALU = mybir.AluOpType

S = 8192
D = 2048
H = 16
HD = 128
NC = 8
ROWS = S // NC
NT = 8
DH = 8192
EPS = 1e-5
SCALE = 1.0 / np.sqrt(HD)
T_BAND = 150.0
NOQFDUMP = 0
LN2B = 0
SBMAP = {}
NEG = -3.0e38
MASKNEG = -1.0e30

SLOPES = [2.0 ** (-0.5 * (h + 1)) for h in range(H)]
GSIZE = [1, 1, 1, 1, 1, 1, 1, 1, 1, 2, 2, 4, 4, 4, 4, 4]
HALVED = [True, True] + [False] * 14


def keep_subtile(h, t_near, b, s):
    dist = 1024 * (t_near - b) - 128 * s - 127
    return SLOPES[h] * max(0, dist) <= T_BAND


def attention_schedule():
    out = []
    for h in range(H):
        G = GSIZE[h]
        for tg in range(2):
            items = []
            for qg in range(4 // G):
                t_lo = tg * 4 + qg * G
                t_hi = t_lo + G - 1
                for b in range(t_lo):
                    for s in range(NC):
                        if not keep_subtile(h, t_lo, b, s):
                            continue
                        if HALVED[h]:
                            for hf in range(2):
                                items.append((b, s, ("n", h, t_hi - b, s, hf),
                                              128 * t_lo + 64 * hf, 64,
                                              128 * (t_lo - 4 * tg) + 64 * hf, None))
                        else:
                            items.append((b, s, ("n", h, t_hi - b, s, 0),
                                          128 * t_lo, 128 * G, 128 * (t_lo - 4 * tg), None))
                for t in range(t_lo, t_hi + 1):
                    for b in range(t_lo, t + 1):
                        diag = (b == t)
                        for s in range(NC):
                            if not keep_subtile(h, t, b, s):
                                continue
                            halves = (0, 1) if HALVED[h] else (0,)
                            for hf in halves:
                                n = 64 if HALVED[h] else 128
                                key = (("d", h, t_hi - t, s, hf) if diag
                                       else ("n", h, t_hi - b, s, hf))
                                mask = (128 * s + 64 * hf) if diag else None
                                items.append((b, s, key,
                                              128 * t + 64 * hf, n,
                                              128 * (t - 4 * tg) + 64 * hf, mask))
            out.append((h, tg, items))
    return out


SCHED = attention_schedule()


def build_bias_table(c):
    keys = []
    seen = set()
    for (_h, _tg, items) in SCHED:
        for (_b, _s, key, _qc, _n, _cc, _m) in items:
            if key not in seen:
                seen.add(key)
                keys.append(key)
    p = np.arange(128, dtype=np.float64)
    cols = {}
    data = []
    for key in keys:
        kind, h, dlt, s, hf = key
        sl = SLOPES[h]
        if kind == "d" and s > c:
            v = np.full(128, NEG)
        else:
            v = sl * (-1024.0 * dlt + 128.0 * (s - c) - 64.0 * hf + p)
        cols[key] = len(data)
        data.append(v)
    a = np.stack(data, axis=1).astype(np.float32)
    ncol = int(np.ceil(a.shape[1] / 16.0) * 16)
    arr = np.zeros((128, max(ncol, 16)), dtype=np.float32)
    arr[:, : a.shape[1]] = a
    return arr, cols


BT_ARR0, COLS = build_bias_table(0)
NCOL = BT_ARR0.shape[1]


def build_masks(c):
    m = np.zeros((128, NC * 128), dtype=np.float32)
    tri = np.where(np.arange(128)[:, None] <= np.arange(128)[None, :], 0.0, MASKNEG)
    m[:, 128 * c : 128 * (c + 1)] = tri
    return m.astype(ml_dtypes.bfloat16)


class Sem:
    __slots__ = ("h", "n")

    def __init__(self, h):
        self.h = h
        self.n = 0


class E:

    def __init__(self, which, eng):
        self.w = which
        self.e = eng

    def _fin(self, inst, sem, k):
        if sem is not None:
            if inst is not None:
                inst.then_inc(sem.h, k)
            sem.n += k
        return inst

    def dma(self, q, out, in_, sem=None, k=16, acc=False):
        i = None
        if self.w == q:
            if acc:
                i = self.e.dma_start(out=out, in_=in_, accum_op=ALU.add)
            else:
                i = self.e.dma_start(out=out, in_=in_)
        return self._fin(i, sem, k)

    def wait(self, q, sem, val=None):
        v = sem.n if val is None else val
        if self.w == q and v > 0:
            self.e.wait_ge(sem.h, v)

    def mm(self, out, lhsT, rhs, start, stop, sem=None, k=1):
        i = None
        if self.w == "tensor":
            i = self.e.matmul(out, lhsT, rhs, start=start, stop=stop)
        return self._fin(i, sem, k)

    def act(self, out, in_, func, bias=0.0, scale=1.0, sem=None, k=1):
        i = None
        if self.w == "scalar":
            i = self.e.activation(out, in_, func, bias=bias, scale=scale)
        return self._fin(i, sem, k)

    def act_table(self, func):
        if self.w == "scalar":
            self.e.preload_act_table(func)

    def vop(self, fn, *args, sem=None, k=1, **kw):
        i = None
        if self.w == "vector":
            i = getattr(self.e, fn)(*args, **kw)
        return self._fin(i, sem, k)

    def pop(self, fn, *args, sem=None, k=1, **kw):
        i = None
        if self.w == "gpsimd":
            i = getattr(self.e, fn)(*args, **kw)
        return self._fin(i, sem, k)


def build_nc(debug=0, upto=99, p4_groups=16, p5_mode=3):
    nc = bass.Bass("TRN2", target_bir_lowering=False, dynamic_dma_scratch_size=4096)

    ap = lambda t: t.ap()
    xT = ap(nc.dram_tensor("xT", [D, S], F32, kind="ExternalInput"))
    xoT = ap(nc.dram_tensor("xoT", [D, ROWS], F32, kind="ExternalInput"))
    Wq = ap(nc.dram_tensor("Wq", [D, D], F32, kind="ExternalInput"))
    Wk = ap(nc.dram_tensor("Wk", [D, D], F32, kind="ExternalInput"))
    Wv = ap(nc.dram_tensor("Wv", [D, D], F32, kind="ExternalInput"))
    Wo = ap(nc.dram_tensor("Wo", [D, D], F32, kind="ExternalInput"))
    bq = ap(nc.dram_tensor("bq", [128, H], F32, kind="ExternalInput"))
    bk = ap(nc.dram_tensor("bk", [128, H], F32, kind="ExternalInput"))
    bo = ap(nc.dram_tensor("bo", [128, H], F32, kind="ExternalInput"))
    bvr = ap(nc.dram_tensor("bvr", [1, D], BF16, kind="ExternalInput"))
    WgT = ap(nc.dram_tensor("WgT", [D, DH], F32, kind="ExternalInput"))
    WuT = ap(nc.dram_tensor("WuT", [D, DH], F32, kind="ExternalInput"))
    WdT = ap(nc.dram_tensor("WdT", [DH, D], F32, kind="ExternalInput"))
    btab = ap(nc.dram_tensor("btab", [128, NCOL], F32, kind="ExternalInput"))
    masks = ap(nc.dram_tensor("masks", [128, NC * 128], BF16, kind="ExternalInput"))
    identb = ap(nc.dram_tensor("identb", [128, 128], BF16, kind="ExternalInput"))
    onecol = ap(nc.dram_tensor("onecol", [128, 1], BF16, kind="ExternalInput"))
    onerow_b = ap(nc.dram_tensor("onerow_b", [1, 128], BF16, kind="ExternalInput"))
    onecol_f = ap(nc.dram_tensor("onecol_f", [128, 1], BF16, kind="ExternalInput"))
    onerow_f = ap(nc.dram_tensor("onerow_f", [1, 128], F32, kind="ExternalInput"))
    zerob = ap(nc.dram_tensor("zerob", [128, 128], BF16, kind="ExternalInput"))
    fconst = ap(nc.dram_tensor("fconst", [1, 8], F32, kind="ExternalInput"))
    fcb = ap(nc.dram_tensor("fcb", [128, 2], F32, kind="ExternalInput"))
    fcu = ap(nc.dram_tensor("fcu", [128, 4], mybir.dt.uint8, kind="ExternalInput"))

    y = ap(nc.dram_tensor("y", [D, ROWS], F32, kind="ExternalOutput"))

    ikind = "ExternalOutput" if debug >= 1 else "Internal"
    ln_dram = ap(nc.dram_tensor("ln_dram", [D, S], BF16, kind=ikind))
    K_g = ap(nc.dram_tensor("K_g", [D, S], BF16, kind=ikind))
    V_g = ap(nc.dram_tensor("V_g", [S, D], BF16, kind=ikind))
    ao_dram = ap(nc.dram_tensor("ao_dram", [D, ROWS], F32, kind=ikind))
    x2_dram = ap(nc.dram_tensor("x2_dram", [D, ROWS], F32, kind=ikind))
    gB_dram = ap(nc.dram_tensor("gB_dram", [DH, 512], BF16, kind=ikind))
    acc_dram = ap(nc.dram_tensor("acc_dram", [D, ROWS], F32, kind=ikind))

    pad_sink = ap(nc.dram_tensor("pad_sink", [1, 64], mybir.dt.uint8, kind="Internal"))

    dbg = {}
    if debug >= 1:
        dbg["Qf"] = ap(nc.dram_tensor("dbg_Qf", [128, H * 1024], BF16, kind="ExternalOutput"))
    if debug >= 2:
        dbg["ctx"] = ap(nc.dram_tensor("dbg_ctx", [128, H * 1024], BF16, kind="ExternalOutput"))
    if debug >= 3:
        dbg["x2"] = ap(nc.dram_tensor("dbg_x2", [D, ROWS], F32, kind="ExternalOutput"))
    if debug >= 3 or LN2B:
        dbg["ln2"] = ap(nc.dram_tensor("dbg_ln2", [128, 16 * 1024], BF16, kind="ExternalOutput"))
    if LN2B:
        dbg["ln2b"] = ap(nc.dram_tensor("dbg_ln2b", [128, 16 * 1024], BF16, kind="ExternalOutput"))
    if LN2B >= 2:
        U8 = mybir.dt.uint8
        dbg["sb1"] = ap(nc.dram_tensor("dbg_sb1", [128, 49152], U8, kind="ExternalOutput"))
        dbg["sb2"] = ap(nc.dram_tensor("dbg_sb2", [128, 49152], U8, kind="ExternalOutput"))
    if debug >= 5:
        dbg["rows"] = ap(nc.dram_tensor("dbg_rows", [1, 3072], F32, kind="ExternalOutput"))
        dbg["bc"] = ap(nc.dram_tensor("dbg_bc", [128, 2048], F32, kind="ExternalOutput"))
        dbg["xbb"] = ap(nc.dram_tensor("dbg_xbb", [128, 1024], BF16, kind="ExternalOutput"))
        dbg["sqb"] = ap(nc.dram_tensor("dbg_sqb", [128, 1024], BF16, kind="ExternalOutput"))
        dbg["pl"] = ap(nc.dram_tensor("dbg_pl", [1, 8192], F32, kind="ExternalOutput"))

    with ExitStack() as stk:
        EE = stk.enter_context

        from math import prod
        ring_pad = nc.alloc_sbuf_tensor_at("ring_pad", [128, 12288], mybir.dt.uint8, offset=4096)
        _cur = [16384]
        sbraw = nc.alloc_sbuf_tensor_at("sbraw", [128, 49152], mybir.dt.uint8, offset=4096) if LN2B >= 2 else None

        def region(tensors):
            base = _cur[0]
            out = []
            end = base
            for name, shape, dt in tensors:
                out.append(nc.alloc_sbuf_tensor_at(name, shape, dt, offset=base))
                SBMAP[name] = (base, base + prod(shape[1:]) * mybir.dt.size(dt))
                end = max(end, base + prod(shape[1:]) * mybir.dt.size(dt))
            _cur[0] = (end + 31) // 32 * 32
            assert _cur[0] <= nc.sbuf_top, f"SBUF overflow: {_cur[0]} > {nc.sbuf_top}"
            return out if len(out) > 1 else out[0]

        btab_sb = region([("btab_sb", [128, NCOL], F32)])
        masks_sb = region([("masks_sb", [128, NC * 128], BF16)])
        identb_sb = region([("identb_sb", [128, 128], BF16)])
        onecol_sb = region([("onecol_sb", [128, 1], BF16)])
        onerow_b_sb = region([("onerow_b_sb", [1, 128], BF16)])
        onecol_f_sb = region([("onecol_f_sb", [128, 1], BF16)])
        onerow_f_sb = region([("onerow_f_sb", [1, 128], F32)])
        zerob_sb = region([("zerob_sb", [128, 128], BF16)])
        bq_sb = region([("bq_sb", [128, H], F32)])
        bk_sb = region([("bk_sb", [128, H], F32)])
        bo_sb = region([("bo_sb", [128, H], F32)])
        fc_sb = region([("fc_sb", [1, 8], F32)])
        fcb_sb = region([("fcb_sb", [128, 2], F32)])
        fcu_sb = region([("fcu_sb", [128, 4], mybir.dt.uint8)])

        LNO, ln_sl2, ln2T, gbuf2 = region([
            ("LNO", [128, 16 * 1024], BF16),
            ("ln_sl2", [128, 2 * 8192], BF16),
            ("ln2T", [128, 16 * 1024], BF16),
            ("gbuf2", [128, 2 * 4096], BF16),
        ])
        Qf, gated_lo = region([
            ("Qf", [128, H * 1024], BF16),
            ("gated_lo", [128, 32 * 512], BF16),
        ])
        ctx_fm, gated_hi = region([
            ("ctx_fm", [128, H * 1024], BF16),
            ("gated_hi", [128, 32 * 512], BF16),
        ])
        W_sb, KV_sb, WGU_sb, WD_sb = region([
            ("W_sb", [128, 16 * 2048], BF16),
            ("KV_sb", [128, 2 * 16384], BF16),
            ("WGU_sb", [128, 4 * 8192], BF16),
            ("WD_sb", [128, 2 * 16384], BF16),
        ])
        xst_b = region([("xst_b", [128, 2 * 512], F32)])
        xnm_b = region([("xnm_b", [128, 2 * 512], F32)])
        sqb = region([("sqb", [128, 2 * 512], BF16)])
        xbb = region([("xbb", [128, 2 * 512], BF16)])
        kst, vst, ptil = region([
            ("kst", [128, 2 * 512], BF16),
            ("vst", [128, 2 * 1024], BF16),
            ("ptil", [128, 3 * 512], BF16),
        ])
        aosb_a = nc.alloc_sbuf_tensor_at("aosb_a", [128, 1024], F32, offset=SBMAP["xnm_b"][0])
        xo2_sb = nc.alloc_sbuf_tensor_at("xo2_sb", [128, 1024], F32, offset=SBMAP["Qf"][0])
        aosb_b = nc.alloc_sbuf_tensor_at("aosb_b", [128, 1024], F32, offset=SBMAP["kst"][0])
        bvr_sb = nc.alloc_sbuf_tensor_at("bvr_sb2", [1, D], BF16, offset=SBMAP["ctx_fm"][0])
        rows_sb = region([("rows_sb", [1, 3072], F32)])
        linv_sb = rows_sb
        bc_sb = region([("bc_sb", [128, 4 * 512], F32)])
        t1b, lbc_sb, sg_sb, st32 = region([
            ("t1b", [128, 2 * 512], F32),
            ("lbc_sb", [128, 2 * 512], F32),
            ("sg_sb", [128, 2 * 512], F32),
            ("st32", [128, 2 * 512], F32),
        ])

        PS = [EE(nc.psum_tensor(f"PS{i}", [128, 512], F32)) for i in range(6)]
        PL = [EE(nc.psum_tensor(f"PL{i}", [1, 512], F32)) for i in range(2)]

        nc.const_aps.register(F32, EPS, fc_sb[0:1, 0:1])
        nc.const_aps.register(F32, 0.0, fcb_sb[:, 0:1])
        nc.const_aps.register(F32, 1.0, fcb_sb[:, 1:2])
        nc.const_aps.register(BF16, 1.0, onecol_sb[:, 0:1])
        nc.const_aps.register(mybir.dt.uint8, 127, fcu_sb[:, 0:1])
        sem_names = ["ld", "ldc", "pe", "acts", "dve", "pool", "sto", "pe2", "acte",
                     "la0", "la1", "ln0", "ln1", "lb0", "lb1", "kv0", "kv1",
                     "gu0", "gu1", "wd0", "wd1", "gb0", "gb1",
                     "sk0", "sk1", "sv0", "sv1", "sl0", "sl1",
                     "st_ao", "st_x2", "st_acc", "st_gb", "st_kv", "st_y", "dbgs"]
        sem_h = {n: EE(nc.semaphore(n)) for n in sem_names}

        blk = EE(nc.Block())
        dbg_state = {}

        def ln_fm(e, sm, dram_col0, dram_src, n, out, war_sem=None, war_val=0):
            ld = sm["ld"]; pe = sm["pe"]; acts = sm["acts"]; dve = sm["dve"]; pool = sm["pool"]
            la = (sm["la0"], sm["la1"])
            lnS = (sm["ln0"], sm["ln1"])
            nh = n // 512
            sums = [(PL[0], PL[1]), (PS[4], PS[5])]
            cnt = 0
            war = []
            for g in range(16):
                for hh in range(nh):
                    sl = cnt % 2
                    xc = xst_b[:, 512 * sl : 512 * (sl + 1)]
                    xb = xbb[:, 512 * sl : 512 * (sl + 1)]
                    sq = sqb[:, 512 * sl : 512 * (sl + 1)]
                    if cnt >= 2:
                        e.wait("sync", pe, war[cnt - 2])
                    e.dma("sync", xc, dram_src[128 * g : 128 * (g + 1),
                                              dram_col0 + 512 * hh : dram_col0 + 512 * (hh + 1)], la[sl])
                    e.wait("scalar", la[sl], la[sl].n)
                    if cnt >= 2:
                        e.wait("scalar", pe, war[cnt - 2])
                    e.act(xb, xc, AF.Copy, sem=acts)
                    e.wait("vector", acts, acts.n)
                    e.vop("tensor_mul", sq, xb, xb, sem=dve)
                    e.wait("tensor", dve, dve.n)
                    ts_, tq_ = sums[hh]
                    e.mm(ts_[0:1, 0:512], onecol_f_sb[:], xb, start=(g == 0), stop=(g == 15))
                    e.mm(tq_[0:1, 0:512], onecol_f_sb[:], sq, start=(g == 0), stop=(g == 15),
                         sem=pe)
                    war.append(pe.n)
                    cnt += 1
            if debug >= 5 and not dbg_state.get("pl_dumped"):
                dbg_state["pl_dumped"] = True
                e.wait("vector", pe, pe.n)
                e.vop("tensor_copy", t1b[:, 0:512][0:1, :], PL[0][0:1, :], sem=dve)
                e.vop("tensor_copy", t1b[:, 512:1024][0:1, :], PL[1][0:1, :], sem=dve)
                e.wait("sync", dve, dve.n)
                e.dma("sync", dbg["pl"][0:1, 0:1024], t1b[0:1, 0:1024], sm["dbgs"])
            r_mean = rows_sb[0:1, 0:n]
            r_var = rows_sb[0:1, 1024 : 1024 + n]
            r_rstd = rows_sb[0:1, 2048 : 2048 + n]
            e.wait("vector", pe, pe.n)
            for hh in range(nh):
                ts_, tq_ = sums[hh]
                e.vop("tensor_scalar_mul", r_mean[0:1, 512 * hh : 512 * (hh + 1)],
                      ts_[0:1, 0:512], 1.0 / D, sem=dve)
                e.vop("tensor_scalar_mul", r_var[0:1, 512 * hh : 512 * (hh + 1)],
                      tq_[0:1, 0:512], 1.0 / D, sem=dve)
            dbgrows = debug >= 5 and not dbg_state.get("rows_dumped")
            def snap(i):
                if dbgrows:
                    e.wait("sync", dve, dve.n)
                    e.wait("sync", acts, acts.n)
                    e.dma("sync", dbg["pl"][0:1, 1024 * i : 1024 * i + 512],
                          rows_sb[0:1, 1024:1536], sm["dbgs"])
                    e.dma("sync", dbg["pl"][0:1, 1024 * i + 512 : 1024 * (i + 1)],
                          rows_sb[0:1, 2048:2560], sm["dbgs"])
                    e.wait("vector", sm["dbgs"], sm["dbgs"].n)
                    e.wait("scalar", sm["dbgs"], sm["dbgs"].n)
            e.wait("vector", dve, dve.n)
            snap(1)
            e.vop("tensor_tensor", r_rstd, r_mean, r_mean, op=ALU.mult, sem=dve)
            e.wait("vector", dve, dve.n)
            snap(2)
            e.vop("tensor_sub", r_var, r_var, r_rstd, sem=dve)
            e.wait("scalar", dve, dve.n)
            snap(3)
            e.act(r_rstd, r_var, AF.Ln, bias=EPS, sem=acts)
            e.wait("scalar", acts, acts.n)
            snap(4)
            e.act(r_rstd, r_rstd, AF.Exp, scale=-0.5, sem=acts)
            e.wait("vector", acts, acts.n)
            snap(5)
            if dbgrows:
                dbg_state["rows_dumped"] = True
            e.vop("tensor_tensor", r_var, r_mean, r_rstd, op=ALU.mult, sem=dve)
            e.wait("tensor", dve, dve.n)
            for hh in range(nh):
                if hh >= 1:
                    e.wait("tensor", dve, dve.n)
                e.mm(PS[4][:, 0:512], onerow_f_sb[:], r_rstd[0:1, 512 * hh : 512 * (hh + 1)],
                     start=True, stop=True, sem=pe)
                e.mm(PS[5][:, 0:512], onerow_f_sb[:], r_var[0:1, 512 * hh : 512 * (hh + 1)],
                     start=True, stop=True, sem=pe)
                e.wait("vector", pe, pe.n)
                e.vop("tensor_copy", bc_sb[:, 1024 * hh : 1024 * hh + 512], PS[4][:, 0:512], sem=dve)
                e.vop("tensor_copy", bc_sb[:, 1024 * hh + 512 : 1024 * (hh + 1)], PS[5][:, 0:512],
                      sem=dve)
            if war_sem is not None:
                e.wait("gpsimd", war_sem, war_val)
            cnt = 0
            dwar = []
            pwar = []
            for g in range(16):
                for hh in range(nh):
                    sl = cnt % 2
                    xc = xnm_b[:, 512 * sl : 512 * (sl + 1)]
                    t1 = t1b[:, 512 * sl : 512 * (sl + 1)]
                    if cnt >= 2:
                        e.wait("sync", dve, dwar[cnt - 2])
                    e.dma("sync", xc, dram_src[128 * g : 128 * (g + 1),
                                              dram_col0 + 512 * hh : dram_col0 + 512 * (hh + 1)], lnS[sl])
                    if cnt >= 2:
                        e.wait("vector", pool, pwar[cnt - 2])
                    e.wait("vector", lnS[sl], lnS[sl].n)
                    e.wait("vector", dve, dve.n)
                    e.vop("tensor_tensor", t1, xc,
                          bc_sb[:, 1024 * hh : 1024 * hh + 512], op=ALU.mult, sem=dve)
                    dwar.append(dve.n)
                    e.wait("gpsimd", dve, dve.n)
                    e.pop("tensor_tensor", out(g)[:, 512 * hh : 512 * (hh + 1)],
                          t1, bc_sb[:, 1024 * hh + 512 : 1024 * (hh + 1)],
                          op=ALU.subtract, sem=pool)
                    pwar.append(pool.n)
                    cnt += 1

        def program(e):
            dbg_state.clear()
            sm = {n: Sem(h) for n, h in sem_h.items()}
            phases(e, sm)
            for nm in ("st_y", "dbgs", "sto", "st_ao", "st_x2", "st_acc", "st_gb",
                       "sk0", "sk1", "sv0", "sv1", "sl0", "sl1"):
                e.wait("sync", sm[nm])
            e.wait("gpsimd", sm["st_y"])

        def phases(e, sm):
            ld = sm["ld"]; ldc = sm["ldc"]; pe = sm["pe"]; acts = sm["acts"]
            dve = sm["dve"]; pool = sm["pool"]; sto = sm["sto"]
            pe2 = sm["pe2"]; acte = sm["acte"]
            sk = (sm["sk0"], sm["sk1"]); sv = (sm["sv0"], sm["sv1"])
            slst = (sm["sl0"], sm["sl1"]); kvs = (sm["kv0"], sm["kv1"])
            gus = (sm["gu0"], sm["gu1"]); wds = (sm["wd0"], sm["wd1"])
            gbs = (sm["gb0"], sm["gb1"])
            st_ao = sm["st_ao"]; st_x2 = sm["st_x2"]; st_acc = sm["st_acc"]
            st_gb = sm["st_gb"]; st_kv = sm["st_kv"]; st_y = sm["st_y"]

            for dst, src in [(fc_sb, fconst), (fcb_sb, fcb), (fcu_sb, fcu), (btab_sb, btab), (masks_sb, masks), (identb_sb, identb),
                             (onecol_sb, onecol), (onerow_b_sb, onerow_b),
                             (onecol_f_sb, onecol_f), (onerow_f_sb, onerow_f),
                             (zerob_sb, zerob), (bq_sb, bq), (bk_sb, bk),
                             (bo_sb, bo), (bvr_sb, bvr)]:
                e.dma("sync", dst[:], src, ld)
            for q in ("sync", "gpsimd", "tensor", "scalar", "vector"):
                e.wait(q, ld)

            ln_fm(e, sm, 0, xoT, 1024,
                  out=lambda g: LNO[:, 1024 * g : 1024 * (g + 1)])
            lno_done = pool.n
            if debug >= 5:
                e.wait("sync", pool, pool.n)
                e.dma("sync", dbg["rows"], rows_sb[0:1, :], sm["dbgs"])
                e.dma("sync", dbg["bc"], bc_sb[:], sm["dbgs"])
                e.dma("sync", dbg["xbb"], xbb[:], sm["dbgs"])
                e.dma("sync", dbg["sqb"], sqb[:], sm["dbgs"])

            for d in range(16):
                e.dma("gpsimd", W_sb[:, 2048 * d : 2048 * (d + 1)],
                      Wq[128 * d : 128 * (d + 1), :], ldc)
            e.wait("tensor", ldc, ldc.n)
            e.wait("tensor", pool, lno_done)
            q_ev = []
            for h in range(H):
                for half in range(2):
                    i = 2 * h + half
                    ps = PS[i % 2]
                    if i >= 2:
                        e.wait("tensor", acts, q_ev[i - 2])
                    for d in range(16):
                        e.mm(ps[:], W_sb[:, 2048 * d + 128 * h : 2048 * d + 128 * (h + 1)],
                             LNO[:, 1024 * d + 512 * half : 1024 * d + 512 * (half + 1)],
                             start=(d == 0), stop=(d == 15), sem=pe if d == 15 else None)
                    e.wait("scalar", pe, pe.n)
                    e.act(Qf[:, 1024 * h + 512 * half : 1024 * h + 512 * (half + 1)], ps[:],
                          AF.Identity, bias=bq_sb[:, h : h + 1], sem=acts)
                    q_ev.append(acts.n)

            if debug >= 1 and not NOQFDUMP:
                e.wait("sync", acts, acts.n)
                for i in range(4):
                    e.dma("sync", dbg["Qf"][:, 4096 * i : 4096 * (i + 1)],
                          Qf[:, 4096 * i : 4096 * (i + 1)], sm["dbgs"])

            if upto <= 0:
                return
            e.wait("gpsimd", pe, pe.n)
            for d in range(16):
                e.dma("gpsimd", W_sb[:, 2048 * d : 2048 * (d + 1)],
                      Wk[128 * d : 128 * (d + 1), :], ldc)
            wk_ld = ldc.n

            ln_done = []
            ln_st = []
            k_ev = []
            kst_free = []
            kts_pe = []

            def k_pass(ts):
                lnb = ln_sl2[:, (ts % 2) * 8192 : (ts % 2 + 1) * 8192]
                if ts == 0:
                    e.wait("tensor", ldc, wk_ld)
                e.wait("tensor", pool, ln_done[ts])
                for h in range(H):
                    i = 16 * ts + h
                    sl = i % 2
                    ps = PS[2 + sl]
                    if i >= 2:
                        e.wait("tensor", acts, k_ev[i - 2])
                    for d in range(16):
                        e.mm(ps[:], W_sb[:, 2048 * d + 128 * h : 2048 * d + 128 * (h + 1)],
                             lnb[:, 512 * d : 512 * (d + 1)],
                             start=(d == 0), stop=(d == 15), sem=pe if d == 15 else None)
                    e.wait("scalar", pe, pe.n)
                    if i >= 2:
                        e.wait("scalar", sk[sl], 16 * (i // 2))
                    e.act(kst[:, 512 * sl : 512 * (sl + 1)], ps[:], AF.Identity,
                          bias=bk_sb[:, h : h + 1], sem=acts)
                    k_ev.append(acts.n)
                    e.wait("sync", acts, acts.n)
                    e.dma("sync", K_g[128 * h : 128 * (h + 1), 512 * ts : 512 * (ts + 1)],
                          kst[:, 512 * sl : 512 * (sl + 1)], sk[sl])
                kts_pe.append(pe.n)

            for ts in range(16):
                lnb = ln_sl2[:, (ts % 2) * 8192 : (ts % 2 + 1) * 8192]
                if ts >= 2:
                    e.wait("gpsimd", slst[ts % 2], 256 * (ts // 2))
                war_val = kts_pe[ts - 2] if ts >= 2 else 0
                ln_fm(e, sm, 512 * ts, xT, 512,
                      out=lambda g, lnb=lnb: lnb[:, 512 * g : 512 * (g + 1)],
                      war_sem=pe if ts >= 2 else None, war_val=war_val)
                ln_done.append(pool.n)
                e.wait("sync", pool, pool.n)
                for g in range(16):
                    e.dma("sync", ln_dram[128 * g : 128 * (g + 1), 512 * ts : 512 * (ts + 1)],
                          lnb[:, 512 * g : 512 * (g + 1)], slst[ts % 2])
                if ts >= 1:
                    k_pass(ts - 1)
            k_pass(15)

            e.wait("gpsimd", pe, pe.n)
            for d in range(16):
                e.dma("gpsimd", W_sb[:, 2048 * d : 2048 * (d + 1)],
                      Wv[128 * d : 128 * (d + 1), :], ldc)
            e.wait("tensor", ldc, ldc.n)
            e.wait("sync", sm["sl0"], sm["sl0"].n)
            e.wait("sync", sm["sl1"], sm["sl1"].n)
            e.wait("sync", pe, kts_pe[-1])
            e.wait("scalar", sk[0], sk[0].n)
            e.wait("scalar", sk[1], sk[1].n)
            v_ev = []
            vts_pe = []
            lb = (sm["lb0"], sm["lb1"])
            for ts in range(16):
                lnb = ln_sl2[:, (ts % 2) * 8192 : (ts % 2 + 1) * 8192]
                if ts >= 2:
                    e.wait("sync", pe, vts_pe[ts - 2])
                for g in range(16):
                    e.dma("sync", lnb[:, 512 * g : 512 * (g + 1)],
                          ln_dram[128 * g : 128 * (g + 1), 512 * ts : 512 * (ts + 1)], lb[ts % 2])
                e.wait("tensor", lb[ts % 2], lb[ts % 2].n)
                for tt in range(4):
                    for dh in range(2):
                        i = len(v_ev)
                        sl = i % 2
                        pa, pb = PS[2 * sl], PS[2 * sl + 1]
                        if i >= 2:
                            e.wait("tensor", acts, v_ev[i - 2])
                        for d in range(16):
                            lhsT = lnb[:, 512 * d + 128 * tt : 512 * d + 128 * (tt + 1)]
                            e.mm(pa[:], lhsT,
                                 W_sb[:, 2048 * d + 1024 * dh : 2048 * d + 1024 * dh + 512],
                                 start=(d == 0), stop=False)
                            e.mm(pb[:], lhsT,
                                 W_sb[:, 2048 * d + 1024 * dh + 512 : 2048 * d + 1024 * (dh + 1)],
                                 start=(d == 0), stop=False)
                        e.mm(pa[:], onerow_b_sb[:], bvr_sb[0:1, 1024 * dh : 1024 * dh + 512],
                             start=False, stop=True)
                        e.mm(pb[:], onerow_b_sb[:], bvr_sb[0:1, 1024 * dh + 512 : 1024 * (dh + 1)],
                             start=False, stop=True, sem=pe)
                        e.wait("scalar", pe, pe.n)
                        if i >= 2:
                            e.wait("scalar", sv[sl], 16 * (i // 2))
                        e.act(vst[:, 1024 * sl : 1024 * sl + 512], pa[:], AF.Identity)
                        e.act(vst[:, 1024 * sl + 512 : 1024 * (sl + 1)], pb[:], AF.Identity, sem=acts)
                        v_ev.append(acts.n)
                        e.wait("sync", acts, acts.n)
                        e.dma("sync", V_g[512 * ts + 128 * tt : 512 * ts + 128 * (tt + 1),
                                          1024 * dh : 1024 * (dh + 1)],
                              vst[:, 1024 * sl : 1024 * (sl + 1)], sv[sl])
                vts_pe.append(pe.n)

            if upto <= 1:
                return
            for s_ in (sm["sk0"], sm["sk1"], sm["sv0"], sm["sv1"]):
                e.wait("sync", s_, s_.n)
            e.wait("sync", pe, pe.n)
            e.wait("scalar", sv[0], sv[0].n)
            e.wait("scalar", sv[1], sv[1].n)
            e.wait("tensor", acts, acts.n)
            e.wait("tensor", dve, dve.n)
            kv_ld = []
            att_done = []
            ctx_evac = {0: 0, 1: 0}

            def load_kv(h):
                half = h % 2
                kb = KV_sb[:, 16384 * half : 16384 * half + 8192]
                vb = KV_sb[:, 16384 * half + 8192 : 16384 * (half + 1)]
                if h >= 2:
                    e.wait("sync", pe, att_done[h - 2])
                e.dma("sync", kb[:], K_g[128 * h : 128 * (h + 1), :], kvs[half])
                for b in range(NC):
                    src = V_g[1024 * b : 1024 * (b + 1), 128 * h : 128 * (h + 1)]
                    src = src.rearrange("(s k) d -> k s d", k=128)
                    dst = vb[:, 1024 * b : 1024 * (b + 1)].rearrange("k (s d) -> k s d", s=8)
                    e.dma("sync", dst, src, kvs[half])
                kv_ld.append(kvs[half].n)

            load_kv(0)

            q_state = {"q": []}

            def flush(idx):
                q = q_state["q"]
                ent = q[idx]
                if ent["done"]:
                    return
                e.wait("tensor", acte, ent["exp"])
                pt = ptil[:, 512 * ent["slot"] : 512 * ent["slot"] + ent["N"]]
                e.mm(ent["l_ps"][0:1, ent["ctxcol"] : ent["ctxcol"] + ent["N"]],
                     onecol_sb[:], pt, start=False, stop=False)
                e.mm(ent["ctx_ps"][:, ent["ctxcol"] : ent["ctxcol"] + ent["N"]],
                     ent["vb"][:, 1024 * ent["b"] + 128 * ent["s"] : 1024 * ent["b"] + 128 * (ent["s"] + 1)],
                     pt, start=False, stop=False, sem=pe)
                ent["done"] = True
                ent["ctxpe"] = pe.n

            for (h, tg, items) in SCHED:
                half = h % 2
                kb = KV_sb[:, 16384 * half : 16384 * half + 8192]
                vb = KV_sb[:, 16384 * half + 8192 : 16384 * (half + 1)]
                if tg == 0:
                    if h + 1 < H:
                        load_kv(h + 1)
                    e.wait("tensor", kvs[h % 2], kv_ld[h])
                pp = (2 * h + tg) % 2
                ctx_ps = PS[2 + pp]
                l_ps = PL[pp]
                if 2 * h + tg >= 2:
                    e.wait("tensor", dve, ctx_evac[pp])
                e.mm(ctx_ps[:], zerob_sb[:], Qf[:, 0:512], start=True, stop=False)
                e.mm(l_ps[0:1, :], zerob_sb[:, 0:1], Qf[:, 0:512], start=True, stop=False)
                for (b, s, colkey, qoff, N, ctxcol, mask) in items:
                    q = q_state["q"]
                    i = len(q)
                    slot = i % 3
                    sps = PS[(0, 1, 4)[slot]]
                    if i >= 3:
                        e.wait("tensor", acte, q[i - 3]["exp"])
                    e.mm(sps[:, 0:N], kb[:, 1024 * b + 128 * s : 1024 * b + 128 * (s + 1)],
                         Qf[:, 1024 * h + qoff : 1024 * h + qoff + N],
                         start=True, stop=(mask is None), sem=None if mask is not None else pe2)
                    if mask is not None:
                        e.mm(sps[:, 0:N], identb_sb[:], masks_sb[:, mask : mask + N],
                             start=False, stop=True, sem=pe2)
                    e.wait("scalar", pe2, pe2.n)
                    if i >= 3:
                        e.wait("scalar", pe, q[i - 3]["ctxpe"])
                    cidx = COLS[colkey]
                    e.act(ptil[:, 512 * slot : 512 * slot + N], sps[:, 0:N], AF.Exp,
                          bias=btab_sb[:, cidx : cidx + 1], scale=SCALE, sem=acte)
                    q.append(dict(exp=acte.n, slot=slot, N=N, vb=vb, b=b, s=s,
                                  ctx_ps=ctx_ps, l_ps=l_ps, ctxcol=ctxcol,
                                  ctxpe=0, done=False))
                    if i >= 2:
                        flush(i - 2)
                if len(q_state["q"]) >= 2:
                    flush(-2)
                if q_state["q"]:
                    flush(-1)
                e.mm(ctx_ps[:], zerob_sb[:], Qf[:, 0:512], start=False, stop=True)
                e.mm(l_ps[0:1, :], zerob_sb[:, 0:1], Qf[:, 0:512], start=False, stop=True, sem=pe)
                peval = pe.n
                e.wait("vector", pe, peval)
                lrow = linv_sb[0:1, 512 * pp : 512 * (pp + 1)]
                e.vop("reciprocal", lrow, l_ps[0:1, :], sem=dve)
                e.wait("tensor", dve, dve.n)
                e.mm(PS[5][:, 0:512], onerow_f_sb[:], lrow, start=True, stop=True, sem=pe)
                e.wait("vector", pe, pe.n)
                lb = lbc_sb[:, 512 * pp : 512 * (pp + 1)]
                e.vop("tensor_copy", lb, PS[5][:, 0:512], sem=dve)
                e.wait("vector", dve, dve.n)
                e.vop("tensor_tensor", ctx_fm[:, 1024 * h + 512 * tg : 1024 * h + 512 * (tg + 1)],
                      ctx_ps[:], lb, op=ALU.mult, sem=dve)
                ctx_evac[pp] = dve.n
                if tg == 1:
                    att_done.append(pe.n)

            if debug >= 2:
                e.wait("sync", dve, dve.n)
                for i in range(4):
                    e.dma("sync", dbg["ctx"][:, 4096 * i : 4096 * (i + 1)],
                          ctx_fm[:, 4096 * i : 4096 * (i + 1)], sm["dbgs"])

            if upto <= 2:
                return
            e.wait("gpsimd", pe, pe.n)
            for d in range(16):
                e.dma("gpsimd", W_sb[:, 2048 * d : 2048 * (d + 1)],
                      Wo[128 * d : 128 * (d + 1), :], ldc)
            e.wait("tensor", ldc, ldc.n)
            e.wait("tensor", dve, dve.n)

            def rows_pass(scale_div):
                r_mean = rows_sb[0:1, 0:1024]
                r_var = rows_sb[0:1, 1024:2048]
                r_rstd = rows_sb[0:1, 2048:3072]
                r_mrstd = rows_sb[0:1, 1024:2048]
                e.wait("vector", pe, pe.n)
                e.vop("tensor_scalar_mul", r_mean[0:1, 0:512], PL[0][0:1, :], scale_div, sem=dve)
                e.vop("tensor_scalar_mul", r_mean[0:1, 512:1024], PS[4][0:1, 0:512], scale_div, sem=dve)
                e.vop("tensor_scalar_mul", r_var[0:1, 0:512], PL[1][0:1, :], scale_div, sem=dve)
                e.vop("tensor_scalar_mul", r_var[0:1, 512:1024], PS[5][0:1, 0:512], scale_div, sem=dve)
                e.wait("vector", dve, dve.n)
                e.vop("tensor_tensor", r_rstd, r_mean, r_mean, op=ALU.mult, sem=dve)
                e.wait("vector", dve, dve.n)
                e.vop("tensor_sub", r_var, r_var, r_rstd, sem=dve)
                e.wait("scalar", dve, dve.n)
                e.act(r_rstd, r_var, AF.Ln, bias=EPS, sem=acts)
                e.wait("scalar", acts, acts.n)
                e.act(r_rstd, r_rstd, AF.Exp, scale=-0.5, sem=acts)
                e.wait("vector", acts, acts.n)
                e.vop("tensor_tensor", r_mrstd, r_mean, r_rstd, op=ALU.mult, sem=dve)
                e.wait("tensor", dve, dve.n)
                for hh in range(2):
                    if hh >= 1:
                        e.wait("tensor", dve, dve.n)
                    e.mm(PS[4][:, 0:512], onerow_f_sb[:], r_rstd[0:1, 512 * hh : 512 * (hh + 1)],
                         start=True, stop=True, sem=pe)
                    e.mm(PS[5][:, 0:512], onerow_f_sb[:], r_mrstd[0:1, 512 * hh : 512 * (hh + 1)],
                         start=True, stop=True, sem=pe)
                    e.wait("vector", pe, pe.n)
                    e.vop("tensor_copy", bc_sb[:, 1024 * hh : 1024 * hh + 512],
                          PS[4][:, 0:512], sem=dve)
                    e.vop("tensor_copy", bc_sb[:, 1024 * hh + 512 : 1024 * (hh + 1)],
                          PS[5][:, 0:512], sem=dve)

            def stats_chunk(dc, ac):
                e.wait("vector", acts, acts.n)
                e.wait("vector", dve, dve.n)
                e.wait("vector", pe, pe.n)
                for hh in range(2):
                    xb = xbb[:, 512 * hh : 512 * (hh + 1)]
                    sq = sqb[:, 512 * hh : 512 * (hh + 1)]
                    e.vop("tensor_copy", xb, ac[:, 512 * hh : 512 * (hh + 1)], sem=dve)
                    e.wait("vector", dve, dve.n)
                    e.vop("tensor_mul", sq, xb, xb, sem=dve)
                    e.wait("tensor", dve, dve.n)
                    ts_ = PL[0] if hh == 0 else PS[4]
                    tq_ = PL[1] if hh == 0 else PS[5]
                    e.mm(ts_[0:1, 0:512], onecol_f_sb[:], xb, start=(dc == 0), stop=(dc == 15))
                    e.mm(tq_[0:1, 0:512], onecol_f_sb[:], sq, start=(dc == 0), stop=(dc == 15),
                         sem=pe)

            p3_ev = []
            p3_st = []
            for dc in range(16):
                sl = dc % 2
                ps0, ps1 = PS[2 * sl], PS[2 * sl + 1]
                if dc >= 2:
                    e.wait("tensor", acts, p3_ev[dc - 2])
                for h in range(H):
                    lhsT = W_sb[:, 2048 * h + 128 * dc : 2048 * h + 128 * (dc + 1)]
                    e.mm(ps0[:], lhsT, ctx_fm[:, 1024 * h : 1024 * h + 512],
                         start=(h == 0), stop=(h == 15))
                    e.mm(ps1[:], lhsT, ctx_fm[:, 1024 * h + 512 : 1024 * (h + 1)],
                         start=(h == 0), stop=(h == 15), sem=pe if h == 15 else None)
                e.wait("scalar", pe, pe.n)
                ao = aosb_a[:, 0:1024]
                if dc >= 1:
                    e.wait("scalar", st_ao, 16 * dc)
                    e.wait("scalar", dve, dve.n)
                e.act(ao[:, 0:512], ps0[:], AF.Identity, bias=bo_sb[:, dc : dc + 1])
                e.act(ao[:, 512:1024], ps1[:], AF.Identity, bias=bo_sb[:, dc : dc + 1], sem=acts)
                p3_ev.append(acts.n)
                stats_chunk(dc, ao)
                e.wait("sync", acts, p3_ev[-1])
                e.dma("sync", ao_dram[128 * dc : 128 * (dc + 1), :], ao, st_ao)
            rows_pass(1.0 / D)
            p3b_dve = []
            e.wait("sync", st_ao, st_ao.n)
            e.wait("sync", dve, dve.n)
            for dc in range(16):
                buf2 = aosb_a if dc % 2 == 0 else aosb_b
                xs2 = xst_b if dc % 2 == 0 else xo2_sb
                ao = buf2[:, 0:1024]
                if dc >= 2:
                    e.wait("sync", dve, p3b_dve[dc - 2])
                    e.wait("sync", st_x2, 16 * (dc - 1))
                    if debug >= 3:
                        e.wait("sync", sm["dbgs"], sm["dbgs"].n)
                e.dma("sync", ao, ao_dram[128 * dc : 128 * (dc + 1), :], ld)
                for hh in range(2):
                    xo = xs2[:, 512 * hh : 512 * (hh + 1)]
                    e.dma("sync", xo, xoT[128 * dc : 128 * (dc + 1), 512 * hh : 512 * (hh + 1)], ld)
                e.wait("vector", ld, ld.n)
                e.wait("vector", dve, dve.n)
                for hh in range(2):
                    xo = xs2[:, 512 * hh : 512 * (hh + 1)]
                    t1 = t1b[:, 512 * hh : 512 * (hh + 1)]
                    e.vop("tensor_tensor", t1, ao[:, 512 * hh : 512 * (hh + 1)],
                          bc_sb[:, 1024 * hh : 1024 * hh + 512], op=ALU.mult, sem=dve)
                    e.wait("vector", dve, dve.n)
                    e.vop("tensor_sub", t1, t1, bc_sb[:, 1024 * hh + 512 : 1024 * (hh + 1)], sem=dve)
                    e.wait("vector", dve, dve.n)
                    e.vop("tensor_add", ao[:, 512 * hh : 512 * (hh + 1)],
                          xo, t1, sem=dve)
                e.wait("sync", dve, dve.n)
                e.dma("sync", x2_dram[128 * dc : 128 * (dc + 1), :], ao, st_x2)
                if debug >= 3:
                    e.dma("sync", dbg["x2"][128 * dc : 128 * (dc + 1), :], ao, sm["dbgs"])
                stats_chunk(dc, ao)
                p3b_dve.append(dve.n)
            rows_pass(1.0 / D)
            p3c = []
            e.wait("sync", st_x2, st_x2.n)
            e.wait("sync", dve, dve.n)
            if debug >= 3:
                e.wait("sync", sm["dbgs"], sm["dbgs"].n)
            for dc in range(16):
                x2c = (aosb_a if dc % 2 == 0 else aosb_b)[:, 0:1024]
                if dc >= 2:
                    e.wait("sync", pool, p3c[dc - 2])
                e.dma("sync", x2c, x2_dram[128 * dc : 128 * (dc + 1), :], ld)
                e.wait("vector", ld, ld.n)
                e.wait("vector", dve, dve.n)
                for hh in range(2):
                    t1 = t1b[:, 512 * hh : 512 * (hh + 1)]
                    e.vop("tensor_tensor", t1, x2c[:, 512 * hh : 512 * (hh + 1)],
                          bc_sb[:, 1024 * hh : 1024 * hh + 512], op=ALU.mult, sem=dve)
                    e.wait("gpsimd", dve, dve.n)
                    e.pop("tensor_tensor",
                          ln2T[:, 1024 * dc + 512 * hh : 1024 * dc + 512 * (hh + 1)],
                          t1, bc_sb[:, 1024 * hh + 512 : 1024 * (hh + 1)],
                          op=ALU.subtract, sem=pool)
                p3c.append(pool.n)

            if debug >= 3 or LN2B:
                e.wait("sync", pool, pool.n)
                for i in range(4):
                    e.dma("sync", dbg["ln2"][:, 4096 * i : 4096 * (i + 1)],
                          ln2T[:, 4096 * i : 4096 * (i + 1)], sm["dbgs"])
                e.wait("sync", sm["dbgs"], sm["dbgs"].n)
            if LN2B >= 2:
                for i in range(12):
                    e.dma("sync", dbg["sb1"][:, 4096 * i : 4096 * (i + 1)],
                          sbraw[:][:, 4096 * i : 4096 * (i + 1)], sm["dbgs"])
                e.wait("sync", sm["dbgs"], sm["dbgs"].n)

            if upto <= 3:
                return
            e.wait("gpsimd", pool, pool.n)
            e.wait("gpsimd", pe, pe.n)
            e.wait("vector", sv[0], sv[0].n)
            e.wait("vector", sv[1], sv[1].n)
            e.wait("vector", pe, pe.n)
            if debug >= 1:
                e.wait("vector", sm["dbgs"], sm["dbgs"].n)
            p4_pe = []
            for g in range(p4_groups):
                gsl = g % 2
                wg = WGU_sb[:, 8192 * (2 * gsl) : 8192 * (2 * gsl + 1)]
                wu = WGU_sb[:, 8192 * (2 * gsl + 1) : 8192 * (2 * gsl + 2)]
                if g >= 2:
                    e.wait("gpsimd", pe, p4_pe[g - 2])
                src_g = WgT[:, 512 * g : 512 * (g + 1)].rearrange("(c p) n -> p c n", p=128)
                src_u = WuT[:, 512 * g : 512 * (g + 1)].rearrange("(c p) n -> p c n", p=128)
                wg_v = wg[:].rearrange("p (c n) -> p c n", c=16)
                wu_v = wu[:].rearrange("p (c n) -> p c n", c=16)
                for q4 in range(4):
                    e.dma("gpsimd", wg_v[:, 4 * q4 : 4 * (q4 + 1), :], src_g[:, 4 * q4 : 4 * (q4 + 1), :], gus[gsl])
                    e.dma("gpsimd", wu_v[:, 4 * q4 : 4 * (q4 + 1), :], src_u[:, 4 * q4 : 4 * (q4 + 1), :], gus[gsl])
                e.wait("tensor", gus[gsl], gus[gsl].n)
                for sub in range(4):
                    for half in range(2):
                        i = 8 * g + 2 * sub + half
                        if i >= 1:
                            e.wait("tensor", dve, dve.n)
                        for d in range(16):
                            rhs = ln2T[:, 1024 * d + 512 * half : 1024 * d + 512 * (half + 1)]
                            e.mm(PS[4][:], wg[:, 512 * d + 128 * sub : 512 * d + 128 * (sub + 1)],
                                 rhs, start=(d == 0), stop=(d == 15), sem=pe if d == 15 else None)
                        for d in range(16):
                            rhs = ln2T[:, 1024 * d + 512 * half : 1024 * d + 512 * (half + 1)]
                            e.mm(PS[5][:], wu[:, 512 * d + 128 * sub : 512 * d + 128 * (sub + 1)],
                                 rhs, start=(d == 0), stop=(d == 15), sem=pe if d == 15 else None)
                        e.wait("scalar", pe, pe.n)
                        sgs = sg_sb[:, 512 * (i % 2) : 512 * (i % 2 + 1)]
                        e.act(sgs, PS[4][:], AF.Silu, sem=acts)
                        e.wait("vector", acts, acts.n)
                        hc_g = 4 * g + sub
                        if half == 0:
                            dst = gated_lo if hc_g < 32 else gated_hi
                            dcol = 512 * (hc_g % 32)
                            e.vop("tensor_tensor", dst[:, dcol : dcol + 512], sgs, PS[5][:],
                                  op=ALU.mult, sem=dve)
                        else:
                            stg = vst[:, 1024 * (hc_g % 2) : 1024 * (hc_g % 2) + 512]
                            e.wait("vector", st_gb, st_gb.n)
                            e.vop("tensor_tensor", stg, sgs, PS[5][:], op=ALU.mult, sem=dve)
                            e.wait("sync", dve, dve.n)
                            e.dma("sync", gB_dram[128 * hc_g : 128 * (hc_g + 1), :], stg, st_gb)
                p4_pe.append(pe.n)

            if upto <= 4:
                return
            e.wait("gpsimd", pe, pe.n)
            e.wait("sync", pe, pe.n)
            e.wait("sync", st_gb, st_gb.n)
            if LN2B:
                for i in range(4):
                    e.dma("sync", dbg["ln2b"][:, 4096 * i : 4096 * (i + 1)],
                          ln2T[:, 4096 * i : 4096 * (i + 1)], sm["dbgs"])
                e.wait("sync", sm["dbgs"], sm["dbgs"].n)
            if LN2B >= 2:
                for i in range(12):
                    e.dma("sync", dbg["sb2"][:, 4096 * i : 4096 * (i + 1)],
                          sbraw[:][:, 4096 * i : 4096 * (i + 1)], sm["dbgs"])
                e.wait("sync", sm["dbgs"], sm["dbgs"].n)
            p5_pe = []
            p5_copy = {}
            for sc in range(8):
                wsl = sc % 2
                wd = WD_sb[:, 16384 * wsl : 16384 * (wsl + 1)]
                if sc >= 2:
                    e.wait("gpsimd", pe, p5_pe[sc - 2])
                for hc in range(8):
                    e.dma("gpsimd", wd[:, 2048 * hc : 2048 * (hc + 1)],
                          WdT[1024 * sc + 128 * hc : 1024 * sc + 128 * (hc + 1), :], wds[wsl])
                e.wait("tensor", wds[wsl], wds[wsl].n)
                gsl = sc % 2
                gbuf = gbuf2[:, 4096 * gsl : 4096 * (gsl + 1)]
                if sc >= 2:
                    e.wait("sync", pe, p5_pe[sc - 2])
                srcg = gB_dram[1024 * sc : 1024 * (sc + 1), :].rearrange("(c p) n -> p c n", p=128)
                e.dma("sync", gbuf[:].rearrange("p (c n) -> p c n", c=8), srcg, gbs[gsl])
                for half in range(2):
                    if half == 1:
                        e.wait("tensor", gbs[gsl], gbs[gsl].n)
                    for dq in range(4):
                        mm_pe = {}
                        for dc4 in range(4):
                            dci = 4 * dq + dc4
                            ps = PS[dc4]
                            if dc4 in p5_copy:
                                e.wait("tensor", dve, p5_copy[dc4])
                            for hc in range(8):
                                hc_g = 8 * sc + hc
                                if half == 0:
                                    gsrc = gated_lo if hc_g < 32 else gated_hi
                                    rhs = gsrc[:, 512 * (hc_g % 32) : 512 * (hc_g % 32 + 1)]
                                else:
                                    rhs = gbuf[:, 512 * hc : 512 * (hc + 1)]
                                e.mm(ps[:], wd[:, 2048 * hc + 128 * dci : 2048 * hc + 128 * (dci + 1)],
                                     rhs, start=(hc == 0), stop=(hc == 7),
                                     sem=pe if hc == 7 else None)
                            mm_pe[dc4] = pe.n
                        for dc4 in range(4):
                            dci = 4 * dq + dc4
                            stg = st32[:, 512 * (dc4 % 2) : 512 * (dc4 % 2 + 1)]
                            e.wait("vector", pe, mm_pe[dc4])
                            e.wait("vector", st_acc, st_acc.n)
                            e.vop("tensor_copy", stg, PS[dc4][:], sem=dve)
                            p5_copy[dc4] = dve.n
                            e.wait("gpsimd", dve, dve.n)
                            dst = acc_dram[128 * dci : 128 * (dci + 1), 512 * half : 512 * (half + 1)]
                            e.dma("gpsimd", dst, stg, st_acc, acc=(sc > 0))
                p5_pe.append(pe.n)

            if p5_mode <= 1:
                return
            e.wait("sync", st_acc, st_acc.n)
            e.wait("sync", dve, dve.n)
            p5s = []
            for dc in range(16):
                ac = (aosb_a if dc % 2 == 0 else aosb_b)[:, 0:1024]
                if dc >= 2:
                    e.wait("sync", dve, p5s[dc - 2])
                e.dma("sync", ac, acc_dram[128 * dc : 128 * (dc + 1), :], ld)
                e.wait("vector", ld, ld.n)
                stats_chunk(dc, ac)
                p5s.append(dve.n)
            rows_pass(1.0 / D)
            if p5_mode <= 2:
                return
            p5b = []
            e.wait("sync", st_x2, st_x2.n)
            e.wait("sync", dve, dve.n)
            for dc in range(16):
                ac = aosb_a[:, 0:1024]
                x2c = aosb_b[:, 0:1024]
                if dc >= 1:
                    e.wait("sync", st_y, 16 * dc)
                    e.wait("sync", dve, dve.n)
                e.dma("sync", ac, acc_dram[128 * dc : 128 * (dc + 1), :], ld)
                e.dma("sync", x2c, x2_dram[128 * dc : 128 * (dc + 1), :], ld)
                e.wait("vector", ld, ld.n)
                e.wait("vector", dve, dve.n)
                for hh in range(2):
                    t1 = t1b[:, 512 * hh : 512 * (hh + 1)]
                    e.vop("tensor_tensor", t1, ac[:, 512 * hh : 512 * (hh + 1)],
                          bc_sb[:, 1024 * hh : 1024 * hh + 512], op=ALU.mult, sem=dve)
                    e.wait("vector", dve, dve.n)
                    e.vop("tensor_sub", t1, t1, bc_sb[:, 1024 * hh + 512 : 1024 * (hh + 1)], sem=dve)
                    e.wait("vector", dve, dve.n)
                    e.vop("tensor_add", ac[:, 512 * hh : 512 * (hh + 1)],
                          x2c[:, 512 * hh : 512 * (hh + 1)], t1, sem=dve)
                e.wait("sync", dve, dve.n)
                e.dma("sync", y[128 * dc : 128 * (dc + 1), :], ac, st_y)
                p5b.append(st_y.n)


        @blk.sync
        def _(eng):
            program(E("sync", eng))

        @blk.gpsimd
        def _(eng):
            program(E("gpsimd", eng))

        @blk.tensor
        def _(eng):
            program(E("tensor", eng))

        @blk.scalar
        def _(eng):
            program(E("scalar", eng))

        @blk.vector
        def _(eng):
            program(E("vector", eng))

    return nc


def make_inputs(c, x, Wq, bq, Wk, bk, Wv, bv, Wo, bo, w_gate, w_up, w_down):
    xf = np.ascontiguousarray(x.reshape(S, D))
    rows = np.concatenate([np.arange(1024 * t + 128 * c, 1024 * t + 128 * (c + 1))
                           for t in range(NT)])
    btab_arr, _ = build_bias_table(c)
    return {
        "xT": np.ascontiguousarray(xf.T),
        "xoT": np.ascontiguousarray(xf[rows].T),
        "Wq": np.ascontiguousarray(Wq), "Wk": np.ascontiguousarray(Wk),
        "Wv": np.ascontiguousarray(Wv), "Wo": np.ascontiguousarray(Wo),
        "bq": np.ascontiguousarray(bq.reshape(H, 128).T),
        "bk": np.ascontiguousarray(bk.reshape(H, 128).T),
        "bo": np.ascontiguousarray(bo.reshape(H, 128).T),
        "bvr": bv.reshape(1, D).astype(ml_dtypes.bfloat16),
        "WgT": np.ascontiguousarray(w_gate.T),
        "WuT": np.ascontiguousarray(w_up.T),
        "WdT": np.ascontiguousarray(w_down.T),
        "btab": btab_arr,
        "masks": build_masks(c),
        "identb": np.eye(128, dtype=ml_dtypes.bfloat16),
        "onecol": np.ones((128, 1), dtype=ml_dtypes.bfloat16),
        "onerow_b": np.ones((1, 128), dtype=ml_dtypes.bfloat16),
        "onecol_f": np.ones((128, 1), dtype=ml_dtypes.bfloat16),
        "onerow_f": np.ones((1, 128), dtype=np.float32),
        "zerob": np.zeros((128, 128), dtype=ml_dtypes.bfloat16),
        "fconst": np.array([[EPS, 0.0, 1.0, 0, 0, 0, 0, 0]], dtype=np.float32),
        "fcb": np.tile(np.array([[0.0, 1.0]], dtype=np.float32), (128, 1)),
        "fcu": np.full((128, 4), 127, dtype=np.uint8),
    }


def run(inputs, debug=0, upto=99, p4_groups=16, p5_mode=3):
    args = [np.asarray(inputs[k], dtype=np.float32) for k in
            ("L0_attn_x", "L0_attn_Wq", "L0_attn_bq", "L0_attn_Wk", "L0_attn_bk",
             "L0_attn_Wv", "L0_attn_bv", "L0_attn_Wo", "L0_attn_bo",
             "L0_ffn_w_gate", "L0_ffn_w_up", "L0_ffn_w_down")]
    nc = build_nc(debug=debug, upto=upto, p4_groups=p4_groups, p5_mode=p5_mode)
    in_maps = [make_inputs(c, args[0], args[1], args[2], args[3], args[4], args[5],
                           args[6], args[7], args[8], args[9], args[10], args[11])
               for c in range(NC)]
    res = run_bass_kernel_spmd(nc, in_maps, core_ids=list(range(NC)))
    out = np.zeros((S, D), dtype=np.float32)
    for c in range(NC):
        yc = np.asarray(res.results[c]["y"]).T
        for t in range(NT):
            out[1024 * t + 128 * c : 1024 * t + 128 * (c + 1)] = yc[128 * t : 128 * (t + 1)]
    return out.reshape(1, S, D), res


def kernel(**inputs):
    out, _ = run(inputs, debug=0)
    return out
